# version 03
# baseline (speedup 1.0000x reference)
import numpy as np
import ml_dtypes
from contextlib import ExitStack
import concourse.bass as bass
import concourse.mybir as mybir
from concourse.bass_utils import run_bass_kernel_spmd

F32 = mybir.dt.float32
BF16 = mybir.dt.bfloat16
AF = mybir.ActivationFunctionType
ALU = mybir.AluOpType
AX = mybir.AxisListType

D = 2048
TOK = 2048
HALO = 1024
TOT = TOK + HALO
NKT = 16
DH = 128

ENGINES = ("tensor", "vector", "scalar", "gpsimd", "sync")
DMA_SEMS = ("dg", "ds", "st", "d0", "d1")


class Rec:

    def __init__(self, nc, ctx):
        self.nc = nc
        self.ops = {e: [] for e in ENGINES}
        self.sems = {}
        self.cnt = {}
        for name in ("pe", "ve", "se", "ge", "sy") + DMA_SEMS:
            self.sems[name] = ctx.enter_context(nc.semaphore(name))
            self.cnt[name] = 0
        self.eng_sem = {"tensor": "pe", "vector": "ve", "scalar": "se",
                        "gpsimd": "ge", "sync": "sy"}

    def wait(self, eng, sem, val):
        if val and val > 0:
            self.ops[eng].append(["w", sem, val])

    def op(self, eng, fn, inc=None, inc_by=1):
        tgt = None
        if inc is not None:
            self.cnt[inc] += inc_by
            tgt = self.cnt[inc]
        self.ops[eng].append(["o", fn, inc, inc_by])
        return tgt

    def cop(self, eng, fn):
        sem = self.eng_sem[eng]
        self.wait(eng, sem, self.cnt[sem])
        return self.op(eng, fn, inc=sem)

    def dma(self, eng, out, in_, sem, waits=()):
        for s, v in waits:
            self.wait(eng, s, v)
        return self.op(eng, (lambda e, o=out, i=in_: e.dma_start(o, i)), inc=sem, inc_by=16)

    def _seal(self, eng):
        sem = self.eng_sem[eng]
        for it in reversed(self.ops[eng]):
            if it[0] == "o":
                if it[2] is None:
                    it[2] = sem
                    it[3] = 1
                    self.cnt[sem] += 1
                elif it[2] != sem:
                    self.cnt[sem] += 1
                    self.ops[eng].append(["o", (lambda e: e.nop()), sem, 1])
                break
        return self.cnt[sem]

    def barrier(self, engines=ENGINES):
        targets = {}
        for eng in engines:
            targets[self.eng_sem[eng]] = self._seal(eng)
        for s in DMA_SEMS:
            targets[s] = self.cnt[s]
        for eng in engines:
            own = self.eng_sem[eng]
            for s, v in targets.items():
                if s != own and v > 0:
                    self.wait(eng, s, v)

    def replay(self, block):
        sems = self.sems
        for eng in ENGINES:
            ops = self.ops[eng]

            def mk(ops=ops):
                def run(e):
                    for it in ops:
                        if it[0] == "w":
                            e.wait_ge(sems[it[1]], it[2])
                        else:
                            ins = it[1](e)
                            if it[2] is not None:
                                ins.then_inc(sems[it[2]], it[3])
                return run
            getattr(block, eng)(mk())


class Phase0:

    def __init__(self, r, ctx, nc, io):
        self.r, self.nc, self.io = r, nc, io
        sb = self.sb = {}

        def S(name, shape, dt):
            sb[name] = ctx.enter_context(nc.sbuf_tensor("p0_" + name, shape, dt))

        S("wk", [128, NKT * 512], BF16)
        S("wv", [128, NKT * 512], BF16)
        S("cos", [128, 24 * 64], F32)
        S("sin", [128, 24 * 64], F32)
        S("x0", [128, 2 * D], F32)
        S("sq", [128, D], F32)
        S("stat", [128, 32], F32)
        S("xln", [128, 2 * D], BF16)
        S("xlnT", [128, NKT * 512], BF16)
        S("kr", [128, 512], BF16)
        S("qr", [128, 512], BF16)
        S("ta", [128, 256], F32)
        S("tb", [128, 256], F32)
        S("ringK", [128, 3 * 4 * 512], BF16)
        S("ringV", [128, 3 * 4 * 4 * 128], BF16)
        S("valid", [128, 6], BF16)
        S("qT", [128, 16 * 512], BF16)
        S("attT", [128, 12 * 512], BF16)
        S("yT", [128, 16 * 512], BF16)
        S("ws", [128, 2 * 512], BF16)
        S("tri", [128, 128], BF16)
        S("ident", [128, 128], BF16)
        S("ones1f", [1, 128], F32)
        S("sums", [1, 512], F32)
        S("recip", [1, 512], F32)
        S("av", [128, 512], BF16)
        S("x0r", [128, 4 * 512], F32)
        S("x1p", [128, 2 * 512], F32)
        S("ceps5", [128, 1], F32)
        S("cepsr", [128, 1], F32)
        S("clnsc", [128, 1], F32)

        self.pbig = ctx.enter_context(nc.psum_tensor("p0_pbig", [128, 2048], F32))
        self.pA = ctx.enter_context(nc.psum_tensor("p0_pA", [128, 512], F32))
        self.pB = ctx.enter_context(nc.psum_tensor("p0_pB", [128, 512], F32))
        self.pt = [ctx.enter_context(nc.psum_tensor("p0_pt0", [128, 128], BF16)),
                   ctx.enter_context(nc.psum_tensor("p0_pt1", [128, 128], BF16))]
        self.pp = {"idx": 0, "last": [None, None]}

        self.vwk = sb["wk"][:].rearrange("p (k n) -> p k n", k=NKT)
        self.vwv = sb["wv"][:].rearrange("p (k n) -> p k n", k=NKT)
        self.vcos = sb["cos"][:].rearrange("p (t f) -> p t f", t=24)
        self.vsin = sb["sin"][:].rearrange("p (t f) -> p t f", t=24)
        self.vx0 = sb["x0"][:].rearrange("p (b n) -> p b n", b=2)
        self.vxln = sb["xln"][:].rearrange("p (b n) -> p b n", b=2)
        self.vxlnT = sb["xlnT"][:].rearrange("p (k n) -> p k n", k=NKT)
        self.vringK = sb["ringK"][:].rearrange("p (s h n) -> p s h n", s=3, h=4)
        self.vringV = sb["ringV"][:].rearrange("p (s t h d) -> p s t h d", s=3, t=4, h=4)
        self.vqT = sb["qT"][:].rearrange("p (h n) -> p h n", h=16)
        self.vattT = sb["attT"][:].rearrange("p (j n) -> p j n", j=12)
        self.vyT = sb["yT"][:].rearrange("p (h n) -> p h n", h=16)
        self.vws = sb["ws"][:].rearrange("p (b n) -> p b n", b=2)
        self.vx0r = sb["x0r"][:].rearrange("p (t n) -> p t n", t=4)
        self.vx1p = sb["x1p"][:].rearrange("p (b n) -> p b n", b=2)

    def transpose(self, src_ap, dst_ap):
        r, pp = self.r, self.pp
        i = pp["idx"] % 2
        pt = self.pt[i]
        if pp["last"][i] is not None:
            r.wait("tensor", "ve", pp["last"][i])
        tp = r.op("tensor", lambda e, o=pt, a=src_ap, idn=self.sb["ident"]:
                  e.transpose(o[:], a, idn[:]), inc="pe")
        r.wait("vector", "pe", tp)
        te = r.op("vector", lambda e, o=dst_ap, a=pt: e.tensor_copy(o, a[:]), inc="ve")
        pp["last"][i] = te
        pp["idx"] += 1
        return te

    def ln_tile(self, x_src, dst_bf16):
        r, sb = self.r, self.sb
        st = sb["stat"]
        c0, c1, c2, c3, c4, c5, c6 = (st[:, i:i + 1] for i in range(7))
        r.cop("vector", lambda e, a=x_src, o=c0: e.reduce_sum(o, a, axis=AX.X))
        t1 = r.cop("scalar", lambda e, a=x_src, o=sb["sq"], acc=c1:
                   e.activation(o[:], a, AF.Square, accum_out=acc))
        r.wait("vector", "se", t1)
        r.cop("vector", lambda e, o=c2, a=c0: e.tensor_scalar(o, a, 1.0 / D, None, ALU.mult))
        r.cop("vector", lambda e, o=c3, a=c2: e.tensor_scalar(o, a, a, None, ALU.mult))
        tv = r.cop("vector", lambda e, o=c4, a=c1, b=c3:
                   e.scalar_tensor_tensor(o, a, 1.0 / D, b, ALU.mult, ALU.subtract))
        r.wait("scalar", "ve", tv)
        r.cop("scalar", lambda e, o=c5, a=c4, b=sb["ceps5"]: e.activation(o, a, AF.Ln, bias=b[:]))
        ts_ = r.cop("scalar", lambda e, o=c5, a=c5: e.activation(o, a, AF.Exp, scale=-0.5))
        r.wait("vector", "se", ts_)
        r.cop("vector", lambda e, o=c6, a=c2, b=c5:
              e.scalar_tensor_tensor(o, a, -1.0, b, ALU.mult, ALU.mult))
        return r.cop("vector", lambda e, o=dst_bf16, a=x_src, s1=c5, s2=c6:
                     e.tensor_scalar(o, a, s1, s2, ALU.mult, ALU.add))

    def rms4(self, src_ps3, stat0, fold_scale):
        r, sb = self.r, self.sb
        sq512 = sb["sq"][:, 0:512]
        sqv = sq512.rearrange("p (h f) -> p h f", h=4)
        msq = sb["stat"][:, stat0:stat0 + 4]
        t1 = r.cop("scalar", lambda e, o=sq512, a=src_ps3: e.activation(o, a, AF.Square))
        r.wait("vector", "se", t1)
        tv = r.cop("vector", lambda e, o=msq, a=sqv: e.reduce_sum(o, a, axis=AX.X))
        r.wait("scalar", "ve", tv)
        r.cop("scalar", lambda e, o=msq, a=msq, b=sb["cepsr"]:
              e.activation(o, a, AF.Ln, scale=1.0 / 128.0, bias=b[:]))
        if fold_scale:
            return r.cop("scalar", lambda e, o=msq, a=msq, b=sb["clnsc"]:
                         e.activation(o, a, AF.Exp, scale=-0.5, bias=b[:]))
        return r.cop("scalar", lambda e, o=msq, a=msq: e.activation(o, a, AF.Exp, scale=-0.5))

    def rope4(self, src3, dst3, tile):
        r, sb = self.r, self.sb
        cosb = self.vcos[:, tile, :].unsqueeze(1).to_broadcast([128, 4, 64])
        sinb = self.vsin[:, tile, :].unsqueeze(1).to_broadcast([128, 4, 64])
        e_ = src3.rearrange("p h (f two) -> p h f two", two=2)[:, :, :, 0]
        o_ = src3.rearrange("p h (f two) -> p h f two", two=2)[:, :, :, 1]
        ta = sb["ta"][:].rearrange("p (h f) -> p h f", h=4)[:, :, 0:64]
        tb = sb["tb"][:].rearrange("p (h f) -> p h f", h=4)[:, :, 0:64]
        lo = dst3[:, :, 0:64]
        hi = dst3[:, :, 64:128]
        r.cop("vector", lambda e, o=ta, a=e_, b=cosb: e.tensor_tensor(o, a, b, ALU.mult))
        r.cop("vector", lambda e, o=tb, a=o_, b=sinb: e.tensor_tensor(o, a, b, ALU.mult))
        r.cop("vector", lambda e, o=lo, a=ta, b=tb: e.tensor_tensor(o, a, b, ALU.subtract))
        r.cop("vector", lambda e, o=ta, a=e_, b=sinb: e.tensor_tensor(o, a, b, ALU.mult))
        r.cop("vector", lambda e, o=tb, a=o_, b=cosb: e.tensor_tensor(o, a, b, ALU.mult))
        return r.cop("vector", lambda e, o=hi, a=ta, b=tb: e.tensor_tensor(o, a, b, ALU.add))

    def emit(self, stop=None, dump_it=2):
        r, sb, io = self.r, self.sb, self.io

        def dump(*tensors):
            r.barrier()
            for i, t in enumerate(tensors):
                p, w = t.shape[0], int(np.prod(t.shape[1:]))
                if t.dtype == BF16:
                    r.dma("sync", io["dbg%d" % i][0:p, 0:w], t, "st")
                else:
                    r.dma("gpsimd", io["dbg%d" % i][0:p, 0:w], t, "dg")
            r.wait("sync", "st", r.cnt["st"])
            r.wait("gpsimd", "dg", r.cnt["dg"])
            return True
        pbig, pA, pB = self.pbig, self.pA, self.pB

        tg = r.op("gpsimd", lambda e: e.memset(sb["ident"][:], 1.0), inc="ge")
        r.wait("gpsimd", "ge", tg)
        tg = r.op("gpsimd", lambda e: e.affine_select(sb["ident"][:], sb["ident"][:],
             pattern=[[1, 128]], compare_op=ALU.is_equal, fill=0.0, base=0, channel_multiplier=-1), inc="ge")
        r.wait("gpsimd", "ge", tg)
        tg = r.op("gpsimd", lambda e: e.memset(sb["tri"][:], 1.0), inc="ge")
        r.wait("gpsimd", "ge", tg)
        r.op("gpsimd", lambda e: e.affine_select(sb["tri"][:], sb["tri"][:],
             pattern=[[1, 128]], compare_op=ALU.is_ge, fill=0.0, base=0, channel_multiplier=-1), inc="ge")
        r.op("vector", lambda e: e.memset(sb["ones1f"][:], 1.0))
        r.op("vector", lambda e: e.memset(sb["attT"][:], 0.0))
        r.op("vector", lambda e: e.memset(sb["ceps5"][:], 1e-5))
        r.op("vector", lambda e: e.memset(sb["cepsr"][:], 1e-6))
        r.op("vector", lambda e: e.memset(sb["clnsc"][:], float(np.log(1.0 / np.sqrt(128.0)))))
        r.dma("gpsimd", sb["wk"][:].rearrange("p (k n) -> p k n", k=NKT),
              io["l0_wkT"].rearrange("(k p) n -> p k n", p=128), "dg")
        r.dma("gpsimd", sb["wv"][:].rearrange("p (k n) -> p k n", k=NKT),
              io["l0_wvT"].rearrange("(k p) n -> p k n", p=128), "dg")
        r.dma("gpsimd", sb["valid"][:], io["valid"], "dg")
        r.dma("sync", sb["cos"][:].rearrange("p (t f) -> p t f", t=24),
              io["cosT"].rearrange("(t p) f -> p t f", p=128), "ds")
        r.dma("sync", sb["sin"][:].rearrange("p (t f) -> p t f", t=24),
              io["sinT"].rearrange("(t p) f -> p t f", p=128), "ds")
        r.barrier()

        for it in range(6):
            for ts in range(4):
                tile = it * 4 + ts
                buf = tile % 2
                ld = r.dma("sync", self.vx0[:, buf, :],
                           io["x0c"][tile * 128:(tile + 1) * 128, :], "ds")
                r.wait("vector", "ds", ld)
                r.wait("scalar", "ds", ld)
                tn = self.ln_tile(self.vx0[:, buf, :], self.vxln[:, buf, :])
                r.wait("tensor", "ve", tn)
                for kt in range(NKT):
                    self.transpose(self.vxln[:, buf, kt * 128:(kt + 1) * 128],
                                   self.vxlnT[:, kt, ts * 128:(ts + 1) * 128])
                r.barrier()
            if stop == "ln" and it == dump_it:
                return dump(sb["xlnT"][:], sb["xln"][:])
            if stop == "stat" and it == dump_it:
                return dump(sb["stat"][:], sb["sq"][:])
            if stop == "ident" and it == dump_it:
                return dump(sb["ident"][:], sb["tri"][:])
            slot = it % 3
            for ts in range(4):
                tile = it * 4 + ts
                for kt in range(NKT):
                    r.op("tensor", lambda e, k=kt, t=ts: e.matmul(
                        pA[:], self.vxlnT[:, k, t * 128:(t + 1) * 128], self.vwk[:, k, :],
                        start=(k == 0), stop=(k == NKT - 1), skip_group_check=True))
                for kt in range(NKT):
                    r.op("tensor", lambda e, k=kt, t=ts: e.matmul(
                        pB[:], self.vxlnT[:, k, t * 128:(t + 1) * 128], self.vwv[:, k, :],
                        start=(k == 0), stop=(k == NKT - 1), skip_group_check=True),
                        inc=("pe" if kt == NKT - 1 else None))
                kv_pe = r.cnt["pe"]
                r.wait("vector", "pe", kv_pe)
                r.op("vector", lambda e, s=slot, t=ts: e.tensor_copy(
                    self.vringV[:, s, t, :, :], pB[:].rearrange("p (h d) -> p h d", h=4)))
                r.wait("scalar", "pe", kv_pe)
                trs = self.rms4(pA[:].rearrange("p (h d) -> p h d", h=4), 16, False)
                r.wait("vector", "se", trs)
                khat = sb["sq"][:, 0:512].rearrange("p (h d) -> p h d", h=4)
                r.cop("vector", lambda e, o=khat, a=pA[:].rearrange("p (h d) -> p h d", h=4),
                      b=sb["stat"][:, 16:20].unsqueeze(2).to_broadcast([128, 4, 128]):
                      e.tensor_tensor(o, a, b, ALU.mult))
                tr = self.rope4(khat, sb["kr"][:].rearrange("p (h d) -> p h d", h=4), tile)
                r.wait("tensor", "ve", tr)
                for hh in range(4):
                    self.transpose(sb["kr"][:, hh * 128:(hh + 1) * 128],
                                   self.vringK[:, slot, hh, ts * 128:(ts + 1) * 128])
                r.barrier()
            if stop == "kv" and it == dump_it:
                return dump(sb["ringK"][:], sb["ringV"][:])

            if it < 2:
                continue
            cc = it - 2

            ws_use = [0, 0]
            for colq in range(4):
                for kt in range(NKT):
                    wbuf = kt % 2
                    wsem = "d0" if wbuf == 0 else "d1"
                    ld = r.dma("gpsimd", self.vws[:, wbuf, :],
                               io["l0_wqT"][kt * 128:(kt + 1) * 128, colq * 512:(colq + 1) * 512],
                               wsem, waits=[("pe", ws_use[wbuf])])
                    r.wait("tensor", wsem, ld)
                    for ts in range(4):
                        r.op("tensor", lambda e, k=kt, t=ts, b=wbuf: e.matmul(
                            pbig[:, t * 512:(t + 1) * 512],
                            self.vxlnT[:, k, t * 128:(t + 1) * 128], self.vws[:, b, :],
                            start=(k == 0), stop=(k == NKT - 1), skip_group_check=True),
                            inc=("pe" if ts == 3 else None))
                    ws_use[wbuf] = r.cnt["pe"]
                qpe = r.cnt["pe"]
                r.wait("scalar", "pe", qpe)
                r.wait("vector", "pe", qpe)
                for ts in range(4):
                    tile = it * 4 + ts
                    src = pbig[:, ts * 512:(ts + 1) * 512].rearrange("p (h d) -> p h d", h=4)
                    trs = self.rms4(src, 20, True)
                    r.wait("vector", "se", trs)
                    qhat = sb["sq"][:, 0:512].rearrange("p (h d) -> p h d", h=4)
                    r.cop("vector", lambda e, o=qhat, a=src,
                          b=sb["stat"][:, 20:24].unsqueeze(2).to_broadcast([128, 4, 128]):
                          e.tensor_tensor(o, a, b, ALU.mult))
                    tr = self.rope4(qhat, sb["qr"][:].rearrange("p (h d) -> p h d", h=4), tile)
                    r.wait("tensor", "ve", tr)
                    for hl in range(4):
                        self.transpose(sb["qr"][:, hl * 128:(hl + 1) * 128],
                                       self.vqT[:, colq * 4 + hl, ts * 128:(ts + 1) * 128])
                    r.barrier(("tensor", "vector", "scalar"))
                r.barrier()
            if stop == "q" and it == dump_it:
                return dump(sb["qT"][:])

            sc_evac = [0, 0]
            for h in range(16):
                kvh = h // 4
                for j in range(12):
                    si = it - 2 + j // 4
                    sl = si % 3
                    kk = j % 4
                    qoff = (j - 8) * 128 if j >= 8 else 0
                    pbank = pA if j % 2 == 0 else pB
                    r.wait("tensor", "se", sc_evac[j % 2])
                    mm = r.op("tensor", lambda e, p=pbank, s=sl, k=kvh, K=kk, q=qoff, hh=h:
                              e.matmul(p[:, q:512], self.vringK[:, s, k, K * 128:(K + 1) * 128],
                                       self.vqT[:, hh, q:512], start=True, stop=True,
                                       skip_group_check=True), inc="pe")
                    r.wait("scalar", "pe", mm)
                    te = r.op("scalar", lambda e, p=pbank, q=qoff, J=j: e.activation(
                        self.vattT[:, J, q:512], p[:, q:512], AF.Exp), inc="se")
                    sc_evac[j % 2] = te
                    fin = ("se", te)
                    if j == 0:
                        r.wait("vector", "se", te)
                        tv = r.op("vector", lambda e: e.memset(self.vattT[0:1, 0, :], 0.0), inc="ve")
                        fin = ("ve", tv)
                    elif j >= 8:
                        r.wait("vector", "se", te)
                        tv = r.op("vector", lambda e, J=j, q=qoff: e.tensor_tensor(
                            self.vattT[:, J, q:q + 128], self.vattT[:, J, q:q + 128],
                            sb["tri"][:], ALU.mult), inc="ve")
                        fin = ("ve", tv)
                    r.wait("tensor", fin[0], fin[1])
                    r.op("tensor", lambda e, J=j, si_=si: e.matmul(
                        pbig[0:1, 512:1024], sb["valid"][:, si_:si_ + 1], self.vattT[:, J, :],
                        start=(J == 0), stop=(J == 11), skip_group_check=True))
                    r.op("tensor", lambda e, J=j, s=sl, K=kk, k=kvh: e.matmul(
                        pbig[:, 0:512], self.vringV[:, s, K, k, :], self.vattT[:, J, :],
                        start=(J == 0), stop=(J == 11), skip_group_check=True),
                        inc=("pe" if j == 11 else None))
                av_pe = r.cnt["pe"]
                r.wait("vector", "pe", av_pe)
                r.cop("vector", lambda e: e.tensor_copy(sb["sums"][:], pbig[0:1, 512:1024]))
                tv = r.cop("vector", lambda e: e.reciprocal(sb["recip"][:], sb["sums"][:]))
                r.wait("tensor", "ve", tv)
                tb_ = r.op("tensor", lambda e: e.matmul(
                    pbig[:, 1024:1536], sb["ones1f"][:], sb["recip"][:],
                    start=True, stop=True, skip_group_check=True), inc="pe")
                r.wait("vector", "pe", tb_)
                r.cop("vector", lambda e: e.tensor_copy(sb["av"][:], pbig[:, 0:512]))
                tyh = r.cop("vector", lambda e, hh=h: e.tensor_tensor(
                    self.vyT[:, hh, :], sb["av"][:], pbig[:, 1024:1536], ALU.mult))
                r.wait("tensor", "ve", tyh)
            r.barrier()
            if stop == "att" and it == dump_it:
                return dump(sb["yT"][:], sb["attT"][:])

            ws_use = [0, 0]
            x1p_store = [0, 0]
            prev_evac = 0
            for colq in range(4):
                for ts in range(4):
                    tile = it * 4 + ts
                    r.dma("sync", self.vx0r[:, ts, :],
                          io["x0c"][tile * 128:(tile + 1) * 128, colq * 512:(colq + 1) * 512],
                          "ds", waits=[("ve", prev_evac)])
                ld_x0r = r.cnt["ds"]
                r.wait("vector", "ds", ld_x0r)
                for kt in range(NKT):
                    wbuf = kt % 2
                    wsem = "d0" if wbuf == 0 else "d1"
                    ld = r.dma("gpsimd", self.vws[:, wbuf, :],
                               io["l0_woT"][kt * 128:(kt + 1) * 128, colq * 512:(colq + 1) * 512],
                               wsem, waits=[("pe", ws_use[wbuf])])
                    r.wait("tensor", wsem, ld)
                    for ts in range(4):
                        r.op("tensor", lambda e, k=kt, t=ts, b=wbuf: e.matmul(
                            pbig[:, t * 512:(t + 1) * 512],
                            self.vyT[:, k, t * 128:(t + 1) * 128], self.vws[:, b, :],
                            start=(k == 0), stop=(k == NKT - 1), skip_group_check=True),
                            inc=("pe" if ts == 3 else None))
                    ws_use[wbuf] = r.cnt["pe"]
                ope = r.cnt["pe"]
                r.wait("vector", "pe", ope)
                for ts in range(4):
                    buf = ts % 2
                    r.wait("vector", "st", x1p_store[buf])
                    tv = r.op("vector", lambda e, t=ts, b=buf: e.tensor_tensor(
                        self.vx1p[:, b, :], pbig[:, t * 512:(t + 1) * 512],
                        self.vx0r[:, t, :], ALU.add), inc="ve")
                    st = r.dma("sync",
                               io["x1"][(cc * 4 + ts) * 128:(cc * 4 + ts + 1) * 128,
                                        colq * 512:(colq + 1) * 512],
                               self.vx1p[:, buf, :], "st", waits=[("ve", tv)])
                    x1p_store[buf] = st
                prev_evac = r.cnt["ve"]
                r.barrier(("tensor", "vector"))
            r.barrier()
        return self


class Helpers:

    def __init__(self, r, ctx, nc, tag):
        self.r, self.nc = r, nc
        sb = self.sb = {}

        def S(name, shape, dt):
            sb[name] = ctx.enter_context(nc.sbuf_tensor(tag + name, shape, dt))
        S("sq", [128, D], F32)
        S("stat", [128, 32], F32)
        S("ident", [128, 128], BF16)
        S("ceps5", [128, 1], F32)
        self.pt = [ctx.enter_context(nc.psum_tensor(tag + "pt0", [128, 128], BF16)),
                   ctx.enter_context(nc.psum_tensor(tag + "pt1", [128, 128], BF16))]
        self.pp = {"idx": 0, "last": [None, None]}
        tg = r.op("gpsimd", lambda e: e.memset(sb["ident"][:], 1.0), inc="ge")
        r.wait("gpsimd", "ge", tg)
        r.op("gpsimd", lambda e: e.affine_select(sb["ident"][:], sb["ident"][:],
             pattern=[[1, 128]], compare_op=ALU.is_equal, fill=0.0, base=0,
             channel_multiplier=-1), inc="ge")
        r.op("vector", lambda e: e.memset(sb["ceps5"][:], 1e-5), inc="ve")

    def transpose(self, src_ap, dst_ap):
        r, pp = self.r, self.pp
        i = pp["idx"] % 2
        pt = self.pt[i]
        if pp["last"][i] is not None:
            r.wait("tensor", "ve", pp["last"][i])
        tp = r.op("tensor", lambda e, o=pt, a=src_ap, idn=self.sb["ident"]:
                  e.transpose(o[:], a, idn[:]), inc="pe")
        r.wait("vector", "pe", tp)
        te = r.op("vector", lambda e, o=dst_ap, a=pt: e.tensor_copy(o, a[:]), inc="ve")
        pp["last"][i] = te
        pp["idx"] += 1
        return te

    def ln_tile(self, x_src, dst_bf16):
        r, sb = self.r, self.sb
        st = sb["stat"]
        c0, c1, c2, c3, c4, c5, c6 = (st[:, i:i + 1] for i in range(7))
        r.cop("vector", lambda e, a=x_src, o=c0: e.reduce_sum(o, a, axis=AX.X))
        t1 = r.cop("scalar", lambda e, a=x_src, o=sb["sq"], acc=c1:
                   e.activation(o[:], a, AF.Square, accum_out=acc))
        r.wait("vector", "se", t1)
        r.cop("vector", lambda e, o=c2, a=c0: e.tensor_scalar(o, a, 1.0 / D, None, ALU.mult))
        r.cop("vector", lambda e, o=c3, a=c2: e.tensor_scalar(o, a, a, None, ALU.mult))
        tv = r.cop("vector", lambda e, o=c4, a=c1, b=c3:
                   e.scalar_tensor_tensor(o, a, 1.0 / D, b, ALU.mult, ALU.subtract))
        r.wait("scalar", "ve", tv)
        r.cop("scalar", lambda e, o=c5, a=c4, b=sb["ceps5"]: e.activation(o, a, AF.Ln, bias=b[:]))
        ts_ = r.cop("scalar", lambda e, o=c5, a=c5: e.activation(o, a, AF.Exp, scale=-0.5))
        r.wait("vector", "se", ts_)
        r.cop("vector", lambda e, o=c6, a=c2, b=c5:
              e.scalar_tensor_tensor(o, a, -1.0, b, ALU.mult, ALU.mult))
        return r.cop("vector", lambda e, o=dst_bf16, a=x_src, s1=c5, s2=c6:
                     e.tensor_scalar(o, a, s1, s2, ALU.mult, ALU.add))


class Phase1:

    def __init__(self, r, ctx, nc, io):
        self.r, self.nc, self.io = r, nc, io
        self.h = Helpers(r, ctx, nc, "p1_")
        sb = self.sb = self.h.sb

        def S(name, shape, dt):
            sb[name] = ctx.enter_context(nc.sbuf_tensor("p1_" + name, shape, dt))
        S("x1", [128, 2 * D], F32)
        S("xln", [128, 2 * D], BF16)
        S("xlt", [128, 2 * D], BF16)
        S("rg", [128, NKT * 8], BF16)
        S("we", [128, 16 * 8], F32)
        S("e8", [128, 8], F32)
        S("r8", [128, 8], F32)
        S("wgu", [128, NKT * 2048], BF16)
        S("wd", [128, 2 * 8 * 2048], BF16)
        S("sg", [128, 1024], BF16)
        S("hh", [128, 1024], BF16)
        S("hT", [128, 8 * 128], BF16)
        S("ya", [128, 3 * D], F32)
        S("x1r", [128, D], F32)
        self.pbig = ctx.enter_context(nc.psum_tensor("p1_pbig", [128, 2048], F32))
        self.pA = ctx.enter_context(nc.psum_tensor("p1_pA", [128, 512], F32))
        self.vx1 = sb["x1"][:].rearrange("p (b n) -> p b n", b=2)
        self.vxln = sb["xln"][:].rearrange("p (b n) -> p b n", b=2)
        self.vxlt = sb["xlt"][:].rearrange("p (b n) -> p b n", b=2)
        self.vrg = sb["rg"][:].rearrange("p (k n) -> p k n", k=NKT)
        self.vwe = sb["we"][:].rearrange("p (t n) -> p t n", t=16)
        self.vwgu = sb["wgu"][:].rearrange("p (k n) -> p k n", k=NKT)
        self.vwd = sb["wd"][:].rearrange("p (b k n) -> p b k n", b=2, k=8)
        self.vhT = sb["hT"][:].rearrange("p (k n) -> p k n", k=8)
        self.vya = sb["ya"][:].rearrange("p (b n) -> p b n", b=3)

    def emit(self, stop=None):
        r, sb, io = self.r, self.sb, self.io
        h = self.h
        pbig, pA = self.pbig, self.pA

        r.dma("gpsimd", sb["rg"][:].rearrange("p (k n) -> p k n", k=NKT),
              io["l0_rgT"].rearrange("(k p) n -> p k n", p=128), "dg")
        r.barrier()

        for t in range(16):
            buf = t % 2
            ld = r.dma("sync", self.vx1[:, buf, :], io["x1"][t * 128:(t + 1) * 128, :], "ds")
            r.wait("vector", "ds", ld)
            r.wait("scalar", "ds", ld)
            tn = h.ln_tile(self.vx1[:, buf, :], self.vxln[:, buf, :])
            r.wait("tensor", "ve", tn)
            for kt in range(NKT):
                h.transpose(self.vxln[:, buf, kt * 128:(kt + 1) * 128],
                            self.vxlt[:, 0, kt * 128:(kt + 1) * 128])
            xt_done = r.cnt["ve"]
            r.dma("sync", io["xlnT2"][t * 128:(t + 1) * 128, :], sb["xlt"][:, 0:2048],
                  "st", waits=[("ve", xt_done)])
            r.wait("tensor", "ve", xt_done)
            for kt in range(NKT):
                r.op("tensor", lambda e, k=kt: e.matmul(
                    pA[:, 0:8], self.vxlt[:, 0, k * 128:(k + 1) * 128], self.vrg[:, k, :],
                    start=(k == 0), stop=(k == NKT - 1), skip_group_check=True),
                    inc=("pe" if kt == NKT - 1 else None))
            lg = r.cnt["pe"]
            st = sb["stat"]
            mx, m1, m2, s12, rcp = (st[:, 8 + i:9 + i] for i in range(5))
            e8, r8 = sb["e8"], sb["r8"]
            r.wait("vector", "pe", lg)
            tv = r.op("vector", lambda e: e.reduce_max(mx, pA[:, 0:8], axis=AX.X), inc="ve")
            r.wait("vector", "ve", tv)
            tv = r.op("vector", lambda e: e.tensor_scalar(m1, mx, -1.0, None, ALU.mult), inc="ve")
            r.wait("scalar", "ve", tv)
            te = r.op("scalar", lambda e: e.activation(e8[:], pA[:, 0:8], AF.Exp, bias=m1), inc="se")
            r.wait("vector", "se", te)
            tv = r.op("vector", lambda e: e.reduce_max(m1, e8[:], axis=AX.X), inc="ve")
            r.wait("vector", "ve", tv)
            tv = r.op("vector", lambda e: e.tensor_scalar(r8[:], e8[:], m1, None, ALU.is_ge), inc="ve")
            r.wait("vector", "ve", tv)
            tv = r.op("vector", lambda e: e.scalar_tensor_tensor(r8[:], r8[:], -1e30, e8[:], ALU.mult, ALU.add), inc="ve")
            r.wait("vector", "ve", tv)
            tv = r.op("vector", lambda e: e.reduce_max(m2, r8[:], axis=AX.X), inc="ve")
            r.wait("vector", "ve", tv)
            tv = r.op("vector", lambda e: e.tensor_tensor(s12, m1, m2, ALU.add), inc="ve")
            r.wait("vector", "ve", tv)
            tv = r.op("vector", lambda e: e.reciprocal(rcp, s12), inc="ve")
            r.wait("vector", "ve", tv)
            tv = r.op("vector", lambda e: e.tensor_scalar(r8[:], e8[:], m2, None, ALU.is_ge), inc="ve")
            r.wait("vector", "ve", tv)
            tv = r.op("vector", lambda e, t_=t: e.scalar_tensor_tensor(
                self.vwe[:, t_, :], e8[:], rcp, r8[:], ALU.mult, ALU.mult), inc="ve")
            r.barrier()

        if stop == "we":
            r.barrier()
            r.dma("sync", io["dbg0"][:, 0:128], sb["we"][:], "st")
            r.wait("sync", "st", r.cnt["st"])
            return self

        use_wgu = [0, 0]
        use_xlt = [0, 0]
        use_yain = [0, 0]
        st_yaout = [0, 0]
        prev_pbig_read = 0
        prev_up_read = 0
        ldw_done = [0, 0]
        for e_i in range(8):
            wb = e_i % 2
            wsem = "d0" if wb == 0 else "d1"
            ldg = r.dma("gpsimd", self.vwgu[:].rearrange("p (k n) -> p k n", k=NKT)
                        if False else sb["wgu"][:].rearrange("p (k n) -> p k n", k=NKT),
                        io["l0_wgu"][e_i].rearrange("(k p) n -> p k n", p=128), "dg",
                        waits=[("pe", use_wgu[0])])
            ldw = r.dma("gpsimd", self.vwd[:, wb, :, :].rearrange("p k n -> p (k n)"),
                        io["l0_wd"][e_i].rearrange("(k p) n -> p k n", p=128), wsem,
                        waits=[("pe", use_wgu[1])])
            ldw_done[wb] = ldw
            ldg_done = r.cnt["dg"]
            for t in range(16):
                buf = t % 2
                ld = r.dma("sync", self.vxlt[:, buf, :], io["xlnT2"][t * 128:(t + 1) * 128, :],
                           "ds", waits=[("pe", use_xlt[buf])])
                if e_i > 0:
                    r.dma("sync", self.vya[:, buf, :], io["y_acc"][t * 128:(t + 1) * 128, :],
                          "ds", waits=[("ve", use_yain[buf])])
                if e_i == 7:
                    r.dma("sync", sb["x1r"][:], io["x1"][t * 128:(t + 1) * 128, :], "ds",
                          waits=[("ve", use_yain[buf])])
                ld_all = r.cnt["ds"]
                r.wait("tensor", "ds", ld_all)
                r.wait("tensor", "dg", ldg_done)
                r.wait("tensor", wsem, ldw_done[wb])
                r.wait("tensor", "ve", prev_pbig_read)
                vx = self.vxlt[:, buf, :].rearrange("p (k n) -> p k n", k=NKT)
                for kt in range(NKT):
                    for g in range(4):
                        r.op("tensor", lambda e, k=kt, G=g, VX=vx: e.matmul(
                            pbig[:, G * 512:(G + 1) * 512], VX[:, k, :],
                            self.vwgu[:, k, G * 512:(G + 1) * 512],
                            start=(k == 0), stop=(k == NKT - 1), skip_group_check=True),
                            inc=("pe" if (kt == NKT - 1 and g == 3) else None))
                gu = r.cnt["pe"]
                use_xlt[buf] = gu
                r.wait("scalar", "pe", gu)
                for g in range(2):
                    r.op("scalar", lambda e, G=g: e.activation(
                        sb["sg"][:, G * 512:(G + 1) * 512], pbig[:, G * 512:(G + 1) * 512],
                        AF.Silu), inc=("se" if g == 1 else None))
                sil = r.cnt["se"]
                r.wait("vector", "pe", gu)
                r.wait("vector", "se", sil)
                for g in range(2):
                    r.op("vector", lambda e, G=g, t_=t, E=e_i: e.scalar_tensor_tensor(
                        sb["hh"][:, G * 512:(G + 1) * 512], pbig[:, 1024 + G * 512:1536 + G * 512],
                        self.vwe[:, t_, E:E + 1], sb["sg"][:, G * 512:(G + 1) * 512],
                        ALU.mult, ALU.mult), inc=("ve" if g == 1 else None))
                hdone = r.cnt["ve"]
                prev_up_read = hdone
                r.wait("tensor", "ve", hdone)
                for ki in range(8):
                    h.transpose(sb["hh"][:, ki * 128:(ki + 1) * 128], self.vhT[:, ki, :])
                hT_done = r.cnt["ve"]
                r.wait("tensor", "ve", hT_done)
                for g in range(4):
                    for ki in range(8):
                        r.op("tensor", lambda e, G=g, k=ki, B=wb: e.matmul(
                            pbig[:, G * 512:(G + 1) * 512], self.vhT[:, k, :],
                            self.vwd[:, B, k, G * 512:(G + 1) * 512],
                            start=(k == 0), stop=(k == 7), skip_group_check=True),
                            inc=("pe" if (g == 3 and ki == 7) else None))
                dn = r.cnt["pe"]
                if t == 15:
                    use_wgu[0] = gu
                    use_wgu[1] = dn
                ob = 2
                r.wait("vector", "pe", dn)
                r.wait("vector", "ds", ld_all)
                r.wait("vector", "st", st_yaout[0])
                for g in range(4):
                    sl = slice(g * 512, (g + 1) * 512)
                    if e_i == 0:
                        r.op("vector", lambda e, S_=sl, OB=ob: e.tensor_copy(
                            self.vya[:, OB, S_], pbig[:, S_]), inc=("ve" if g == 3 else None))
                    else:
                        r.op("vector", lambda e, S_=sl, OB=ob, B=buf: e.scalar_tensor_tensor(
                            self.vya[:, OB, S_], pbig[:, S_], 1.0, self.vya[:, B, S_],
                            ALU.mult, ALU.add), inc=("ve" if g == 3 else None))
                ev = r.cnt["ve"]
                use_yain[buf] = ev
                prev_pbig_read = ev
                if e_i == 7:
                    r.wait("vector", "ve", ev)
                    tv = r.op("vector", lambda e, OB=ob: e.tensor_tensor(
                        self.vya[:, OB, :], self.vya[:, OB, :], sb["x1r"][:], ALU.add), inc="ve")
                    ev = tv
                    prev_pbig_read = ev
                dst = io["x2"] if e_i == 7 else io["y_acc"]
                st = r.dma("sync", dst[t * 128:(t + 1) * 128, :], self.vya[:, ob, :], "st",
                           waits=[("ve", ev)])
                st_yaout[0] = st
        r.barrier()
        return self


def declare_io_phase1(nc):
    io = {}
    io["l0_rgT"] = nc.dram_tensor("l0_rgT", [D, 8], F32, kind="ExternalInput").ap()
    io["l0_wgu"] = nc.dram_tensor("l0_wgu", [8, D, 2048], F32, kind="ExternalInput").ap()
    io["l0_wd"] = nc.dram_tensor("l0_wd", [8, 1024, 2048], F32, kind="ExternalInput").ap()
    io["xlnT2"] = nc.dram_tensor("xlnT2", [16 * 128, D], BF16, kind="Internal").ap()
    io["y_acc"] = nc.dram_tensor("y_acc", [TOK, D], F32, kind="Internal").ap()
    return io


def build_p1_program(stop=None):
    nc = bass.Bass("TRN2", target_bir_lowering=False)
    io = declare_io_phase1(nc)
    io["x1"] = nc.dram_tensor("x1", [TOK, D], F32, kind="ExternalInput").ap()
    io["x2"] = nc.dram_tensor("x2", [TOK, D], F32, kind="ExternalOutput").ap()
    if stop is not None:
        io["dbg0"] = nc.dram_tensor("dbg0", [128, 8192], F32, kind="ExternalOutput").ap()
    with ExitStack() as ctx:
        r = Rec(nc, ctx)
        Phase1(r, ctx, nc, io).emit(stop=stop)
        r.wait("sync", "st", r.cnt["st"])
        blk = ctx.enter_context(nc.Block())
        r.replay(blk)
    return nc


def prep_phase1_weights(router_gate, expert_gate_up, expert_down):
    rgT = np.ascontiguousarray(np.asarray(router_gate, np.float32).T)
    eg = np.asarray(expert_gate_up, np.float32)
    wgu = np.ascontiguousarray(np.concatenate([eg[:, :, 0::2], eg[:, :, 1::2]], axis=2))
    wd = np.ascontiguousarray(np.asarray(expert_down, np.float32))
    return rgT, wgu, wd


class Phase2:

    def __init__(self, r, ctx, nc, io):
        self.r, self.nc, self.io = r, nc, io
        self.h = Helpers(r, ctx, nc, "p2_")
        sb = self.sb = self.h.sb

        def S(name, shape, dt):
            sb[name] = ctx.enter_context(nc.sbuf_tensor("p2_" + name, shape, dt))
        S("wq", [128, NKT * 1024], BF16)
        S("wk", [128, NKT * 1024], BF16)
        S("wv", [128, NKT * 1024], BF16)
        S("bqkv", [1, 3 * 1024], BF16)
        S("bo", [1, 2048], BF16)
        S("RT", [128, 64], BF16)
        S("mq", [128, 128], BF16)
        S("ones1", [1, 128], BF16)
        S("cos", [128, 4 * 64], F32)
        S("sin", [128, 4 * 64], F32)
        S("x2t", [128, D], F32)
        S("xln", [128, D], BF16)
        S("xlnT", [128, NKT * 128], BF16)
        S("vst", [128, 1024], BF16)
        S("qh", [128, 1024], BF16)
        S("kh", [128, 1024], BF16)
        S("ta", [128, 512], F32)
        S("tb", [128, 512], F32)
        S("qpT", [128, 8 * 512], BF16)
        S("kpT", [128, 8 * 512], BF16)
        S("vT", [128, 8 * 512], BF16)
        S("chalf", [64, 1], F32)
        S("atm", [128, 128], BF16)
        S("qblk", [128, 128], BF16)
        S("kblk", [128, 128], BF16)
        S("vblk", [128, 128], BF16)
        S("varr", [128, 128], BF16)
        S("o1", [128, 128], BF16)
        S("yTh", [128, 8 * 2048], BF16)
        S("wos", [128, 2 * 512], BF16)
        S("x2o", [128, 512], F32)
        S("x3p", [128, 512], F32)
        self.pbig = ctx.enter_context(nc.psum_tensor("p2_pbig", [128, 2048], F32))
        self.pA = ctx.enter_context(nc.psum_tensor("p2_pA", [128, 512], F32))
        self.pB = ctx.enter_context(nc.psum_tensor("p2_pB", [128, 512], F32))
        self.vwq = sb["wq"][:].rearrange("p (k n) -> p k n", k=NKT)
        self.vwk = sb["wk"][:].rearrange("p (k n) -> p k n", k=NKT)
        self.vwv = sb["wv"][:].rearrange("p (k n) -> p k n", k=NKT)
        self.vcos = sb["cos"][:].rearrange("p (t f) -> p t f", t=4)
        self.vsin = sb["sin"][:].rearrange("p (t f) -> p t f", t=4)

        self.vxlnT = sb["xlnT"][:].rearrange("p (k n) -> p k n", k=NKT)
        self.vqpT = sb["qpT"][:].rearrange("p (h n) -> p h n", h=8)
        self.vkpT = sb["kpT"][:].rearrange("p (h n) -> p h n", h=8)
        self.vvT = sb["vT"][:].rearrange("p (h n) -> p h n", h=8)
        self.vyTh = sb["yTh"][:].rearrange("p (h n) -> p h n", h=8)
        self.vwos = sb["wos"][:].rearrange("p (b n) -> p b n", b=2)


    def rope_rh(self, src, dst, heads, ctile):
        r, sb = self.r, self.sb
        cosb = self.vcos[:, ctile, :].unsqueeze(1).to_broadcast([128, heads, 64])
        sinb = self.vsin[:, ctile, :].unsqueeze(1).to_broadcast([128, heads, 64])
        xlo = src[:, :, 0:64]
        xhi = src[:, :, 64:128]
        ta = sb["ta"][:].rearrange("p (h f) -> p h f", h=heads)[:, :, 0:64]
        tb = sb["tb"][:].rearrange("p (h f) -> p h f", h=heads)[:, :, 0:64]
        r.cop("vector", lambda e, o=ta, a=xlo, b=cosb: e.tensor_tensor(o, a, b, ALU.mult))
        r.cop("vector", lambda e, o=tb, a=xhi, b=sinb: e.tensor_tensor(o, a, b, ALU.mult))
        r.cop("vector", lambda e, o=dst[:, :, 0:64], a=ta, b=tb: e.tensor_tensor(o, a, b, ALU.subtract))
        r.cop("vector", lambda e, o=ta, a=xhi, b=cosb: e.tensor_tensor(o, a, b, ALU.mult))
        r.cop("vector", lambda e, o=tb, a=xlo, b=sinb: e.tensor_tensor(o, a, b, ALU.mult))
        return r.cop("vector", lambda e, o=dst[:, :, 64:128], a=ta, b=tb: e.tensor_tensor(o, a, b, ALU.add))

    def norm8(self, src3, stat0, dep_ve=0):
        r, sb = self.r, self.sb
        nrm = sb["stat"][:, stat0:stat0 + 8]
        r.wait("scalar", "ve", dep_ve)
        t1 = r.cop("scalar", lambda e, o=sb["sq"][:, 0:1024], a=src3.rearrange("p h f -> p (h f)"):
                   e.activation(o, a, AF.Square))
        r.wait("vector", "se", t1)
        tv = r.cop("vector", lambda e, o=nrm, a=sb["sq"][:, 0:1024].rearrange("p (h f) -> p h f", h=8):
                   e.reduce_sum(o, a, axis=AX.X))
        r.wait("scalar", "ve", tv)
        ts_ = r.cop("scalar", lambda e, o=nrm, a=nrm: e.activation(o, a, AF.Ln))
        ts_ = r.cop("scalar", lambda e, o=nrm, a=nrm: e.activation(o, a, AF.Exp, scale=-0.5))
        r.wait("vector", "se", ts_)
        return r.cop("vector", lambda e, o=nrm, a=nrm: e.tensor_scalar_min(o, a, 1e6))

    def emit(self, stop=None):
        r, sb, io = self.r, self.sb, self.io
        h = self.h
        pbig, pA, pB = self.pbig, self.pA, self.pB
        INV2PI = float(1.0 / (2 * np.pi))
        TWOPI = float(2 * np.pi)
        MAGIC = 12582912.0

        r.dma("gpsimd", sb["wq"][:].rearrange("p (k n) -> p k n", k=NKT),
              io["l1_wqT"].rearrange("(k p) n -> p k n", p=128), "dg")
        r.dma("gpsimd", sb["wk"][:].rearrange("p (k n) -> p k n", k=NKT),
              io["l1_wkT"].rearrange("(k p) n -> p k n", p=128), "dg")
        r.dma("gpsimd", sb["wv"][:].rearrange("p (k n) -> p k n", k=NKT),
              io["l1_wvT"].rearrange("(k p) n -> p k n", p=128), "dg")
        r.dma("gpsimd", sb["bqkv"][0:1, 0:1024], io["l1_bq"], "dg")
        r.dma("gpsimd", sb["bqkv"][0:1, 1024:2048], io["l1_bk"], "dg")
        r.dma("gpsimd", sb["bqkv"][0:1, 2048:3072], io["l1_bv"], "dg")
        r.dma("gpsimd", sb["bo"][:], io["l1_bo"], "dg")
        r.dma("gpsimd", sb["RT"][:], io["l1_RT"], "dg")
        r.dma("gpsimd", sb["mq"][:], io["l1_mq"], "dg")
        r.op("vector", lambda e: e.memset(sb["ones1"][:], 1.0), inc="ve")
        r.op("vector", lambda e: e.memset(sb["chalf"][:], float(np.pi / 2)), inc="ve")
        r.barrier()

        for half in range(2):
            for ch in range(4):
                cc = half * 4 + ch
                r.dma("sync", sb["cos"][:].rearrange("p (t f) -> p t f", t=4),
                      io["cosT2"][cc * 512:(cc + 1) * 512, :].rearrange("(t p) f -> p t f", p=128), "ds")
                r.dma("sync", sb["sin"][:].rearrange("p (t f) -> p t f", t=4),
                      io["sinT2"][cc * 512:(cc + 1) * 512, :].rearrange("(t p) f -> p t f", p=128), "ds")
                for ts in range(4):
                    tile = cc * 4 + ts
                    ld = r.dma("sync", sb["x2t"][:], io["x2f"][tile * 128:(tile + 1) * 128, :], "ds")
                    r.wait("vector", "ds", r.cnt["ds"])
                    r.wait("scalar", "ds", r.cnt["ds"])
                    tn = h.ln_tile(sb["x2t"][:], sb["xln"][:])
                    r.wait("tensor", "ve", tn)
                    for kt in range(NKT):
                        h.transpose(sb["xln"][:, kt * 128:(kt + 1) * 128],
                                    self.vxlnT[:, kt, :])
                    xt_done = r.cnt["ve"]
                    r.wait("tensor", "ve", xt_done)
                    for kt in range(NKT):
                        for g in range(2):
                            r.op("tensor", lambda e, k=kt, G=g: e.matmul(
                                pbig[:, G * 512:(G + 1) * 512],
                                self.vxlnT[:, k, :],
                                self.vwq[:, k, G * 512:(G + 1) * 512],
                                start=(k == 0), stop=False, skip_group_check=True))
                            r.op("tensor", lambda e, k=kt, G=g: e.matmul(
                                pbig[:, 1024 + G * 512:1536 + G * 512],
                                self.vxlnT[:, k, :],
                                self.vwk[:, k, G * 512:(G + 1) * 512],
                                start=(k == 0), stop=False, skip_group_check=True))
                            r.op("tensor", lambda e, k=kt, G=g, P=(pA if g == 0 else pB): e.matmul(
                                P[:], self.vxlnT[:, k, :],
                                self.vwv[:, k, G * 512:(G + 1) * 512],
                                start=(k == 0), stop=False, skip_group_check=True))
                    for g in range(2):
                        r.op("tensor", lambda e, G=g: e.matmul(
                            pbig[:, G * 512:(G + 1) * 512], sb["ones1"][:],
                            sb["bqkv"][0:1, G * 512:(G + 1) * 512],
                            start=False, stop=True, skip_group_check=True))
                        r.op("tensor", lambda e, G=g: e.matmul(
                            pbig[:, 1024 + G * 512:1536 + G * 512], sb["ones1"][:],
                            sb["bqkv"][0:1, 1024 + G * 512:1536 + G * 512],
                            start=False, stop=True, skip_group_check=True))
                        r.op("tensor", lambda e, G=g, P=(pA if g == 0 else pB): e.matmul(
                            P[:], sb["ones1"][:],
                            sb["bqkv"][0:1, 2048 + G * 512:2560 + G * 512],
                            start=False, stop=True, skip_group_check=True), inc="pe")
                    qkv_pe = r.cnt["pe"]
                    r.wait("vector", "pe", qkv_pe)
                    r.cop("vector", lambda e: e.tensor_copy(sb["vst"][:, 0:512], pA[:]))
                    tvv = r.cop("vector", lambda e: e.tensor_copy(sb["vst"][:, 512:1024], pB[:]))
                    r.wait("tensor", "ve", tvv)
                    for hh in range(8):
                        h.transpose(sb["vst"][:, hh * 128:(hh + 1) * 128],
                                    self.vvT[:, hh, ts * 128:(ts + 1) * 128])
                    q3 = pbig[:, 0:1024].rearrange("p (h f) -> p h f", h=8)
                    qro3 = sb["sq"][:, 1024:2048].rearrange("p (h f) -> p h f", h=8)
                    trq = self.rope_rh(q3, qro3, 8, ts)
                    self.norm8(qro3, 8, dep_ve=trq)
                    tqh = r.cop("vector", lambda e, o=sb["qh"][:].rearrange("p (h f) -> p h f", h=8),
                                a=qro3,
                                b=sb["stat"][:, 8:16].unsqueeze(2).to_broadcast([128, 8, 128]):
                                e.tensor_tensor(o, a, b, ALU.mult))
                    r.wait("tensor", "ve", tqh)
                    for hh in range(8):
                        h.transpose(sb["qh"][:, hh * 128:(hh + 1) * 128],
                                    self.vqpT[:, hh, ts * 128:(ts + 1) * 128])
                    k3 = pbig[:, 1024:2048].rearrange("p (h f) -> p h f", h=8)
                    kro3 = sb["sq"][:, 1024:2048].rearrange("p (h f) -> p h f", h=8)
                    trk = self.rope_rh(k3, kro3, 8, ts)
                    self.norm8(kro3, 16, dep_ve=trk)
                    tkh = r.cop("vector", lambda e, o=sb["kh"][:].rearrange("p (h f) -> p h f", h=8),
                                a=kro3,
                                b=sb["stat"][:, 16:24].unsqueeze(2).to_broadcast([128, 8, 128]):
                                e.tensor_tensor(o, a, b, ALU.mult))
                    r.wait("tensor", "ve", tkh)
                    for hh in range(8):
                        h.transpose(sb["kh"][:, hh * 128:(hh + 1) * 128],
                                    self.vkpT[:, hh, ts * 128:(ts + 1) * 128])
                    r.barrier()
                pb_ev = [0, 0]
                for src_name, dst_name in (("qpT", "qpT"), ("kpT", "kpT")):
                    for hh in range(8):
                        i = hh % 2
                        pb_ = pbig[:, i * 512:(i + 1) * 512]
                        r.wait("tensor", "ve", pb_ev[i])
                        mm = r.op("tensor", lambda e, H_=hh, S_=src_name, P=pb_: e.matmul(
                            P[0:64, :], sb["RT"][:],
                            self.sb[S_][:].rearrange("p (h n) -> p h n", h=8)[:, H_, :],
                            start=True, stop=True, skip_group_check=True), inc="pe")
                        r.wait("vector", "pe", mm)
                        t1 = r.cop("vector", lambda e, P=pb_: e.tensor_scalar(
                            sb["ta"][0:64, :], P[0:64, :], INV2PI, MAGIC, ALU.mult, ALU.add))
                        t1 = r.cop("vector", lambda e: e.tensor_scalar(
                            sb["ta"][0:64, :], sb["ta"][0:64, :], MAGIC, TWOPI, ALU.subtract, ALU.mult))
                        t1 = r.cop("vector", lambda e, P=pb_: e.scalar_tensor_tensor(
                            sb["tb"][0:64, :], P[0:64, :], 1.0, sb["ta"][0:64, :], ALU.mult, ALU.subtract))
                        pb_ev[i] = t1
                        r.wait("scalar", "ve", t1)
                        dst = self.sb[dst_name][:].rearrange("p (h n) -> p h n", h=8)
                        ts1 = r.cop("scalar", lambda e, H_=hh, DD=dst: e.activation(
                            DD[0:64, H_, :], sb["tb"][0:64, :], AF.Sin))
                        ts1 = r.cop("scalar", lambda e: e.activation(
                            sb["ta"][0:64, :], sb["tb"][0:64, :], AF.Abs))
                        ts1 = r.cop("scalar", lambda e, H_=hh, DD=dst: e.activation(
                            DD[64:128, H_, :], sb["ta"][0:64, :], AF.Sin,
                            scale=-1.0, bias=sb["chalf"][:]))
                        r.wait("tensor", "se", ts1)
                    r.barrier()
                at_ev = [0, 0]
                o_ev = [0, 0]
                for tl in range(4):
                    for blk in range(8):
                        i = blk % 2
                        pat = (pA if i == 0 else pB)
                        po = pbig[:, (2 + i) * 512:(3 + i) * 512]
                        c0 = tl * 128 + blk * 16
                        tq = r.cop("vector", lambda e, C=c0: e.tensor_copy(
                            sb["qblk"][:].rearrange("p (h n) -> p h n", h=8),
                            self.sb["qpT"][:].rearrange("p (h n) -> p h n", h=8)[:, :, C:C + 16]))
                        tq = r.cop("vector", lambda e, C=c0: e.tensor_copy(
                            sb["kblk"][:].rearrange("p (h n) -> p h n", h=8),
                            self.sb["kpT"][:].rearrange("p (h n) -> p h n", h=8)[:, :, C:C + 16]))
                        r.wait("tensor", "ve", max(at_ev[i], tq))
                        mm = r.op("tensor", lambda e, P=pat: e.matmul(
                            P[:, 0:128], sb["kblk"][:], sb["qblk"][:],
                            start=True, stop=True, skip_group_check=True), inc="pe")
                        r.wait("vector", "pe", mm)
                        tv = r.cop("vector", lambda e, P=pat: e.tensor_tensor(
                            sb["atm"][:], P[:, 0:128], sb["mq"][:], ALU.mult))
                        at_ev[i] = tv
                        tvb = r.cop("vector", lambda e, C=c0: e.tensor_copy(
                            sb["vblk"][:].rearrange("p (h n) -> p h n", h=8),
                            self.sb["vT"][:].rearrange("p (h n) -> p h n", h=8)[:, :, C:C + 16]))
                        r.wait("tensor", "ve", tvb)
                        tvr = h.transpose(sb["vblk"][:], sb["varr"][:])
                        r.wait("tensor", "ve", tv)
                        r.wait("tensor", "ve", tvr)
                        r.wait("tensor", "ve", o_ev[i])
                        mo = r.op("tensor", lambda e, P=po: e.matmul(
                            P[:, 0:128], sb["atm"][:], sb["varr"][:],
                            start=True, stop=True, skip_group_check=True), inc="pe")
                        r.wait("vector", "pe", mo)
                        tc = r.cop("vector", lambda e, P=po: e.tensor_copy(sb["o1"][:], P[:, 0:128]))
                        o_ev[i] = tc
                        r.wait("tensor", "ve", tc)
                        i2 = h.pp["idx"] % 2
                        pt = h.pt[i2]
                        if h.pp["last"][i2] is not None:
                            r.wait("tensor", "ve", h.pp["last"][i2])
                        tp = r.op("tensor", lambda e, P=pt: e.transpose(
                            P[:], sb["o1"][:], h.sb["ident"][:]), inc="pe")
                        r.wait("vector", "pe", tp)
                        pos0 = (cc - half * 4) * 512 + c0
                        te = r.cop("vector", lambda e, P=pt, P0=pos0: e.tensor_copy(
                            self.vyTh[:, :, P0:P0 + 16],
                            P[:].rearrange("p (h n) -> p h n", h=8)))
                        h.pp["last"][i2] = te
                        h.pp["idx"] += 1
                r.barrier()
            tiles = [t for t in range(16) if t % 2 == half]
            vyv = self.sb["yTh"][:].rearrange("p (h n s) -> p h s n", h=8, s=16)
            for grp4 in range(2):
                tl4 = tiles[grp4 * 4:(grp4 + 1) * 4]
                ws_use = [0, 0]
                prev_ev = 0
                for colq in range(4):
                    for kt in range(NKT):
                        wbuf = kt % 2
                        wsem = "d0" if wbuf == 0 else "d1"
                        ld = r.dma("gpsimd", self.vwos[:, wbuf, :],
                                   io["l1_woT"][kt * 128:(kt + 1) * 128, colq * 512:(colq + 1) * 512],
                                   wsem, waits=[("pe", ws_use[wbuf])])
                        r.wait("tensor", wsem, ld)
                        for i_t, t in enumerate(tl4):
                            hl = t // 2
                            r.op("tensor", lambda e, K_=kt, IT=i_t, HL=hl, B=wbuf: e.matmul(
                                pbig[:, IT * 512:(IT + 1) * 512],
                                vyv[:, HL, K_, :], self.vwos[:, B, :],
                                start=(K_ == 0), stop=False, skip_group_check=True),
                                inc=("pe" if i_t == 3 else None))
                        ws_use[wbuf] = r.cnt["pe"]
                    for i_t in range(4):
                        r.op("tensor", lambda e, IT=i_t, C=colq: e.matmul(
                            pbig[:, IT * 512:(IT + 1) * 512], sb["ones1"][:],
                            sb["bo"][0:1, C * 512:(C + 1) * 512],
                            start=False, stop=True, skip_group_check=True),
                            inc=("pe" if i_t == 3 else None))
                    ope = r.cnt["pe"]
                    x3st = 0
                    for i_t, t in enumerate(tl4):
                        ld = r.dma("sync", sb["x2o"][:],
                                   io["x2own"][t * 128:(t + 1) * 128, colq * 512:(colq + 1) * 512],
                                   "ds", waits=[("ve", r.cnt["ve"])])
                        r.wait("vector", "ds", r.cnt["ds"])
                        r.wait("vector", "pe", ope)
                        r.wait("vector", "st", x3st)
                        tv = r.cop("vector", lambda e, IT=i_t: e.tensor_tensor(
                            sb["x3p"][:], pbig[:, IT * 512:(IT + 1) * 512],
                            sb["x2o"][:], ALU.add))
                        st = r.dma("sync", io["x3"][t * 128:(t + 1) * 128, colq * 512:(colq + 1) * 512],
                                   sb["x3p"][:], "st", waits=[("ve", tv)])
                        x3st = st
                    prev_ev = r.cnt["ve"]
                    r.barrier()
                r.barrier()
        return self


def declare_io_phase2(nc):
    io = {}
    io["l1_wqT"] = nc.dram_tensor("l1_wqT", [D, 1024], F32, kind="ExternalInput").ap()
    io["l1_wkT"] = nc.dram_tensor("l1_wkT", [D, 1024], F32, kind="ExternalInput").ap()
    io["l1_wvT"] = nc.dram_tensor("l1_wvT", [D, 1024], F32, kind="ExternalInput").ap()
    io["l1_woT"] = nc.dram_tensor("l1_woT", [D, D], F32, kind="ExternalInput").ap()
    io["l1_bq"] = nc.dram_tensor("l1_bq", [1, 1024], F32, kind="ExternalInput").ap()
    io["l1_bk"] = nc.dram_tensor("l1_bk", [1, 1024], F32, kind="ExternalInput").ap()
    io["l1_bv"] = nc.dram_tensor("l1_bv", [1, 1024], F32, kind="ExternalInput").ap()
    io["l1_bo"] = nc.dram_tensor("l1_bo", [1, 2048], F32, kind="ExternalInput").ap()
    io["l1_RT"] = nc.dram_tensor("l1_RT", [128, 64], F32, kind="ExternalInput").ap()
    io["l1_mq"] = nc.dram_tensor("l1_mq", [128, 128], F32, kind="ExternalInput").ap()
    io["cosT2"] = nc.dram_tensor("cosT2", [4096, 64], F32, kind="ExternalInput").ap()
    io["sinT2"] = nc.dram_tensor("sinT2", [4096, 64], F32, kind="ExternalInput").ap()
    return io


def build_p2_program(stop=None):
    nc = bass.Bass("TRN2", target_bir_lowering=False)
    io = declare_io_phase2(nc)
    io["x2f"] = nc.dram_tensor("x2f", [2 * TOK, D], F32, kind="ExternalInput").ap()
    io["x2own"] = nc.dram_tensor("x2own", [TOK, D], F32, kind="ExternalInput").ap()
    io["x3"] = nc.dram_tensor("x3", [TOK, D], F32, kind="ExternalOutput").ap()
    with ExitStack() as ctx:
        r = Rec(nc, ctx)
        Phase2(r, ctx, nc, io).emit(stop=stop)
        r.wait("sync", "st", r.cnt["st"])
        blk = ctx.enter_context(nc.Block())
        r.replay(blk)
    return nc


def prep_phase2_inputs(Wq, bq, Wk, bk, Wv, bv, Wo, bo, rand_matrix):
    WqT = np.ascontiguousarray(np.asarray(Wq, np.float32).T)
    out = {}
    out["l1_wkT"] = np.ascontiguousarray(np.asarray(Wk, np.float32).T)
    out["l1_wvT"] = np.ascontiguousarray(np.asarray(Wv, np.float32).T)
    out["l1_woT"] = np.ascontiguousarray(np.asarray(Wo, np.float32).T)
    out["l1_bk"] = np.asarray(bk, np.float32).reshape(1, -1)
    out["l1_bv"] = np.asarray(bv, np.float32).reshape(1, -1)
    out["l1_bo"] = np.asarray(bo, np.float32).reshape(1, -1)
    out["l1_RT"] = np.ascontiguousarray(np.asarray(rand_matrix, np.float32).T)
    mq = np.zeros((128, 128), np.float32)
    for kv in range(8):
        for p in range(16):
            for hl in range(8):
                mq[kv * 16 + p, hl * 16 + p] = 1.0 / 64.0
    out["l1_mq"] = mq
    inv = 1.0 / 10000.0 ** (np.arange(0, DH, 2, dtype=np.float64) / DH)
    ang = np.outer(np.arange(4096, dtype=np.float64), inv)
    out["cosT2"] = np.ascontiguousarray(np.cos(ang).astype(np.float32))
    out["sinT2"] = np.ascontiguousarray(np.sin(ang).astype(np.float32))
    bq = np.asarray(bq, np.float32).reshape(-1)
    per_core = []
    for c in range(8):
        h_core = c % 2
        per_core.append(dict(
            l1_wqT=np.ascontiguousarray(WqT[:, h_core * 1024:(h_core + 1) * 1024]),
            l1_bq=np.ascontiguousarray(bq[h_core * 1024:(h_core + 1) * 1024].reshape(1, -1)),
        ))
    return out, per_core


class Phase3:

    def __init__(self, r, ctx, nc, io):
        self.r, self.nc, self.io = r, nc, io
        self.h = Helpers(r, ctx, nc, "p3_")
        sb = self.sb = self.h.sb

        def S(name, shape, dt):
            sb[name] = ctx.enter_context(nc.sbuf_tensor("p3_" + name, shape, dt))
        S("x3t", [128, D], F32)
        S("xln", [128, D], BF16)
        S("xlnT", [128, NKT * 128], BF16)
        S("RgT", [128, NKT * 128], BF16)
        S("RuT", [128, NKT * 128], BF16)
        S("sRgT", [128, NKT * 128], BF16)
        S("sRuT", [128, NKT * 128], BF16)
        S("gwT", [128, NKT * 32], BF16)
        S("rgT", [128, 2048], BF16)
        S("ruT", [128, 2048], BF16)
        S("rsgT", [128, 2048], BF16)
        S("rsuT", [128, 2048], BF16)
        S("we4", [128, 16 * 32], F32)
        S("weT", [32, 2048], F32)
        S("e32", [128, 32], F32)
        S("cur", [128, 32], F32)
        S("msk", [128, 32], F32)
        S("wef", [1, 2048], F32)
        S("onesf", [1, 128], F32)
        S("identf", [128, 128], F32)
        S("UgT", [128, 2 * 128], BF16)
        S("UuT", [128, 2 * 128], BF16)
        S("UdT", [128, 2 * 128], BF16)
        S("CgT", [128, 2 * 11 * 128], BF16)
        S("CuT", [128, 2 * 11 * 128], BF16)
        S("sUgT", [128, 128], BF16)
        S("sUuT", [128, 128], BF16)
        S("sUdT", [128, 128], BF16)
        S("sCgT", [128, 22 * 128], BF16)
        S("sCuT", [128, 22 * 128], BF16)
        S("sRdT", [128, 22 * 128], BF16)
        S("RdT", [128, 11 * 128], BF16)
        S("t1g", [128, 512], BF16)
        S("t1u", [128, 512], BF16)
        S("g1", [128, 512], BF16)
        S("u1", [128, 512], BF16)
        S("m1", [128, 512], BF16)
        S("rdb", [128, 512], BF16)
        S("t2b", [128, 512], BF16)
        S("t2T", [128, 17 * 2048], BF16)
        S("CdTg", [128, 17 * 512], BF16)
        S("yp", [128, 512], F32)
        S("x3r", [128, 512], F32)
        S("x4p", [128, 512], F32)
        self.pbig = ctx.enter_context(nc.psum_tensor("p3_pbig", [128, 2048], F32))
        self.pA = ctx.enter_context(nc.psum_tensor("p3_pA", [128, 512], F32))
        self.pB = ctx.enter_context(nc.psum_tensor("p3_pB", [128, 512], F32))
        self.vxlnT = sb["xlnT"][:].rearrange("p (k n) -> p k n", k=NKT)
        self.vRgT = sb["RgT"][:].rearrange("p (k n) -> p k n", k=NKT)
        self.vRuT = sb["RuT"][:].rearrange("p (k n) -> p k n", k=NKT)
        self.vsRgT = sb["sRgT"][:].rearrange("p (k n) -> p k n", k=NKT)
        self.vsRuT = sb["sRuT"][:].rearrange("p (k n) -> p k n", k=NKT)
        self.vgwT = sb["gwT"][:].rearrange("p (k n) -> p k n", k=NKT)
        self.vwe4 = sb["we4"][:].rearrange("p (t n) -> p t n", t=16)
        self.vUgT = sb["UgT"][:].rearrange("p (b n) -> p b n", b=2)
        self.vUuT = sb["UuT"][:].rearrange("p (b n) -> p b n", b=2)
        self.vUdT = sb["UdT"][:].rearrange("p (b n) -> p b n", b=2)
        self.vCgT = sb["CgT"][:].rearrange("p (b k n) -> p b k n", b=2, k=11)
        self.vCuT = sb["CuT"][:].rearrange("p (b k n) -> p b k n", b=2, k=11)
        self.vsCgT = sb["sCgT"][:].rearrange("p (k n) -> p k n", k=22)
        self.vsCuT = sb["sCuT"][:].rearrange("p (k n) -> p k n", k=22)
        self.vsRdT = sb["sRdT"][:].rearrange("p (k n) -> p k n", k=22)
        self.vRdT = sb["RdT"][:].rearrange("p (k n) -> p k n", k=11)
        self.vt2T = sb["t2T"][:].rearrange("p (s n) -> p s n", s=17)
        self.vCdTg = sb["CdTg"][:].rearrange("p (s n) -> p s n", s=17)

    def lowrank_chunk(self, cc, slot, nit, t1w_g, t1w_u, cg, cu, rdw, udw, rg_src, ru_src, wecol):
        r, sb = self.r, self.sb
        pbig, pA, pB = self.pbig, self.pA, self.pB
        csl = slice(cc * 512, (cc + 1) * 512)
        m1_ = r.op("tensor", lambda e, W=t1w_g, S_=rg_src: e.matmul(
            pbig[:, 0:512], W, S_[:, csl], start=True, stop=True, skip_group_check=True))
        m2_ = r.op("tensor", lambda e, W=t1w_u, S_=ru_src: e.matmul(
            pbig[:, 512:1024], W, S_[:, csl], start=True, stop=True, skip_group_check=True), inc="pe")
        t1pe = r.cnt["pe"]
        r.wait("vector", "pe", t1pe)
        r.cop("vector", lambda e: e.tensor_copy(sb["t1g"][:], pbig[:, 0:512]))
        tv = r.cop("vector", lambda e: e.tensor_copy(sb["t1u"][:], pbig[:, 512:1024]))
        r.wait("tensor", "ve", tv)
        g_ev = 0
        for it in range(nit):
            r.wait("tensor", "ve", g_ev)
            r.op("tensor", lambda e, I=it, C=cg: e.matmul(
                pA[:], C(I), sb["t1g"][:], start=True, stop=True, skip_group_check=True))
            mmu = r.op("tensor", lambda e, I=it, C=cu: e.matmul(
                pB[:], C(I), sb["t1u"][:], start=True, stop=True, skip_group_check=True), inc="pe")
            r.wait("scalar", "pe", mmu)
            tsg = r.cop("scalar", lambda e: e.activation(sb["g1"][:], pA[:], AF.Silu))
            r.wait("vector", "pe", mmu)
            r.cop("vector", lambda e: e.tensor_copy(sb["u1"][:], pB[:]))
            r.wait("vector", "se", tsg)
            tv = r.cop("vector", lambda e: e.tensor_tensor(sb["m1"][:], sb["g1"][:], sb["u1"][:], ALU.mult))
            g_ev = tv
            r.wait("tensor", "ve", tv)
            r.op("tensor", lambda e, I=it, W=rdw: e.matmul(
                pbig[:, 1024:1536], W(I), sb["m1"][:],
                start=(I == 0), stop=(I == nit - 1), skip_group_check=True),
                inc=("pe" if it == nit - 1 else None))
        rdpe = r.cnt["pe"]
        r.wait("vector", "pe", rdpe)
        tv = r.cop("vector", lambda e: e.tensor_copy(sb["rdb"][:], pbig[:, 1024:1536]))
        r.wait("tensor", "ve", tv)
        mt2 = r.op("tensor", lambda e, W=udw: e.matmul(
            pbig[:, 1536:2048], W, sb["rdb"][:], start=True, stop=True, skip_group_check=True), inc="pe")
        if wecol is not None:
            mb = r.op("tensor", lambda e, WC=wecol: e.matmul(
                pA[:], sb["onesf"][:], WC, start=True, stop=True, skip_group_check=True), inc="pe")
            r.wait("vector", "pe", r.cnt["pe"])
            tv = r.cop("vector", lambda e: e.tensor_copy(sb["t2b"][:], pbig[:, 1536:2048]))
            tv = r.cop("vector", lambda e, SL=slot, C=cc: e.tensor_tensor(
                self.vt2T[:, SL, C * 512:(C + 1) * 512], sb["t2b"][:], pA[:], ALU.mult))
        else:
            r.wait("vector", "pe", r.cnt["pe"])
            tv = r.cop("vector", lambda e, SL=slot, C=cc: e.tensor_copy(
                self.vt2T[:, SL, C * 512:(C + 1) * 512], pbig[:, 1536:2048]))
        r.wait("tensor", "ve", tv)
        return tv

    def emit(self, stop=None):
        r, sb, io = self.r, self.sb, self.io
        h = self.h
        pbig, pA, pB = self.pbig, self.pA, self.pB
        r.op("vector", lambda e: e.memset(sb["onesf"][:], 1.0), inc="ve")
        tg = r.op("gpsimd", lambda e: e.memset(sb["identf"][:], 1.0), inc="ge")
        r.wait("gpsimd", "ge", tg)
        r.op("gpsimd", lambda e: e.affine_select(sb["identf"][:], sb["identf"][:],
             pattern=[[1, 128]], compare_op=ALU.is_equal, fill=0.0, base=0,
             channel_multiplier=-1), inc="ge")
        r.dma("gpsimd", sb["RgT"][:].rearrange("p (k n) -> p k n", k=NKT),
              io["l1f_RgT"].rearrange("(k p) n -> p k n", p=128), "dg")
        r.dma("gpsimd", sb["RuT"][:].rearrange("p (k n) -> p k n", k=NKT),
              io["l1f_RuT"].rearrange("(k p) n -> p k n", p=128), "dg")
        r.dma("gpsimd", sb["sRgT"][:].rearrange("p (k n) -> p k n", k=NKT),
              io["l1f_sRgT"].rearrange("(k p) n -> p k n", p=128), "dg")
        r.dma("gpsimd", sb["sRuT"][:].rearrange("p (k n) -> p k n", k=NKT),
              io["l1f_sRuT"].rearrange("(k p) n -> p k n", p=128), "dg")
        r.dma("gpsimd", sb["gwT"][:].rearrange("p (k n) -> p k n", k=NKT),
              io["l1f_gwT"].rearrange("(k p) n -> p k n", p=128), "dg")
        r.dma("gpsimd", sb["sUgT"][:], io["l1f_sUgT"], "dg")
        r.dma("gpsimd", sb["sUuT"][:], io["l1f_sUuT"], "dg")
        r.dma("gpsimd", sb["sUdT"][:], io["l1f_sUdT"], "dg")
        r.dma("gpsimd", sb["sCgT"][:].rearrange("p (k n) -> p k n", k=22),
              io["l1f_sCgT"].rearrange("p (k n) -> p k n", k=22), "dg")
        r.dma("gpsimd", sb["sCuT"][:].rearrange("p (k n) -> p k n", k=22),
              io["l1f_sCuT"].rearrange("p (k n) -> p k n", k=22), "dg")
        r.dma("gpsimd", sb["sRdT"][:].rearrange("p (k n) -> p k n", k=22),
              io["l1f_sRdT"].rearrange("(k p) n -> p k n", p=128), "dg")
        r.dma("gpsimd", sb["RdT"][:].rearrange("p (k n) -> p k n", k=11),
              io["l1f_RdT"].rearrange("(k p) n -> p k n", p=128), "dg")
        r.barrier()

        for t in range(16):
            ld = r.dma("sync", sb["x3t"][:], io["x3"][t * 128:(t + 1) * 128, :], "ds")
            r.wait("vector", "ds", r.cnt["ds"])
            r.wait("scalar", "ds", r.cnt["ds"])
            tn = h.ln_tile(sb["x3t"][:], sb["xln"][:])
            r.wait("tensor", "ve", tn)
            for kt in range(NKT):
                h.transpose(sb["xln"][:, kt * 128:(kt + 1) * 128], self.vxlnT[:, kt, :])
            r.wait("tensor", "ve", r.cnt["ve"])
            for name, wsrc in (("rgT", self.vRgT), ("ruT", self.vRuT),
                               ("rsgT", self.vsRgT), ("rsuT", self.vsRuT)):
                for kt in range(NKT):
                    r.op("tensor", lambda e, k=kt, W=wsrc, N=name: e.matmul(
                        pA[:, 0:128], W[:, k, :], self.vxlnT[:, k, :],
                        start=(k == 0), stop=(k == NKT - 1), skip_group_check=True),
                        inc=("pe" if kt == NKT - 1 else None))
                pe_ = r.cnt["pe"]
                r.wait("vector", "pe", pe_)
                tv = r.cop("vector", lambda e, N=name, T=t: e.tensor_copy(
                    sb[N][:, T * 128:(T + 1) * 128], pA[:, 0:128]))
                r.wait("tensor", "ve", tv)
            for kt in range(NKT):
                r.op("tensor", lambda e, k=kt: e.matmul(
                    pB[:, 0:32], self.vxlnT[:, k, :], self.vgwT[:, k, :],
                    start=(k == 0), stop=(k == NKT - 1), skip_group_check=True),
                    inc=("pe" if kt == NKT - 1 else None))
            lg = r.cnt["pe"]
            st = sb["stat"]
            mx, s4, mi, rcp = (st[:, 8 + i:9 + i] for i in range(4))
            r.wait("vector", "pe", lg)
            r.cop("vector", lambda e: e.reduce_max(mx, pB[:, 0:32], axis=AX.X))
            tv = r.cop("vector", lambda e: e.tensor_scalar(mx, mx, -1.0, None, ALU.mult))
            r.wait("scalar", "ve", tv)
            te = r.cop("scalar", lambda e: e.activation(sb["e32"][:], pB[:, 0:32], AF.Exp, bias=mx))
            r.wait("vector", "se", te)
            r.cop("vector", lambda e: e.tensor_copy(sb["cur"][:], sb["e32"][:]))
            r.cop("vector", lambda e: e.memset(s4, 0.0))
            for rnd in range(4):
                r.cop("vector", lambda e: e.reduce_max(mi, sb["cur"][:], axis=AX.X))
                r.cop("vector", lambda e: e.tensor_tensor(s4, s4, mi, ALU.add))
                if rnd < 3:
                    r.cop("vector", lambda e: e.tensor_scalar(sb["msk"][:], sb["cur"][:], mi, None, ALU.is_lt))
                    r.cop("vector", lambda e: e.tensor_tensor(sb["cur"][:], sb["cur"][:], sb["msk"][:], ALU.mult))
                    r.cop("vector", lambda e: e.tensor_scalar(sb["msk"][:], sb["msk"][:], 1e30, -1e30, ALU.mult, ALU.add))
                    r.cop("vector", lambda e: e.tensor_tensor(sb["cur"][:], sb["cur"][:], sb["msk"][:], ALU.add))
            r.cop("vector", lambda e: e.reciprocal(rcp, s4))
            r.cop("vector", lambda e: e.tensor_scalar(sb["msk"][:], sb["e32"][:], mi, None, ALU.is_ge))
            twe = r.cop("vector", lambda e, T=t: e.scalar_tensor_tensor(
                self.vwe4[:, T, :], sb["e32"][:], rcp, sb["msk"][:], ALU.mult, ALU.mult))
            r.wait("tensor", "ve", twe)
            tpw = r.op("tensor", lambda e, T=t: e.transpose(
                pA[0:32, 0:128], self.vwe4[:, T, :], sb["identf"][:]), inc="pe")
            r.wait("vector", "pe", tpw)
            tv = r.cop("vector", lambda e, T=t: e.tensor_copy(
                sb["weT"][0:32, T * 128:(T + 1) * 128], pA[0:32, 0:128]))
            r.wait("tensor", "ve", tv)
            r.barrier()
        st_w = r.dma("sync", io["weT_hbm"], sb["weT"][:], "st")
        r.barrier()

        CG_NIT, S_NIT = 11, 22
        for half in range(2):
            ew_use = [0, 0]
            for ei in range(16):
                e_glob = half * 16 + ei
                b = ei % 2
                wsem = "d0" if b == 0 else "d1"
                r.dma("gpsimd", self.vUgT[:, b, :], io["l1f_UgT"][e_glob], wsem,
                      waits=[("pe", ew_use[b])])
                r.dma("gpsimd", self.vUuT[:, b, :], io["l1f_UuT"][e_glob], wsem)
                r.dma("gpsimd", self.vUdT[:, b, :], io["l1f_UdT"][e_glob], wsem)
                r.dma("gpsimd", self.vCgT[:, b, :, :].rearrange("p k n -> p (k n)"),
                      io["l1f_CgT"][e_glob], wsem)
                ldw = r.dma("gpsimd", self.vCuT[:, b, :, :].rearrange("p k n -> p (k n)"),
                            io["l1f_CuT"][e_glob], wsem)
                r.wait("tensor", wsem, ldw)
                ldwe = r.dma("sync", sb["wef"][:], io["weT_hbm"][e_glob:e_glob + 1, :], "ds",
                             waits=[("st", st_w), ("pe", r.cnt["pe"])])
                r.wait("tensor", "ds", r.cnt["ds"])
                for cc in range(4):
                    self.lowrank_chunk(
                        cc, ei, CG_NIT,
                        sb["UgT"][:, b * 128:(b + 1) * 128],
                        sb["UuT"][:, b * 128:(b + 1) * 128],
                        (lambda I, B=b: self.vCgT[:, B, I, :]),
                        (lambda I, B=b: self.vCuT[:, B, I, :]),
                        (lambda I: self.vRdT[:, I, :]),
                        sb["UdT"][:, b * 128:(b + 1) * 128],
                        sb["rgT"], sb["ruT"],
                        sb["wef"][0:1, cc * 512:(cc + 1) * 512])
                ew_use[b] = r.cnt["pe"]
            if half == 0:
                for cc in range(4):
                    self.lowrank_chunk(
                        cc, 16, S_NIT,
                        sb["sUgT"][:], sb["sUuT"][:],
                        (lambda I: self.vsCgT[:, I, :]),
                        (lambda I: self.vsCuT[:, I, :]),
                        (lambda I: self.vsRdT[:, I, :]),
                        sb["sUdT"][:],
                        sb["rsgT"], sb["rsuT"], None)
            r.barrier()
            nslot = 17 if half == 0 else 16
            for grp in range(4):
                for s_i in range(nslot):
                    if s_i < 16:
                        srcap = io["l1f_CdT"][half * 16 + s_i][0:128, grp * 512:(grp + 1) * 512]
                    else:
                        srcap = io["l1f_sCdT"][0:128, grp * 512:(grp + 1) * 512]
                    r.dma("gpsimd", self.vCdTg[:, s_i, :], srcap, "dg")
                ldc = r.cnt["dg"]
                r.wait("tensor", "dg", ldc)
                x4st = 0
                for t in range(16):
                    pb_ = pbig[:, (t % 4) * 512:((t % 4) + 1) * 512]
                    for s_i in range(nslot):
                        r.op("tensor", lambda e, SI=s_i, T=t, P=pb_: e.matmul(
                            P, self.vt2T[:, SI, T * 128:(T + 1) * 128],
                            self.vCdTg[:, SI, :],
                            start=(SI == 0), stop=(SI == nslot - 1), skip_group_check=True),
                            inc=("pe" if s_i == nslot - 1 else None))
                    dpe = r.cnt["pe"]
                    if half == 0:
                        r.wait("vector", "pe", dpe)
                        r.wait("vector", "st", x4st)
                        tv = r.cop("vector", lambda e, P=pb_: e.tensor_copy(sb["yp"][:], P))
                        stp = r.dma("sync", io["y_part"][t * 128:(t + 1) * 128, grp * 512:(grp + 1) * 512],
                                    sb["yp"][:], "st", waits=[("ve", tv)])
                        x4st = stp
                    else:
                        ld1 = r.dma("sync", sb["yp"][:],
                                    io["y_part"][t * 128:(t + 1) * 128, grp * 512:(grp + 1) * 512],
                                    "ds", waits=[("ve", r.cnt["ve"])])
                        ld2 = r.dma("sync", sb["x3r"][:],
                                    io["x3"][t * 128:(t + 1) * 128, grp * 512:(grp + 1) * 512], "ds")
                        r.wait("vector", "ds", r.cnt["ds"])
                        r.wait("vector", "pe", dpe)
                        r.wait("vector", "st", x4st)
                        r.cop("vector", lambda e, P=pb_: e.tensor_tensor(
                            sb["x4p"][:], P, sb["yp"][:], ALU.add))
                        tv = r.cop("vector", lambda e: e.tensor_tensor(
                            sb["x4p"][:], sb["x4p"][:], sb["x3r"][:], ALU.add))
                        stp = r.dma("sync", io["x4"][t * 128:(t + 1) * 128, grp * 512:(grp + 1) * 512],
                                    sb["x4p"][:], "st", waits=[("ve", tv)])
                        x4st = stp
                    if t % 4 == 3:
                        r.barrier(("tensor", "vector"))
                r.barrier()
            r.barrier()
        return self


def declare_io_phase3(nc):
    io = {}
    io["l1f_gwT"] = nc.dram_tensor("l1f_gwT", [D, 32], F32, kind="ExternalInput").ap()
    io["l1f_RgT"] = nc.dram_tensor("l1f_RgT", [D, 128], F32, kind="ExternalInput").ap()
    io["l1f_RuT"] = nc.dram_tensor("l1f_RuT", [D, 128], F32, kind="ExternalInput").ap()
    io["l1f_RdT"] = nc.dram_tensor("l1f_RdT", [1408, 128], F32, kind="ExternalInput").ap()
    io["l1f_UgT"] = nc.dram_tensor("l1f_UgT", [32, 128, 128], F32, kind="ExternalInput").ap()
    io["l1f_CgT"] = nc.dram_tensor("l1f_CgT", [32, 128, 1408], F32, kind="ExternalInput").ap()
    io["l1f_UuT"] = nc.dram_tensor("l1f_UuT", [32, 128, 128], F32, kind="ExternalInput").ap()
    io["l1f_CuT"] = nc.dram_tensor("l1f_CuT", [32, 128, 1408], F32, kind="ExternalInput").ap()
    io["l1f_UdT"] = nc.dram_tensor("l1f_UdT", [32, 128, 128], F32, kind="ExternalInput").ap()
    io["l1f_CdT"] = nc.dram_tensor("l1f_CdT", [32, 128, 2048], F32, kind="ExternalInput").ap()
    io["l1f_sRgT"] = nc.dram_tensor("l1f_sRgT", [D, 128], F32, kind="ExternalInput").ap()
    io["l1f_sUgT"] = nc.dram_tensor("l1f_sUgT", [128, 128], F32, kind="ExternalInput").ap()
    io["l1f_sCgT"] = nc.dram_tensor("l1f_sCgT", [128, 2816], F32, kind="ExternalInput").ap()
    io["l1f_sRuT"] = nc.dram_tensor("l1f_sRuT", [D, 128], F32, kind="ExternalInput").ap()
    io["l1f_sUuT"] = nc.dram_tensor("l1f_sUuT", [128, 128], F32, kind="ExternalInput").ap()
    io["l1f_sCuT"] = nc.dram_tensor("l1f_sCuT", [128, 2816], F32, kind="ExternalInput").ap()
    io["l1f_sRdT"] = nc.dram_tensor("l1f_sRdT", [2816, 128], F32, kind="ExternalInput").ap()
    io["l1f_sUdT"] = nc.dram_tensor("l1f_sUdT", [128, 128], F32, kind="ExternalInput").ap()
    io["l1f_sCdT"] = nc.dram_tensor("l1f_sCdT", [128, 2048], F32, kind="ExternalInput").ap()
    io["weT_hbm"] = nc.dram_tensor("weT_hbm", [32, 2048], F32, kind="Internal").ap()
    io["y_part"] = nc.dram_tensor("y_part", [TOK, D], F32, kind="Internal").ap()
    return io


def build_p3_program(stop=None):
    nc = bass.Bass("TRN2", target_bir_lowering=False)
    io = declare_io_phase3(nc)
    io["x3"] = nc.dram_tensor("x3", [TOK, D], F32, kind="ExternalInput").ap()
    io["x4"] = nc.dram_tensor("x4", [TOK, D], F32, kind="ExternalOutput").ap()
    with ExitStack() as ctx:
        r = Rec(nc, ctx)
        Phase3(r, ctx, nc, io).emit(stop=stop)
        r.wait("sync", "st", r.cnt["st"])
        blk = ctx.enter_context(nc.Block())
        r.replay(blk)
    return nc


def prep_phase3_weights(ins):
    T = lambda a: np.ascontiguousarray(np.asarray(a, np.float32).T)
    T3 = lambda a: np.ascontiguousarray(np.asarray(a, np.float32).transpose(0, 2, 1))
    return dict(
        l1f_gwT=T(ins["L1_ffn_gate_w"]), l1f_RgT=T(ins["L1_ffn_Rg"]),
        l1f_RuT=T(ins["L1_ffn_Ru"]), l1f_RdT=T(ins["L1_ffn_Rd"]),
        l1f_UgT=T3(ins["L1_ffn_Ug"]), l1f_CgT=T3(ins["L1_ffn_Cg"]),
        l1f_UuT=T3(ins["L1_ffn_Uu"]), l1f_CuT=T3(ins["L1_ffn_Cu"]),
        l1f_UdT=T3(ins["L1_ffn_Ud"]), l1f_CdT=T3(ins["L1_ffn_Cd"]),
        l1f_sRgT=T(ins["L1_ffn_s_Rg"]), l1f_sUgT=T(ins["L1_ffn_s_Ug"]),
        l1f_sCgT=T(ins["L1_ffn_s_Cg"]), l1f_sRuT=T(ins["L1_ffn_s_Ru"]),
        l1f_sUuT=T(ins["L1_ffn_s_Uu"]), l1f_sCuT=T(ins["L1_ffn_s_Cu"]),
        l1f_sRdT=T(ins["L1_ffn_s_Rd"]), l1f_sUdT=T(ins["L1_ffn_s_Ud"]),
        l1f_sCdT=T(ins["L1_ffn_s_Cd"]))


def declare_io_phase0(nc):
    io = {}
    io["x0c"] = nc.dram_tensor("x0c", [TOT, D], F32, kind="ExternalInput").ap()
    io["l0_wqT"] = nc.dram_tensor("l0_wqT", [D, D], F32, kind="ExternalInput").ap()
    io["l0_wkT"] = nc.dram_tensor("l0_wkT", [D, 512], F32, kind="ExternalInput").ap()
    io["l0_wvT"] = nc.dram_tensor("l0_wvT", [D, 512], F32, kind="ExternalInput").ap()
    io["l0_woT"] = nc.dram_tensor("l0_woT", [D, D], F32, kind="ExternalInput").ap()
    io["cosT"] = nc.dram_tensor("cosT", [TOT, 64], F32, kind="ExternalInput").ap()
    io["sinT"] = nc.dram_tensor("sinT", [TOT, 64], F32, kind="ExternalInput").ap()
    io["valid"] = nc.dram_tensor("valid", [128, 6], F32, kind="ExternalInput").ap()
    return io


def build_p0_program(x1_kind="ExternalOutput", stop=None, dump_it=2):
    nc = bass.Bass("TRN2", target_bir_lowering=False)
    io = declare_io_phase0(nc)
    io["x1"] = nc.dram_tensor("x1", [TOK, D], F32, kind=x1_kind).ap()
    if stop is not None:
        io["dbg0"] = nc.dram_tensor("dbg0", [128, 16 * 512], BF16, kind="ExternalOutput").ap()
        io["dbg1"] = nc.dram_tensor("dbg1", [128, 16 * 512], BF16, kind="ExternalOutput").ap()
    with ExitStack() as ctx:
        r = Rec(nc, ctx)
        Phase0(r, ctx, nc, io).emit(stop=stop, dump_it=dump_it)
        r.wait("sync", "st", r.cnt["st"])
        blk = ctx.enter_context(nc.Block())
        r.replay(blk)
    return nc



def prep_phase0_inputs(x, wq, wk, wv, wo):
    x = np.asarray(x, dtype=np.float32)
    inv = (1.0 / 10000.0 ** (np.arange(0, DH, 2, dtype=np.float64) / DH))
    wqT = np.ascontiguousarray(np.asarray(wq, np.float32).T)
    wkT = np.ascontiguousarray(np.asarray(wk, np.float32).T)
    wvT = np.ascontiguousarray(np.asarray(wv, np.float32).T)
    woT = np.ascontiguousarray(np.asarray(wo, np.float32).T)
    maps = []
    for c in range(8):
        b, h = c // 2, c % 2
        own = x[b, h * TOK:(h + 1) * TOK]
        halo = x[b, TOK - HALO:TOK] if h == 1 else np.zeros((HALO, D), np.float32)
        x0c = np.ascontiguousarray(np.concatenate([halo, own], axis=0))
        pos = (h * TOK - HALO) + np.arange(TOT, dtype=np.float64)
        pos = np.maximum(pos, 0.0)
        ang = np.outer(pos, inv)
        cosT = np.ascontiguousarray(np.cos(ang).astype(np.float32))
        sinT = np.ascontiguousarray(np.sin(ang).astype(np.float32))
        valid = np.ones((128, 6), np.float32)
        if h == 0:
            valid[:, 0:2] = 0.0
        maps.append(dict(x0c=x0c, l0_wqT=wqT, l0_wkT=wkT, l0_wvT=wvT, l0_woT=woT,
                         cosT=cosT, sinT=sinT, valid=valid))
    return maps


def build_full_program():
    nc = bass.Bass("TRN2", target_bir_lowering=False)
    io = {}
    io.update(declare_io_phase0(nc))
    io["x1"] = nc.dram_tensor("x1", [TOK, D], F32, kind="Internal").ap()
    io.update(declare_io_phase1(nc))
    io["x2"] = nc.dram_tensor("x2", [TOK, D], F32, kind="Internal").ap()
    io.update(declare_io_phase2(nc))
    io["batchX2"] = nc.dram_tensor("batchX2", [2 * TOK, D], F32, kind="Internal").ap()
    io["x2f"] = io["batchX2"]
    io["x2own"] = io["x2"]
    io["x3"] = nc.dram_tensor("x3", [TOK, D], F32, kind="Internal").ap()
    io.update(declare_io_phase3(nc))
    io["x4"] = nc.dram_tensor("x4", [TOK, D], F32, kind="ExternalOutput").ap()
    with ExitStack() as ctx:
        r = Rec(nc, ctx)
        with ExitStack() as c0:
            Phase0(r, c0, nc, io).emit()
            r.barrier()
        with ExitStack() as c1:
            Phase1(r, c1, nc, io).emit()
            r.barrier()
        cc_t = r.op("gpsimd", lambda e: e.collective_compute(
            "AllGather", ALU.bypass,
            replica_groups=[[0, 1], [2, 3], [4, 5], [6, 7]],
            ins=[io["x2"].opt()], outs=[io["batchX2"].opt()]), inc="dg", inc_by=16)
        r.wait("gpsimd", "dg", cc_t)
        r.barrier()
        with ExitStack() as c2:
            Phase2(r, c2, nc, io).emit()
            r.barrier()
        with ExitStack() as c3:
            Phase3(r, c3, nc, io).emit()
        r.wait("sync", "st", r.cnt["st"])
        blk = ctx.enter_context(nc.Block())
        r.replay(blk)
    return nc


def prep_full_maps(inputs):
    maps0 = prep_phase0_inputs(inputs["L0_attn_x"], inputs["L0_attn_wq"],
                               inputs["L0_attn_wk"], inputs["L0_attn_wv"],
                               inputs["L0_attn_wo"])
    rgT, wgu, wd = prep_phase1_weights(inputs["L0_ffn_router_gate"],
                                       inputs["L0_ffn_expert_gate_up"],
                                       inputs["L0_ffn_expert_down"])
    shared2, per_core2 = prep_phase2_inputs(
        inputs["L1_attn_Wq"], inputs["L1_attn_bq"], inputs["L1_attn_Wk"],
        inputs["L1_attn_bk"], inputs["L1_attn_Wv"], inputs["L1_attn_bv"],
        inputs["L1_attn_Wo"], inputs["L1_attn_bo"], inputs["L1_attn_rand_matrix"])
    w3 = prep_phase3_weights(inputs)
    maps = []
    for c in range(8):
        m = dict(maps0[c])
        m.update(dict(l0_rgT=rgT, l0_wgu=wgu, l0_wd=wd))
        m.update(shared2)
        m.update(per_core2[c])
        m.update(w3)
        maps.append(m)
    return maps


def kernel_fused(**inputs):
    nc = build_full_program()
    maps = prep_full_maps(inputs)
    res = run_bass_kernel_spmd(nc, maps, core_ids=list(range(8)))
    out = np.zeros((4, 4096, D), np.float32)
    for c in range(8):
        b, h = c // 2, c % 2
        out[b, h * TOK:(h + 1) * TOK] = np.asarray(res.results[c]["x4"])
    return out


def build_A_program():
    nc = bass.Bass("TRN2", target_bir_lowering=False)
    io = {}
    io.update(declare_io_phase0(nc))
    io["x1"] = nc.dram_tensor("x1", [TOK, D], F32, kind="Internal").ap()
    io.update(declare_io_phase1(nc))
    io["x2"] = nc.dram_tensor("x2", [TOK, D], F32, kind="ExternalOutput").ap()
    with ExitStack() as ctx:
        r = Rec(nc, ctx)
        with ExitStack() as c0:
            Phase0(r, c0, nc, io).emit()
            r.barrier()
        with ExitStack() as c1:
            Phase1(r, c1, nc, io).emit()
        r.wait("sync", "st", r.cnt["st"])
        blk = ctx.enter_context(nc.Block())
        r.replay(blk)
    return nc


def build_B_program():
    nc = bass.Bass("TRN2", target_bir_lowering=False)
    io = {}
    io.update(declare_io_phase2(nc))
    io["x2f"] = nc.dram_tensor("x2f", [2 * TOK, D], F32, kind="ExternalInput").ap()
    io["x2own"] = nc.dram_tensor("x2own", [TOK, D], F32, kind="ExternalInput").ap()
    io["x3"] = nc.dram_tensor("x3", [TOK, D], F32, kind="Internal").ap()
    io.update(declare_io_phase3(nc))
    io["x4"] = nc.dram_tensor("x4", [TOK, D], F32, kind="ExternalOutput").ap()
    with ExitStack() as ctx:
        r = Rec(nc, ctx)
        with ExitStack() as c2:
            Phase2(r, c2, nc, io).emit()
            r.barrier()
        with ExitStack() as c3:
            Phase3(r, c3, nc, io).emit()
        r.wait("sync", "st", r.cnt["st"])
        blk = ctx.enter_context(nc.Block())
        r.replay(blk)
    return nc


def kernel(**inputs):
    n = 8
    ncA = build_A_program()
    mapsA = prep_phase0_inputs(inputs["L0_attn_x"], inputs["L0_attn_wq"],
                               inputs["L0_attn_wk"], inputs["L0_attn_wv"],
                               inputs["L0_attn_wo"])
    rgT, wgu, wd = prep_phase1_weights(inputs["L0_ffn_router_gate"],
                                       inputs["L0_ffn_expert_gate_up"],
                                       inputs["L0_ffn_expert_down"])
    for m in mapsA:
        m.update(dict(l0_rgT=rgT, l0_wgu=wgu, l0_wd=wd))
    resA = run_bass_kernel_spmd(ncA, mapsA, core_ids=list(range(n)))
    x2 = [np.asarray(resA.results[c]["x2"]) for c in range(n)]

    ncB = build_B_program()
    shared2, per_core2 = prep_phase2_inputs(
        inputs["L1_attn_Wq"], inputs["L1_attn_bq"], inputs["L1_attn_Wk"],
        inputs["L1_attn_bk"], inputs["L1_attn_Wv"], inputs["L1_attn_bv"],
        inputs["L1_attn_Wo"], inputs["L1_attn_bo"], inputs["L1_attn_rand_matrix"])
    w3 = prep_phase3_weights(inputs)
    mapsB = []
    for c in range(n):
        b, h = c // 2, c % 2
        m = dict(shared2)
        m.update(per_core2[c])
        m.update(w3)
        m["x2f"] = np.ascontiguousarray(np.concatenate([x2[2 * b], x2[2 * b + 1]], axis=0))
        m["x2own"] = x2[c]
        mapsB.append(m)
    resB = run_bass_kernel_spmd(ncB, mapsB, core_ids=list(range(n)))
    out = np.zeros((4, 4096, D), np.float32)
    for c in range(n):
        b, h = c // 2, c % 2
        out[b, h * TOK:(h + 1) * TOK] = np.asarray(resB.results[c]["x4"])
    return out


# version 04
# speedup vs baseline: 1.0256x; 1.0256x over previous
import numpy as np
import ml_dtypes
from contextlib import ExitStack
import concourse.bass as bass
import concourse.mybir as mybir
from concourse.bass_utils import run_bass_kernel_spmd

F32 = mybir.dt.float32
BF16 = mybir.dt.bfloat16
AF = mybir.ActivationFunctionType
ALU = mybir.AluOpType
AX = mybir.AxisListType

D = 2048
TOK = 2048
HALO = 1024
TOT = TOK + HALO
NKT = 16
DH = 128

ENGINES = ("tensor", "vector", "scalar", "gpsimd", "sync")
DMA_SEMS = ("dg", "ds", "st", "d0", "d1")


class Rec:

    def __init__(self, nc, ctx):
        self.nc = nc
        self.ops = {e: [] for e in ENGINES}
        self.sems = {}
        self.cnt = {}
        for name in ("pe", "ve", "se", "ge", "sy") + DMA_SEMS:
            self.sems[name] = ctx.enter_context(nc.semaphore(name))
            self.cnt[name] = 0
        self.eng_sem = {"tensor": "pe", "vector": "ve", "scalar": "se",
                        "gpsimd": "ge", "sync": "sy"}

    def wait(self, eng, sem, val):
        if val and val > 0:
            self.ops[eng].append(["w", sem, val])

    def op(self, eng, fn, inc=None, inc_by=1):
        tgt = None
        if inc is not None:
            self.cnt[inc] += inc_by
            tgt = self.cnt[inc]
        self.ops[eng].append(["o", fn, inc, inc_by])
        return tgt

    def cop(self, eng, fn):
        sem = self.eng_sem[eng]
        self.wait(eng, sem, self.cnt[sem])
        return self.op(eng, fn, inc=sem)

    def dma(self, eng, out, in_, sem, waits=()):
        for s, v in waits:
            self.wait(eng, s, v)
        return self.op(eng, (lambda e, o=out, i=in_: e.dma_start(o, i)), inc=sem, inc_by=16)

    def _seal(self, eng):
        sem = self.eng_sem[eng]
        for it in reversed(self.ops[eng]):
            if it[0] == "o":
                if it[2] is None:
                    it[2] = sem
                    it[3] = 1
                    self.cnt[sem] += 1
                elif it[2] != sem:
                    self.cnt[sem] += 1
                    self.ops[eng].append(["o", (lambda e: e.nop()), sem, 1])
                break
        return self.cnt[sem]

    def barrier(self, engines=ENGINES):
        targets = {}
        for eng in engines:
            targets[self.eng_sem[eng]] = self._seal(eng)
        for s in DMA_SEMS:
            targets[s] = self.cnt[s]
        for eng in engines:
            own = self.eng_sem[eng]
            for s, v in targets.items():
                if s != own and v > 0:
                    self.wait(eng, s, v)

    def replay(self, block):
        sems = self.sems
        for eng in ENGINES:
            ops = self.ops[eng]

            def mk(ops=ops):
                def run(e):
                    for it in ops:
                        if it[0] == "w":
                            e.wait_ge(sems[it[1]], it[2])
                        else:
                            ins = it[1](e)
                            if it[2] is not None:
                                ins.then_inc(sems[it[2]], it[3])
                return run
            getattr(block, eng)(mk())


class Phase0:

    def __init__(self, r, ctx, nc, io):
        self.r, self.nc, self.io = r, nc, io
        sb = self.sb = {}

        def S(name, shape, dt):
            sb[name] = ctx.enter_context(nc.sbuf_tensor("p0_" + name, shape, dt))

        S("wk", [128, NKT * 512], BF16)
        S("wv", [128, NKT * 512], BF16)
        S("cos", [128, 24 * 64], F32)
        S("sin", [128, 24 * 64], F32)
        S("x0", [128, 2 * D], F32)
        S("sq", [128, D], F32)
        S("stat", [128, 32], F32)
        S("xln", [128, 2 * D], BF16)
        S("xlnT", [128, NKT * 512], BF16)
        S("kr", [128, 512], BF16)
        S("qr", [128, 512], BF16)
        S("ta", [128, 256], F32)
        S("tb", [128, 256], F32)
        S("ringK", [128, 3 * 4 * 512], BF16)
        S("ringV", [128, 3 * 4 * 4 * 128], BF16)
        S("valid", [128, 6], BF16)
        S("qT", [128, 16 * 512], BF16)
        S("attT", [128, 12 * 512], BF16)
        S("yT", [128, 16 * 512], BF16)
        S("ws", [128, 2 * 512], BF16)
        S("tri", [128, 128], BF16)
        S("ident", [128, 128], BF16)
        S("ones1f", [1, 128], F32)
        S("sums", [1, 512], F32)
        S("recip", [1, 512], F32)
        S("av", [128, 512], BF16)
        S("x0r", [128, 4 * 512], F32)
        S("x1p", [128, 2 * 512], F32)
        S("ceps5", [128, 1], F32)
        S("cepsr", [128, 1], F32)
        S("clnsc", [128, 1], F32)

        self.pbig = ctx.enter_context(nc.psum_tensor("p0_pbig", [128, 2048], F32))
        self.pA = ctx.enter_context(nc.psum_tensor("p0_pA", [128, 512], F32))
        self.pB = ctx.enter_context(nc.psum_tensor("p0_pB", [128, 512], F32))
        self.pt = [ctx.enter_context(nc.psum_tensor("p0_pt0", [128, 128], BF16)),
                   ctx.enter_context(nc.psum_tensor("p0_pt1", [128, 128], BF16))]
        self.pp = {"idx": 0, "last": [None, None]}

        self.vwk = sb["wk"][:].rearrange("p (k n) -> p k n", k=NKT)
        self.vwv = sb["wv"][:].rearrange("p (k n) -> p k n", k=NKT)
        self.vcos = sb["cos"][:].rearrange("p (t f) -> p t f", t=24)
        self.vsin = sb["sin"][:].rearrange("p (t f) -> p t f", t=24)
        self.vx0 = sb["x0"][:].rearrange("p (b n) -> p b n", b=2)
        self.vxln = sb["xln"][:].rearrange("p (b n) -> p b n", b=2)
        self.vxlnT = sb["xlnT"][:].rearrange("p (k n) -> p k n", k=NKT)
        self.vringK = sb["ringK"][:].rearrange("p (s h n) -> p s h n", s=3, h=4)
        self.vringV = sb["ringV"][:].rearrange("p (s t h d) -> p s t h d", s=3, t=4, h=4)
        self.vqT = sb["qT"][:].rearrange("p (h n) -> p h n", h=16)
        self.vattT = sb["attT"][:].rearrange("p (j n) -> p j n", j=12)
        self.vyT = sb["yT"][:].rearrange("p (h n) -> p h n", h=16)
        self.vws = sb["ws"][:].rearrange("p (b n) -> p b n", b=2)
        self.vx0r = sb["x0r"][:].rearrange("p (t n) -> p t n", t=4)
        self.vx1p = sb["x1p"][:].rearrange("p (b n) -> p b n", b=2)

    def transpose(self, src_ap, dst_ap):
        r, pp = self.r, self.pp
        i = pp["idx"] % 2
        pt = self.pt[i]
        if pp["last"][i] is not None:
            s_, v_ = pp["last"][i]
            r.wait("tensor", s_, v_)
        tp = r.op("tensor", lambda e, o=pt, a=src_ap, idn=self.sb["ident"]:
                  e.transpose(o[:], a, idn[:]), inc="pe")
        if i == 0:
            r.wait("vector", "pe", tp)
            te = r.op("vector", lambda e, o=dst_ap, a=pt: e.tensor_copy(o, a[:]), inc="ve")
            pp["last"][i] = ("ve", te)
        else:
            r.wait("scalar", "pe", tp)
            te = r.op("scalar", lambda e, o=dst_ap, a=pt: e.copy(o, a[:]), inc="se")
            pp["last"][i] = ("se", te)
        pp["idx"] += 1
        return pp["last"][i]

    def ln_tile(self, x_src, dst_bf16):
        r, sb = self.r, self.sb
        st = sb["stat"]
        c0, c1, c2, c3, c4, c5, c6 = (st[:, i:i + 1] for i in range(7))
        r.cop("vector", lambda e, a=x_src, o=c0: e.reduce_sum(o, a, axis=AX.X))
        t1 = r.cop("scalar", lambda e, a=x_src, o=sb["sq"], acc=c1:
                   e.activation(o[:], a, AF.Square, accum_out=acc))
        r.wait("vector", "se", t1)
        r.cop("vector", lambda e, o=c2, a=c0: e.tensor_scalar(o, a, 1.0 / D, None, ALU.mult))
        r.cop("vector", lambda e, o=c3, a=c2: e.tensor_scalar(o, a, a, None, ALU.mult))
        tv = r.cop("vector", lambda e, o=c4, a=c1, b=c3:
                   e.scalar_tensor_tensor(o, a, 1.0 / D, b, ALU.mult, ALU.subtract))
        r.wait("scalar", "ve", tv)
        r.cop("scalar", lambda e, o=c5, a=c4, b=sb["ceps5"]: e.activation(o, a, AF.Ln, bias=b[:]))
        ts_ = r.cop("scalar", lambda e, o=c5, a=c5: e.activation(o, a, AF.Exp, scale=-0.5))
        r.wait("vector", "se", ts_)
        r.cop("vector", lambda e, o=c6, a=c2, b=c5:
              e.scalar_tensor_tensor(o, a, -1.0, b, ALU.mult, ALU.mult))
        return r.cop("vector", lambda e, o=dst_bf16, a=x_src, s1=c5, s2=c6:
                     e.tensor_scalar(o, a, s1, s2, ALU.mult, ALU.add))

    def rms4(self, src_ps3, stat0, fold_scale):
        r, sb = self.r, self.sb
        sq512 = sb["sq"][:, 0:512]
        sqv = sq512.rearrange("p (h f) -> p h f", h=4)
        msq = sb["stat"][:, stat0:stat0 + 4]
        t1 = r.cop("scalar", lambda e, o=sq512, a=src_ps3: e.activation(o, a, AF.Square))
        r.wait("vector", "se", t1)
        tv = r.cop("vector", lambda e, o=msq, a=sqv: e.reduce_sum(o, a, axis=AX.X))
        r.wait("scalar", "ve", tv)
        r.cop("scalar", lambda e, o=msq, a=msq, b=sb["cepsr"]:
              e.activation(o, a, AF.Ln, scale=1.0 / 128.0, bias=b[:]))
        if fold_scale:
            return r.cop("scalar", lambda e, o=msq, a=msq, b=sb["clnsc"]:
                         e.activation(o, a, AF.Exp, scale=-0.5, bias=b[:]))
        return r.cop("scalar", lambda e, o=msq, a=msq: e.activation(o, a, AF.Exp, scale=-0.5))

    def rope4(self, src3, dst3, tile):
        r, sb = self.r, self.sb
        cosb = self.vcos[:, tile, :].unsqueeze(1).to_broadcast([128, 4, 64])
        sinb = self.vsin[:, tile, :].unsqueeze(1).to_broadcast([128, 4, 64])
        e_ = src3.rearrange("p h (f two) -> p h f two", two=2)[:, :, :, 0]
        o_ = src3.rearrange("p h (f two) -> p h f two", two=2)[:, :, :, 1]
        ta = sb["ta"][:].rearrange("p (h f) -> p h f", h=4)[:, :, 0:64]
        tb = sb["tb"][:].rearrange("p (h f) -> p h f", h=4)[:, :, 0:64]
        lo = dst3[:, :, 0:64]
        hi = dst3[:, :, 64:128]
        r.cop("vector", lambda e, o=ta, a=e_, b=cosb: e.tensor_tensor(o, a, b, ALU.mult))
        r.cop("vector", lambda e, o=tb, a=o_, b=sinb: e.tensor_tensor(o, a, b, ALU.mult))
        r.cop("vector", lambda e, o=lo, a=ta, b=tb: e.tensor_tensor(o, a, b, ALU.subtract))
        r.cop("vector", lambda e, o=ta, a=e_, b=sinb: e.tensor_tensor(o, a, b, ALU.mult))
        r.cop("vector", lambda e, o=tb, a=o_, b=cosb: e.tensor_tensor(o, a, b, ALU.mult))
        return r.cop("vector", lambda e, o=hi, a=ta, b=tb: e.tensor_tensor(o, a, b, ALU.add))

    def emit(self, stop=None, dump_it=2):
        r, sb, io = self.r, self.sb, self.io

        def dump(*tensors):
            r.barrier()
            for i, t in enumerate(tensors):
                p, w = t.shape[0], int(np.prod(t.shape[1:]))
                if t.dtype == BF16:
                    r.dma("sync", io["dbg%d" % i][0:p, 0:w], t, "st")
                else:
                    r.dma("gpsimd", io["dbg%d" % i][0:p, 0:w], t, "dg")
            r.wait("sync", "st", r.cnt["st"])
            r.wait("gpsimd", "dg", r.cnt["dg"])
            return True
        pbig, pA, pB = self.pbig, self.pA, self.pB

        tg = r.op("gpsimd", lambda e: e.memset(sb["ident"][:], 1.0), inc="ge")
        r.wait("gpsimd", "ge", tg)
        tg = r.op("gpsimd", lambda e: e.affine_select(sb["ident"][:], sb["ident"][:],
             pattern=[[1, 128]], compare_op=ALU.is_equal, fill=0.0, base=0, channel_multiplier=-1), inc="ge")
        r.wait("gpsimd", "ge", tg)
        tg = r.op("gpsimd", lambda e: e.memset(sb["tri"][:], 1.0), inc="ge")
        r.wait("gpsimd", "ge", tg)
        r.op("gpsimd", lambda e: e.affine_select(sb["tri"][:], sb["tri"][:],
             pattern=[[1, 128]], compare_op=ALU.is_ge, fill=0.0, base=0, channel_multiplier=-1), inc="ge")
        r.op("vector", lambda e: e.memset(sb["ones1f"][:], 1.0))
        r.op("vector", lambda e: e.memset(sb["attT"][:], 0.0))
        r.op("vector", lambda e: e.memset(sb["ceps5"][:], 1e-5))
        r.op("vector", lambda e: e.memset(sb["cepsr"][:], 1e-6))
        r.op("vector", lambda e: e.memset(sb["clnsc"][:], float(np.log(1.0 / np.sqrt(128.0)))))
        r.dma("gpsimd", sb["wk"][:].rearrange("p (k n) -> p k n", k=NKT),
              io["l0_wkT"].rearrange("(k p) n -> p k n", p=128), "dg")
        r.dma("gpsimd", sb["wv"][:].rearrange("p (k n) -> p k n", k=NKT),
              io["l0_wvT"].rearrange("(k p) n -> p k n", p=128), "dg")
        r.dma("gpsimd", sb["valid"][:], io["valid"], "dg")
        r.dma("sync", sb["cos"][:].rearrange("p (t f) -> p t f", t=24),
              io["cosT"].rearrange("(t p) f -> p t f", p=128), "ds")
        r.dma("sync", sb["sin"][:].rearrange("p (t f) -> p t f", t=24),
              io["sinT"].rearrange("(t p) f -> p t f", p=128), "ds")
        r.barrier()

        for it in range(6):
            for ts in range(4):
                tile = it * 4 + ts
                buf = tile % 2
                ld = r.dma("sync", self.vx0[:, buf, :],
                           io["x0c"][tile * 128:(tile + 1) * 128, :], "ds")
                r.wait("vector", "ds", ld)
                r.wait("scalar", "ds", ld)
                tn = self.ln_tile(self.vx0[:, buf, :], self.vxln[:, buf, :])
                r.wait("tensor", "ve", tn)
                for kt in range(NKT):
                    self.transpose(self.vxln[:, buf, kt * 128:(kt + 1) * 128],
                                   self.vxlnT[:, kt, ts * 128:(ts + 1) * 128])
                r.barrier()
            if stop == "ln" and it == dump_it:
                return dump(sb["xlnT"][:], sb["xln"][:])
            if stop == "stat" and it == dump_it:
                return dump(sb["stat"][:], sb["sq"][:])
            if stop == "ident" and it == dump_it:
                return dump(sb["ident"][:], sb["tri"][:])
            slot = it % 3
            for ts in range(4):
                tile = it * 4 + ts
                for kt in range(NKT):
                    r.op("tensor", lambda e, k=kt, t=ts: e.matmul(
                        pA[:], self.vxlnT[:, k, t * 128:(t + 1) * 128], self.vwk[:, k, :],
                        start=(k == 0), stop=(k == NKT - 1), skip_group_check=True))
                for kt in range(NKT):
                    r.op("tensor", lambda e, k=kt, t=ts: e.matmul(
                        pB[:], self.vxlnT[:, k, t * 128:(t + 1) * 128], self.vwv[:, k, :],
                        start=(k == 0), stop=(k == NKT - 1), skip_group_check=True),
                        inc=("pe" if kt == NKT - 1 else None))
                kv_pe = r.cnt["pe"]
                r.wait("vector", "pe", kv_pe)
                r.op("vector", lambda e, s=slot, t=ts: e.tensor_copy(
                    self.vringV[:, s, t, :, :], pB[:].rearrange("p (h d) -> p h d", h=4)))
                r.wait("scalar", "pe", kv_pe)
                trs = self.rms4(pA[:].rearrange("p (h d) -> p h d", h=4), 16, False)
                r.wait("vector", "se", trs)
                khat = sb["sq"][:, 0:512].rearrange("p (h d) -> p h d", h=4)
                r.cop("vector", lambda e, o=khat, a=pA[:].rearrange("p (h d) -> p h d", h=4),
                      b=sb["stat"][:, 16:20].unsqueeze(2).to_broadcast([128, 4, 128]):
                      e.tensor_tensor(o, a, b, ALU.mult))
                tr = self.rope4(khat, sb["kr"][:].rearrange("p (h d) -> p h d", h=4), tile)
                r.wait("tensor", "ve", tr)
                for hh in range(4):
                    self.transpose(sb["kr"][:, hh * 128:(hh + 1) * 128],
                                   self.vringK[:, slot, hh, ts * 128:(ts + 1) * 128])
                r.barrier()
            if stop == "kv" and it == dump_it:
                return dump(sb["ringK"][:], sb["ringV"][:])

            if it < 2:
                continue
            cc = it - 2

            ws_use = [0, 0]
            for colq in range(4):
                for kt in range(NKT):
                    wbuf = kt % 2
                    wsem = "d0" if wbuf == 0 else "d1"
                    ld = r.dma("gpsimd", self.vws[:, wbuf, :],
                               io["l0_wqT"][kt * 128:(kt + 1) * 128, colq * 512:(colq + 1) * 512],
                               wsem, waits=[("pe", ws_use[wbuf])])
                    r.wait("tensor", wsem, ld)
                    for ts in range(4):
                        r.op("tensor", lambda e, k=kt, t=ts, b=wbuf: e.matmul(
                            pbig[:, t * 512:(t + 1) * 512],
                            self.vxlnT[:, k, t * 128:(t + 1) * 128], self.vws[:, b, :],
                            start=(k == 0), stop=(k == NKT - 1), skip_group_check=True),
                            inc=("pe" if ts == 3 else None))
                    ws_use[wbuf] = r.cnt["pe"]
                qpe = r.cnt["pe"]
                r.wait("scalar", "pe", qpe)
                r.wait("vector", "pe", qpe)
                for ts in range(4):
                    tile = it * 4 + ts
                    src = pbig[:, ts * 512:(ts + 1) * 512].rearrange("p (h d) -> p h d", h=4)
                    trs = self.rms4(src, 20, True)
                    r.wait("vector", "se", trs)
                    qhat = sb["sq"][:, 0:512].rearrange("p (h d) -> p h d", h=4)
                    r.cop("vector", lambda e, o=qhat, a=src,
                          b=sb["stat"][:, 20:24].unsqueeze(2).to_broadcast([128, 4, 128]):
                          e.tensor_tensor(o, a, b, ALU.mult))
                    tr = self.rope4(qhat, sb["qr"][:].rearrange("p (h d) -> p h d", h=4), tile)
                    r.wait("tensor", "ve", tr)
                    for hl in range(4):
                        self.transpose(sb["qr"][:, hl * 128:(hl + 1) * 128],
                                       self.vqT[:, colq * 4 + hl, ts * 128:(ts + 1) * 128])
                    r.barrier(("tensor", "vector", "scalar"))
                r.barrier()
            if stop == "q" and it == dump_it:
                return dump(sb["qT"][:])

            sc_evac = [0, 0]
            prev_norm = 0
            for h in range(16):
                kvh = h // 4
                for j in range(12):
                    si = it - 2 + j // 4
                    sl = si % 3
                    kk = j % 4
                    qoff = (j - 8) * 128 if j >= 8 else 0
                    pbank = pA if j % 2 == 0 else pB
                    r.wait("tensor", "se", sc_evac[j % 2])
                    mm = r.op("tensor", lambda e, p=pbank, s=sl, k=kvh, K=kk, q=qoff, hh=h:
                              e.matmul(p[:, q:512], self.vringK[:, s, k, K * 128:(K + 1) * 128],
                                       self.vqT[:, hh, q:512], start=True, stop=True,
                                       skip_group_check=True), inc="pe")
                    r.wait("scalar", "pe", mm)
                    te = r.op("scalar", lambda e, p=pbank, q=qoff, J=j: e.activation(
                        self.vattT[:, J, q:512], p[:, q:512], AF.Exp), inc="se")
                    sc_evac[j % 2] = te
                    fin = ("se", te)
                    if j == 0:
                        r.wait("vector", "se", te)
                        tv = r.op("vector", lambda e: e.memset(self.vattT[0:1, 0, :], 0.0), inc="ve")
                        fin = ("ve", tv)
                    elif j >= 8:
                        r.wait("vector", "se", te)
                        tv = r.op("vector", lambda e, J=j, q=qoff: e.tensor_tensor(
                            self.vattT[:, J, q:q + 128], self.vattT[:, J, q:q + 128],
                            sb["tri"][:], ALU.mult), inc="ve")
                        fin = ("ve", tv)
                    r.wait("tensor", fin[0], fin[1])
                    r.op("tensor", lambda e, J=j, si_=si: e.matmul(
                        pbig[0:1, 512:1024], sb["valid"][:, si_:si_ + 1], self.vattT[:, J, :],
                        start=(J == 0), stop=(J == 11), skip_group_check=True))
                    r.op("tensor", lambda e, J=j, s=sl, K=kk, k=kvh: e.matmul(
                        pbig[:, 0:512], self.vringV[:, s, K, k, :], self.vattT[:, J, :],
                        start=(J == 0), stop=(J == 11), skip_group_check=True),
                        inc=("pe" if j == 11 else None))
                av_pe = r.cnt["pe"]
                r.wait("vector", "pe", av_pe)
                r.cop("vector", lambda e: e.tensor_copy(sb["sums"][:], pbig[0:1, 512:1024]))
                tv = r.cop("vector", lambda e: e.reciprocal(sb["recip"][:], sb["sums"][:]))
                r.wait("tensor", "ve", tv)
                tb_ = r.op("tensor", lambda e: e.matmul(
                    pbig[:, 1024:1536], sb["ones1f"][:], sb["recip"][:],
                    start=True, stop=True, skip_group_check=True), inc="pe")
                r.wait("vector", "pe", tb_)
                r.cop("vector", lambda e: e.tensor_copy(sb["av"][:], pbig[:, 0:512]))
                tyh = r.cop("vector", lambda e, hh=h: e.tensor_tensor(
                    self.vyT[:, hh, :], sb["av"][:], pbig[:, 1024:1536], ALU.mult))
                prev_norm = tyh
            r.barrier()
            if stop == "att" and it == dump_it:
                return dump(sb["yT"][:], sb["attT"][:])

            ws_use = [0, 0]
            x1p_store = [0, 0]
            prev_evac = 0
            for colq in range(4):
                for ts in range(4):
                    tile = it * 4 + ts
                    r.dma("sync", self.vx0r[:, ts, :],
                          io["x0c"][tile * 128:(tile + 1) * 128, colq * 512:(colq + 1) * 512],
                          "ds", waits=[("ve", prev_evac)])
                ld_x0r = r.cnt["ds"]
                r.wait("vector", "ds", ld_x0r)
                for kt in range(NKT):
                    wbuf = kt % 2
                    wsem = "d0" if wbuf == 0 else "d1"
                    ld = r.dma("gpsimd", self.vws[:, wbuf, :],
                               io["l0_woT"][kt * 128:(kt + 1) * 128, colq * 512:(colq + 1) * 512],
                               wsem, waits=[("pe", ws_use[wbuf])])
                    r.wait("tensor", wsem, ld)
                    for ts in range(4):
                        r.op("tensor", lambda e, k=kt, t=ts, b=wbuf: e.matmul(
                            pbig[:, t * 512:(t + 1) * 512],
                            self.vyT[:, k, t * 128:(t + 1) * 128], self.vws[:, b, :],
                            start=(k == 0), stop=(k == NKT - 1), skip_group_check=True),
                            inc=("pe" if ts == 3 else None))
                    ws_use[wbuf] = r.cnt["pe"]
                ope = r.cnt["pe"]
                r.wait("vector", "pe", ope)
                for ts in range(4):
                    buf = ts % 2
                    r.wait("vector", "st", max(x1p_store))
                    tv = r.op("vector", lambda e, t=ts, b=buf: e.tensor_tensor(
                        self.vx1p[:, b, :], pbig[:, t * 512:(t + 1) * 512],
                        self.vx0r[:, t, :], ALU.add), inc="ve")
                    st = r.dma("sync",
                               io["x1"][(cc * 4 + ts) * 128:(cc * 4 + ts + 1) * 128,
                                        colq * 512:(colq + 1) * 512],
                               self.vx1p[:, buf, :], "st", waits=[("ve", tv)])
                    x1p_store[buf] = st
                prev_evac = r.cnt["ve"]
                r.barrier(("tensor", "vector"))
            r.barrier()
        return self


class Helpers:

    def __init__(self, r, ctx, nc, tag):
        self.r, self.nc = r, nc
        sb = self.sb = {}

        def S(name, shape, dt):
            sb[name] = ctx.enter_context(nc.sbuf_tensor(tag + name, shape, dt))
        S("sq", [128, D], F32)
        S("stat", [128, 32], F32)
        S("ident", [128, 128], BF16)
        S("ceps5", [128, 1], F32)
        self.pt = [ctx.enter_context(nc.psum_tensor(tag + "pt0", [128, 128], BF16)),
                   ctx.enter_context(nc.psum_tensor(tag + "pt1", [128, 128], BF16))]
        self.pp = {"idx": 0, "last": [None, None]}
        tg = r.op("gpsimd", lambda e: e.memset(sb["ident"][:], 1.0), inc="ge")
        r.wait("gpsimd", "ge", tg)
        r.op("gpsimd", lambda e: e.affine_select(sb["ident"][:], sb["ident"][:],
             pattern=[[1, 128]], compare_op=ALU.is_equal, fill=0.0, base=0,
             channel_multiplier=-1), inc="ge")
        r.op("vector", lambda e: e.memset(sb["ceps5"][:], 1e-5), inc="ve")

    def transpose(self, src_ap, dst_ap):
        r, pp = self.r, self.pp
        i = pp["idx"] % 2
        pt = self.pt[i]
        if pp["last"][i] is not None:
            s_, v_ = pp["last"][i]
            r.wait("tensor", s_, v_)
        tp = r.op("tensor", lambda e, o=pt, a=src_ap, idn=self.sb["ident"]:
                  e.transpose(o[:], a, idn[:]), inc="pe")
        if i == 0:
            r.wait("vector", "pe", tp)
            te = r.op("vector", lambda e, o=dst_ap, a=pt: e.tensor_copy(o, a[:]), inc="ve")
            pp["last"][i] = ("ve", te)
        else:
            r.wait("scalar", "pe", tp)
            te = r.op("scalar", lambda e, o=dst_ap, a=pt: e.copy(o, a[:]), inc="se")
            pp["last"][i] = ("se", te)
        pp["idx"] += 1
        return pp["last"][i]

    def ln_tile(self, x_src, dst_bf16):
        r, sb = self.r, self.sb
        st = sb["stat"]
        c0, c1, c2, c3, c4, c5, c6 = (st[:, i:i + 1] for i in range(7))
        r.cop("vector", lambda e, a=x_src, o=c0: e.reduce_sum(o, a, axis=AX.X))
        t1 = r.cop("scalar", lambda e, a=x_src, o=sb["sq"], acc=c1:
                   e.activation(o[:], a, AF.Square, accum_out=acc))
        r.wait("vector", "se", t1)
        r.cop("vector", lambda e, o=c2, a=c0: e.tensor_scalar(o, a, 1.0 / D, None, ALU.mult))
        r.cop("vector", lambda e, o=c3, a=c2: e.tensor_scalar(o, a, a, None, ALU.mult))
        tv = r.cop("vector", lambda e, o=c4, a=c1, b=c3:
                   e.scalar_tensor_tensor(o, a, 1.0 / D, b, ALU.mult, ALU.subtract))
        r.wait("scalar", "ve", tv)
        r.cop("scalar", lambda e, o=c5, a=c4, b=sb["ceps5"]: e.activation(o, a, AF.Ln, bias=b[:]))
        ts_ = r.cop("scalar", lambda e, o=c5, a=c5: e.activation(o, a, AF.Exp, scale=-0.5))
        r.wait("vector", "se", ts_)
        r.cop("vector", lambda e, o=c6, a=c2, b=c5:
              e.scalar_tensor_tensor(o, a, -1.0, b, ALU.mult, ALU.mult))
        return r.cop("vector", lambda e, o=dst_bf16, a=x_src, s1=c5, s2=c6:
                     e.tensor_scalar(o, a, s1, s2, ALU.mult, ALU.add))


class Phase1:

    def __init__(self, r, ctx, nc, io):
        self.r, self.nc, self.io = r, nc, io
        self.h = Helpers(r, ctx, nc, "p1_")
        sb = self.sb = self.h.sb

        def S(name, shape, dt):
            sb[name] = ctx.enter_context(nc.sbuf_tensor("p1_" + name, shape, dt))
        S("x1", [128, 2 * D], F32)
        S("xln", [128, 2 * D], BF16)
        S("xlt", [128, 2 * D], BF16)
        S("rg", [128, NKT * 8], BF16)
        S("we", [128, 16 * 8], F32)
        S("e8", [128, 8], F32)
        S("r8", [128, 8], F32)
        S("wgu", [128, NKT * 2048], BF16)
        S("wd", [128, 2 * 8 * 2048], BF16)
        S("sg", [128, 1024], BF16)
        S("hh", [128, 1024], BF16)
        S("hT", [128, 8 * 128], BF16)
        S("ya", [128, 3 * D], F32)
        S("x1r", [128, D], F32)
        self.pbig = ctx.enter_context(nc.psum_tensor("p1_pbig", [128, 2048], F32))
        self.pA = ctx.enter_context(nc.psum_tensor("p1_pA", [128, 512], F32))
        self.pB = ctx.enter_context(nc.psum_tensor("p1_pB", [128, 512], F32))
        self.vx1 = sb["x1"][:].rearrange("p (b n) -> p b n", b=2)
        self.vxln = sb["xln"][:].rearrange("p (b n) -> p b n", b=2)
        self.vxlt = sb["xlt"][:].rearrange("p (b n) -> p b n", b=2)
        self.vrg = sb["rg"][:].rearrange("p (k n) -> p k n", k=NKT)
        self.vwe = sb["we"][:].rearrange("p (t n) -> p t n", t=16)
        self.vwgu = sb["wgu"][:].rearrange("p (k n) -> p k n", k=NKT)
        self.vwd = sb["wd"][:].rearrange("p (b k n) -> p b k n", b=2, k=8)
        self.vhT = sb["hT"][:].rearrange("p (k n) -> p k n", k=8)
        self.vya = sb["ya"][:].rearrange("p (b n) -> p b n", b=3)

    def emit(self, stop=None):
        r, sb, io = self.r, self.sb, self.io
        h = self.h
        pbig, pA, pB = self.pbig, self.pA, self.pB

        r.dma("gpsimd", sb["rg"][:].rearrange("p (k n) -> p k n", k=NKT),
              io["l0_rgT"].rearrange("(k p) n -> p k n", p=128), "dg")
        r.barrier()

        for t in range(16):
            buf = t % 2
            ld = r.dma("sync", self.vx1[:, buf, :], io["x1"][t * 128:(t + 1) * 128, :], "ds")
            r.wait("vector", "ds", ld)
            r.wait("scalar", "ds", ld)
            tn = h.ln_tile(self.vx1[:, buf, :], self.vxln[:, buf, :])
            r.wait("tensor", "ve", tn)
            for kt in range(NKT):
                h.transpose(self.vxln[:, buf, kt * 128:(kt + 1) * 128],
                            self.vxlt[:, 0, kt * 128:(kt + 1) * 128])
            xt_ve, xt_se = r.cnt["ve"], r.cnt["se"]
            r.dma("sync", io["xlnT2"][t * 128:(t + 1) * 128, :], sb["xlt"][:, 0:2048],
                  "st", waits=[("ve", xt_ve), ("se", xt_se)])
            r.wait("tensor", "ve", xt_ve)
            r.wait("tensor", "se", xt_se)
            for kt in range(NKT):
                r.op("tensor", lambda e, k=kt: e.matmul(
                    pA[:, 0:8], self.vxlt[:, 0, k * 128:(k + 1) * 128], self.vrg[:, k, :],
                    start=(k == 0), stop=(k == NKT - 1), skip_group_check=True),
                    inc=("pe" if kt == NKT - 1 else None))
            lg = r.cnt["pe"]
            st = sb["stat"]
            mx, m1, m2, s12, rcp = (st[:, 8 + i:9 + i] for i in range(5))
            e8, r8 = sb["e8"], sb["r8"]
            r.wait("vector", "pe", lg)
            tv = r.op("vector", lambda e: e.reduce_max(mx, pA[:, 0:8], axis=AX.X), inc="ve")
            r.wait("vector", "ve", tv)
            tv = r.op("vector", lambda e: e.tensor_scalar(m1, mx, -1.0, None, ALU.mult), inc="ve")
            r.wait("scalar", "ve", tv)
            te = r.op("scalar", lambda e: e.activation(e8[:], pA[:, 0:8], AF.Exp, bias=m1), inc="se")
            r.wait("vector", "se", te)
            tv = r.op("vector", lambda e: e.reduce_max(m1, e8[:], axis=AX.X), inc="ve")
            r.wait("vector", "ve", tv)
            tv = r.op("vector", lambda e: e.tensor_scalar(r8[:], e8[:], m1, None, ALU.is_ge), inc="ve")
            r.wait("vector", "ve", tv)
            tv = r.op("vector", lambda e: e.scalar_tensor_tensor(r8[:], r8[:], -1e30, e8[:], ALU.mult, ALU.add), inc="ve")
            r.wait("vector", "ve", tv)
            tv = r.op("vector", lambda e: e.reduce_max(m2, r8[:], axis=AX.X), inc="ve")
            r.wait("vector", "ve", tv)
            tv = r.op("vector", lambda e: e.tensor_tensor(s12, m1, m2, ALU.add), inc="ve")
            r.wait("vector", "ve", tv)
            tv = r.op("vector", lambda e: e.reciprocal(rcp, s12), inc="ve")
            r.wait("vector", "ve", tv)
            tv = r.op("vector", lambda e: e.tensor_scalar(r8[:], e8[:], m2, None, ALU.is_ge), inc="ve")
            r.wait("vector", "ve", tv)
            tv = r.op("vector", lambda e, t_=t: e.scalar_tensor_tensor(
                self.vwe[:, t_, :], e8[:], rcp, r8[:], ALU.mult, ALU.mult), inc="ve")
            r.barrier()

        if stop == "we":
            r.barrier()
            r.dma("sync", io["dbg0"][:, 0:128], sb["we"][:], "st")
            r.wait("sync", "st", r.cnt["st"])
            return self

        use_wgu = [0, 0]
        use_xlt = [0, 0]
        use_yain = [0, 0]
        st_yaout = [0, 0]
        prev_pbig_read = [("se", 0), ("se", 0), ("ve", 0), ("ve", 0)]
        dn_read = [0, 0]
        ldw_done = [0, 0]
        for e_i in range(8):
            wb = e_i % 2
            wsem = "d0" if wb == 0 else "d1"
            ldg = r.dma("gpsimd", self.vwgu[:].rearrange("p (k n) -> p k n", k=NKT)
                        if False else sb["wgu"][:].rearrange("p (k n) -> p k n", k=NKT),
                        io["l0_wgu"][e_i].rearrange("(k p) n -> p k n", p=128), "dg",
                        waits=[("pe", use_wgu[0])])
            ldw = r.dma("gpsimd", self.vwd[:, wb, :, :].rearrange("p k n -> p (k n)"),
                        io["l0_wd"][e_i].rearrange("(k p) n -> p k n", p=128), wsem,
                        waits=[("pe", use_wgu[1])])
            ldw_done[wb] = ldw
            ldg_done = r.cnt["dg"]
            for t in range(16):
                buf = t % 2
                ld = r.dma("sync", self.vxlt[:, buf, :], io["xlnT2"][t * 128:(t + 1) * 128, :],
                           "ds", waits=[("pe", use_xlt[buf])])
                if e_i > 0:
                    r.dma("sync", self.vya[:, buf, :], io["y_acc"][t * 128:(t + 1) * 128, :],
                          "ds", waits=[("ve", use_yain[buf])])
                if e_i == 7:
                    r.dma("sync", sb["x1r"][:], io["x1"][t * 128:(t + 1) * 128, :], "ds",
                          waits=[("ve", use_yain[buf])])
                ld_all = r.cnt["ds"]
                r.wait("tensor", "ds", ld_all)
                r.wait("tensor", "dg", ldg_done)
                r.wait("tensor", wsem, ldw_done[wb])
                vx = self.vxlt[:, buf, :].rearrange("p (k n) -> p k n", k=NKT)
                gu_t = [0, 0, 0, 0]
                for g in range(4):
                    s_, v_ = prev_pbig_read[g]
                    r.wait("tensor", s_, v_)
                    for kt in range(NKT):
                        r.op("tensor", lambda e, k=kt, G=g, VX=vx: e.matmul(
                            pbig[:, G * 512:(G + 1) * 512], VX[:, k, :],
                            self.vwgu[:, k, G * 512:(G + 1) * 512],
                            start=(k == 0), stop=(k == NKT - 1), skip_group_check=True),
                            inc=("pe" if kt == NKT - 1 else None))
                    gu_t[g] = r.cnt["pe"]
                gu = gu_t[3]
                use_xlt[buf] = gu
                sil_t = []
                for g in range(2):
                    r.wait("scalar", "pe", gu_t[g])
                    r.op("scalar", lambda e, G=g: e.activation(
                        sb["sg"][:, G * 512:(G + 1) * 512], pbig[:, G * 512:(G + 1) * 512],
                        AF.Silu), inc="se")
                    sil_t.append(r.cnt["se"])
                h_t = []
                for g in range(2):
                    r.wait("vector", "pe", gu_t[2 + g])
                    r.wait("vector", "se", sil_t[g])
                    r.op("vector", lambda e, G=g, t_=t, E=e_i: e.scalar_tensor_tensor(
                        sb["hh"][:, G * 512:(G + 1) * 512], pbig[:, 1024 + G * 512:1536 + G * 512],
                        self.vwe[:, t_, E:E + 1], sb["sg"][:, G * 512:(G + 1) * 512],
                        ALU.mult, ALU.mult), inc="ve")
                    h_t.append(r.cnt["ve"])
                hdone = r.cnt["ve"]
                prev_pbig_read = [("se", sil_t[0]), ("se", sil_t[1]),
                                  ("ve", h_t[0]), ("ve", h_t[1])]
                hT_t = []
                for ki in range(8):
                    if ki < 2:
                        r.wait("tensor", "ve", hdone)
                    hT_t.append(h.transpose(sb["hh"][:, ki * 128:(ki + 1) * 128],
                                            self.vhT[:, ki, :]))
                dn_groups = []
                for half_g in range(2):
                    for g2 in range(2):
                        P = pA if g2 == 0 else pB
                        G = half_g * 2 + g2
                        r.wait("tensor", "ve", dn_read[g2])
                        for ki in range(8):
                            r.wait("tensor", hT_t[ki][0], hT_t[ki][1])
                            r.op("tensor", lambda e, K_=ki, P_=P, G_=G, WB=wb: e.matmul(
                                P_[:], self.vhT[:, K_, :],
                                self.vwd[:, WB, K_, G_ * 512:(G_ + 1) * 512],
                                start=(K_ == 0), stop=(K_ == 7), skip_group_check=True),
                                inc=("pe" if ki == 7 else None))
                        dn_groups.append((G, g2, r.cnt["pe"]))
                    for (G, g2, tgt) in dn_groups[-2:]:
                        P = pA if g2 == 0 else pB
                        sl = slice(G * 512, (G + 1) * 512)
                        r.wait("vector", "pe", tgt)
                        if half_g == 0 and G == 0:
                            r.wait("vector", "ds", ld_all)
                            r.wait("vector", "st", st_yaout[0])
                        if e_i == 0:
                            tvv = r.op("vector", lambda e, S_=sl, P_=P: e.tensor_copy(
                                self.vya[:, 2, S_], P_[:]), inc="ve")
                        else:
                            tvv = r.op("vector", lambda e, S_=sl, P_=P, B=buf: e.scalar_tensor_tensor(
                                self.vya[:, 2, S_], P_[:], 1.0, self.vya[:, B, S_],
                                ALU.mult, ALU.add), inc="ve")
                        dn_read[g2] = tvv
                if t == 15:
                    use_wgu[0] = gu
                    use_wgu[1] = r.cnt["pe"]
                ev = r.cnt["ve"]
                use_yain[buf] = ev
                if e_i == 7:
                    r.wait("vector", "ve", ev)
                    tv = r.op("vector", lambda e: e.tensor_tensor(
                        self.vya[:, 2, :], self.vya[:, 2, :], sb["x1r"][:], ALU.add), inc="ve")
                    ev = tv
                dst = io["x2"] if e_i == 7 else io["y_acc"]
                st = r.dma("sync", dst[t * 128:(t + 1) * 128, :], self.vya[:, 2, :], "st",
                           waits=[("ve", ev)])
                st_yaout[0] = st
        r.barrier()
        return self


def declare_io_phase1(nc):
    io = {}
    io["l0_rgT"] = nc.dram_tensor("l0_rgT", [D, 8], F32, kind="ExternalInput").ap()
    io["l0_wgu"] = nc.dram_tensor("l0_wgu", [8, D, 2048], F32, kind="ExternalInput").ap()
    io["l0_wd"] = nc.dram_tensor("l0_wd", [8, 1024, 2048], F32, kind="ExternalInput").ap()
    io["xlnT2"] = nc.dram_tensor("xlnT2", [16 * 128, D], BF16, kind="Internal").ap()
    io["y_acc"] = nc.dram_tensor("y_acc", [TOK, D], F32, kind="Internal").ap()
    return io


def build_p1_program(stop=None):
    nc = bass.Bass("TRN2", target_bir_lowering=False)
    io = declare_io_phase1(nc)
    io["x1"] = nc.dram_tensor("x1", [TOK, D], F32, kind="ExternalInput").ap()
    io["x2"] = nc.dram_tensor("x2", [TOK, D], F32, kind="ExternalOutput").ap()
    if stop is not None:
        io["dbg0"] = nc.dram_tensor("dbg0", [128, 8192], F32, kind="ExternalOutput").ap()
    with ExitStack() as ctx:
        r = Rec(nc, ctx)
        Phase1(r, ctx, nc, io).emit(stop=stop)
        r.wait("sync", "st", r.cnt["st"])
        blk = ctx.enter_context(nc.Block())
        r.replay(blk)
    return nc


def prep_phase1_weights(router_gate, expert_gate_up, expert_down):
    rgT = np.ascontiguousarray(np.asarray(router_gate, np.float32).T)
    eg = np.asarray(expert_gate_up, np.float32)
    wgu = np.ascontiguousarray(np.concatenate([eg[:, :, 0::2], eg[:, :, 1::2]], axis=2))
    wd = np.ascontiguousarray(np.asarray(expert_down, np.float32))
    return rgT, wgu, wd


class Phase2:

    def __init__(self, r, ctx, nc, io):
        self.r, self.nc, self.io = r, nc, io
        self.h = Helpers(r, ctx, nc, "p2_")
        sb = self.sb = self.h.sb

        def S(name, shape, dt):
            sb[name] = ctx.enter_context(nc.sbuf_tensor("p2_" + name, shape, dt))
        S("wq", [128, NKT * 1024], BF16)
        S("wk", [128, NKT * 1024], BF16)
        S("wv", [128, NKT * 1024], BF16)
        S("bqkv", [1, 3 * 1024], BF16)
        S("bo", [1, 2048], BF16)
        S("RT", [128, 64], BF16)
        S("mq", [128, 128], BF16)
        S("ones1", [1, 128], BF16)
        S("cos", [128, 4 * 64], F32)
        S("sin", [128, 4 * 64], F32)
        S("x2t", [128, D], F32)
        S("xln", [128, D], BF16)
        S("xlnT", [128, NKT * 128], BF16)
        S("vst", [128, 1024], BF16)
        S("qh", [128, 1024], BF16)
        S("kh", [128, 1024], BF16)
        S("ta", [128, 512], F32)
        S("tb", [128, 512], F32)
        S("qpT", [128, 8 * 512], BF16)
        S("kpT", [128, 8 * 512], BF16)
        S("vT", [128, 8 * 512], BF16)
        S("chalf", [64, 1], F32)
        S("atm", [128, 128], BF16)
        S("qblk", [128, 128], BF16)
        S("kblk", [128, 128], BF16)
        S("vblk", [128, 128], BF16)
        S("varr", [128, 128], BF16)
        S("o1", [128, 128], BF16)
        S("yTh", [128, 8 * 2048], BF16)
        S("wos", [128, 2 * 512], BF16)
        S("x2o", [128, 512], F32)
        S("x3p", [128, 512], F32)
        self.pbig = ctx.enter_context(nc.psum_tensor("p2_pbig", [128, 2048], F32))
        self.pA = ctx.enter_context(nc.psum_tensor("p2_pA", [128, 512], F32))
        self.pB = ctx.enter_context(nc.psum_tensor("p2_pB", [128, 512], F32))
        self.vwq = sb["wq"][:].rearrange("p (k n) -> p k n", k=NKT)
        self.vwk = sb["wk"][:].rearrange("p (k n) -> p k n", k=NKT)
        self.vwv = sb["wv"][:].rearrange("p (k n) -> p k n", k=NKT)
        self.vcos = sb["cos"][:].rearrange("p (t f) -> p t f", t=4)
        self.vsin = sb["sin"][:].rearrange("p (t f) -> p t f", t=4)

        self.vxlnT = sb["xlnT"][:].rearrange("p (k n) -> p k n", k=NKT)
        self.vqpT = sb["qpT"][:].rearrange("p (h n) -> p h n", h=8)
        self.vkpT = sb["kpT"][:].rearrange("p (h n) -> p h n", h=8)
        self.vvT = sb["vT"][:].rearrange("p (h n) -> p h n", h=8)
        self.vyTh = sb["yTh"][:].rearrange("p (h n) -> p h n", h=8)
        self.vwos = sb["wos"][:].rearrange("p (b n) -> p b n", b=2)


    def rope_rh(self, src, dst, heads, ctile):
        r, sb = self.r, self.sb
        cosb = self.vcos[:, ctile, :].unsqueeze(1).to_broadcast([128, heads, 64])
        sinb = self.vsin[:, ctile, :].unsqueeze(1).to_broadcast([128, heads, 64])
        xlo = src[:, :, 0:64]
        xhi = src[:, :, 64:128]
        ta = sb["ta"][:].rearrange("p (h f) -> p h f", h=heads)[:, :, 0:64]
        tb = sb["tb"][:].rearrange("p (h f) -> p h f", h=heads)[:, :, 0:64]
        r.cop("vector", lambda e, o=ta, a=xlo, b=cosb: e.tensor_tensor(o, a, b, ALU.mult))
        r.cop("vector", lambda e, o=tb, a=xhi, b=sinb: e.tensor_tensor(o, a, b, ALU.mult))
        r.cop("vector", lambda e, o=dst[:, :, 0:64], a=ta, b=tb: e.tensor_tensor(o, a, b, ALU.subtract))
        r.cop("vector", lambda e, o=ta, a=xhi, b=cosb: e.tensor_tensor(o, a, b, ALU.mult))
        r.cop("vector", lambda e, o=tb, a=xlo, b=sinb: e.tensor_tensor(o, a, b, ALU.mult))
        return r.cop("vector", lambda e, o=dst[:, :, 64:128], a=ta, b=tb: e.tensor_tensor(o, a, b, ALU.add))

    def norm8(self, src3, stat0, dep_ve=0):
        r, sb = self.r, self.sb
        nrm = sb["stat"][:, stat0:stat0 + 8]
        r.wait("scalar", "ve", dep_ve)
        t1 = r.cop("scalar", lambda e, o=sb["sq"][:, 0:1024], a=src3.rearrange("p h f -> p (h f)"):
                   e.activation(o, a, AF.Square))
        r.wait("vector", "se", t1)
        tv = r.cop("vector", lambda e, o=nrm, a=sb["sq"][:, 0:1024].rearrange("p (h f) -> p h f", h=8):
                   e.reduce_sum(o, a, axis=AX.X))
        r.wait("scalar", "ve", tv)
        ts_ = r.cop("scalar", lambda e, o=nrm, a=nrm: e.activation(o, a, AF.Ln))
        ts_ = r.cop("scalar", lambda e, o=nrm, a=nrm: e.activation(o, a, AF.Exp, scale=-0.5))
        r.wait("vector", "se", ts_)
        return r.cop("vector", lambda e, o=nrm, a=nrm: e.tensor_scalar_min(o, a, 1e6))

    def emit(self, stop=None):
        r, sb, io = self.r, self.sb, self.io
        h = self.h
        pbig, pA, pB = self.pbig, self.pA, self.pB
        INV2PI = float(1.0 / (2 * np.pi))
        TWOPI = float(2 * np.pi)
        MAGIC = 12582912.0

        r.dma("gpsimd", sb["wq"][:].rearrange("p (k n) -> p k n", k=NKT),
              io["l1_wqT"].rearrange("(k p) n -> p k n", p=128), "dg")
        r.dma("gpsimd", sb["wk"][:].rearrange("p (k n) -> p k n", k=NKT),
              io["l1_wkT"].rearrange("(k p) n -> p k n", p=128), "dg")
        r.dma("gpsimd", sb["wv"][:].rearrange("p (k n) -> p k n", k=NKT),
              io["l1_wvT"].rearrange("(k p) n -> p k n", p=128), "dg")
        r.dma("gpsimd", sb["bqkv"][0:1, 0:1024], io["l1_bq"], "dg")
        r.dma("gpsimd", sb["bqkv"][0:1, 1024:2048], io["l1_bk"], "dg")
        r.dma("gpsimd", sb["bqkv"][0:1, 2048:3072], io["l1_bv"], "dg")
        r.dma("gpsimd", sb["bo"][:], io["l1_bo"], "dg")
        r.dma("gpsimd", sb["RT"][:], io["l1_RT"], "dg")
        r.dma("gpsimd", sb["mq"][:], io["l1_mq"], "dg")
        r.op("vector", lambda e: e.memset(sb["ones1"][:], 1.0), inc="ve")
        r.op("vector", lambda e: e.memset(sb["chalf"][:], float(np.pi / 2)), inc="ve")
        r.barrier()

        for half in range(2):
            for ch in range(4):
                cc = half * 4 + ch
                r.dma("sync", sb["cos"][:].rearrange("p (t f) -> p t f", t=4),
                      io["cosT2"][cc * 512:(cc + 1) * 512, :].rearrange("(t p) f -> p t f", p=128), "ds")
                r.dma("sync", sb["sin"][:].rearrange("p (t f) -> p t f", t=4),
                      io["sinT2"][cc * 512:(cc + 1) * 512, :].rearrange("(t p) f -> p t f", p=128), "ds")
                for ts in range(4):
                    tile = cc * 4 + ts
                    ld = r.dma("sync", sb["x2t"][:], io["x2f"][tile * 128:(tile + 1) * 128, :], "ds")
                    r.wait("vector", "ds", r.cnt["ds"])
                    r.wait("scalar", "ds", r.cnt["ds"])
                    tn = h.ln_tile(sb["x2t"][:], sb["xln"][:])
                    r.wait("tensor", "ve", tn)
                    for kt in range(NKT):
                        h.transpose(sb["xln"][:, kt * 128:(kt + 1) * 128],
                                    self.vxlnT[:, kt, :])
                    r.wait("tensor", "ve", r.cnt["ve"])
                    r.wait("tensor", "se", r.cnt["se"])
                    for kt in range(NKT):
                        for g in range(2):
                            r.op("tensor", lambda e, k=kt, G=g: e.matmul(
                                pbig[:, G * 512:(G + 1) * 512],
                                self.vxlnT[:, k, :],
                                self.vwq[:, k, G * 512:(G + 1) * 512],
                                start=(k == 0), stop=False, skip_group_check=True))
                            r.op("tensor", lambda e, k=kt, G=g: e.matmul(
                                pbig[:, 1024 + G * 512:1536 + G * 512],
                                self.vxlnT[:, k, :],
                                self.vwk[:, k, G * 512:(G + 1) * 512],
                                start=(k == 0), stop=False, skip_group_check=True))
                            r.op("tensor", lambda e, k=kt, G=g, P=(pA if g == 0 else pB): e.matmul(
                                P[:], self.vxlnT[:, k, :],
                                self.vwv[:, k, G * 512:(G + 1) * 512],
                                start=(k == 0), stop=False, skip_group_check=True))
                    for g in range(2):
                        r.op("tensor", lambda e, G=g: e.matmul(
                            pbig[:, G * 512:(G + 1) * 512], sb["ones1"][:],
                            sb["bqkv"][0:1, G * 512:(G + 1) * 512],
                            start=False, stop=True, skip_group_check=True))
                        r.op("tensor", lambda e, G=g: e.matmul(
                            pbig[:, 1024 + G * 512:1536 + G * 512], sb["ones1"][:],
                            sb["bqkv"][0:1, 1024 + G * 512:1536 + G * 512],
                            start=False, stop=True, skip_group_check=True))
                        r.op("tensor", lambda e, G=g, P=(pA if g == 0 else pB): e.matmul(
                            P[:], sb["ones1"][:],
                            sb["bqkv"][0:1, 2048 + G * 512:2560 + G * 512],
                            start=False, stop=True, skip_group_check=True), inc="pe")
                    qkv_pe = r.cnt["pe"]
                    r.wait("vector", "pe", qkv_pe)
                    r.cop("vector", lambda e: e.tensor_copy(sb["vst"][:, 0:512], pA[:]))
                    tvv = r.cop("vector", lambda e: e.tensor_copy(sb["vst"][:, 512:1024], pB[:]))
                    r.wait("tensor", "ve", tvv)
                    for hh in range(8):
                        h.transpose(sb["vst"][:, hh * 128:(hh + 1) * 128],
                                    self.vvT[:, hh, ts * 128:(ts + 1) * 128])
                    q3 = pbig[:, 0:1024].rearrange("p (h f) -> p h f", h=8)
                    qro3 = sb["sq"][:, 1024:2048].rearrange("p (h f) -> p h f", h=8)
                    trq = self.rope_rh(q3, qro3, 8, ts)
                    self.norm8(qro3, 8, dep_ve=trq)
                    tqh = r.cop("vector", lambda e, o=sb["qh"][:].rearrange("p (h f) -> p h f", h=8),
                                a=qro3,
                                b=sb["stat"][:, 8:16].unsqueeze(2).to_broadcast([128, 8, 128]):
                                e.tensor_tensor(o, a, b, ALU.mult))
                    r.wait("tensor", "ve", tqh)
                    for hh in range(8):
                        h.transpose(sb["qh"][:, hh * 128:(hh + 1) * 128],
                                    self.vqpT[:, hh, ts * 128:(ts + 1) * 128])
                    k3 = pbig[:, 1024:2048].rearrange("p (h f) -> p h f", h=8)
                    kro3 = sb["sq"][:, 1024:2048].rearrange("p (h f) -> p h f", h=8)
                    trk = self.rope_rh(k3, kro3, 8, ts)
                    self.norm8(kro3, 16, dep_ve=trk)
                    tkh = r.cop("vector", lambda e, o=sb["kh"][:].rearrange("p (h f) -> p h f", h=8),
                                a=kro3,
                                b=sb["stat"][:, 16:24].unsqueeze(2).to_broadcast([128, 8, 128]):
                                e.tensor_tensor(o, a, b, ALU.mult))
                    r.wait("tensor", "ve", tkh)
                    for hh in range(8):
                        h.transpose(sb["kh"][:, hh * 128:(hh + 1) * 128],
                                    self.vkpT[:, hh, ts * 128:(ts + 1) * 128])
                    r.barrier()
                pb_ev = [0, 0]
                for src_name, dst_name in (("qpT", "qpT"), ("kpT", "kpT")):
                    for hh in range(8):
                        i = hh % 2
                        pb_ = pbig[:, i * 512:(i + 1) * 512]
                        r.wait("tensor", "ve", pb_ev[i])
                        mm = r.op("tensor", lambda e, H_=hh, S_=src_name, P=pb_: e.matmul(
                            P[0:64, :], sb["RT"][:],
                            self.sb[S_][:].rearrange("p (h n) -> p h n", h=8)[:, H_, :],
                            start=True, stop=True, skip_group_check=True), inc="pe")
                        r.wait("vector", "pe", mm)
                        t1 = r.cop("vector", lambda e, P=pb_: e.tensor_scalar(
                            sb["ta"][0:64, :], P[0:64, :], INV2PI, MAGIC, ALU.mult, ALU.add))
                        t1 = r.cop("vector", lambda e: e.tensor_scalar(
                            sb["ta"][0:64, :], sb["ta"][0:64, :], MAGIC, TWOPI, ALU.subtract, ALU.mult))
                        t1 = r.cop("vector", lambda e, P=pb_: e.scalar_tensor_tensor(
                            sb["tb"][0:64, :], P[0:64, :], 1.0, sb["ta"][0:64, :], ALU.mult, ALU.subtract))
                        pb_ev[i] = t1
                        r.wait("scalar", "ve", t1)
                        dst = self.sb[dst_name][:].rearrange("p (h n) -> p h n", h=8)
                        ts1 = r.cop("scalar", lambda e, H_=hh, DD=dst: e.activation(
                            DD[0:64, H_, :], sb["tb"][0:64, :], AF.Sin))
                        ts1 = r.cop("scalar", lambda e: e.activation(
                            sb["ta"][0:64, :], sb["tb"][0:64, :], AF.Abs))
                        ts1 = r.cop("scalar", lambda e, H_=hh, DD=dst: e.activation(
                            DD[64:128, H_, :], sb["ta"][0:64, :], AF.Sin,
                            scale=-1.0, bias=sb["chalf"][:]))
                        r.wait("tensor", "se", ts1)
                    r.barrier()
                at_ev = [0, 0]
                o_ev = [0, 0]
                for tl in range(4):
                    for blk in range(8):
                        i = blk % 2
                        pat = (pA if i == 0 else pB)
                        po = pbig[:, (2 + i) * 512:(3 + i) * 512]
                        c0 = tl * 128 + blk * 16
                        tq = r.cop("vector", lambda e, C=c0: e.tensor_copy(
                            sb["qblk"][:].rearrange("p (h n) -> p h n", h=8),
                            self.sb["qpT"][:].rearrange("p (h n) -> p h n", h=8)[:, :, C:C + 16]))
                        tq = r.cop("vector", lambda e, C=c0: e.tensor_copy(
                            sb["kblk"][:].rearrange("p (h n) -> p h n", h=8),
                            self.sb["kpT"][:].rearrange("p (h n) -> p h n", h=8)[:, :, C:C + 16]))
                        r.wait("tensor", "ve", max(at_ev[i], tq))
                        mm = r.op("tensor", lambda e, P=pat: e.matmul(
                            P[:, 0:128], sb["kblk"][:], sb["qblk"][:],
                            start=True, stop=True, skip_group_check=True), inc="pe")
                        r.wait("vector", "pe", mm)
                        tv = r.cop("vector", lambda e, P=pat: e.tensor_tensor(
                            sb["atm"][:], P[:, 0:128], sb["mq"][:], ALU.mult))
                        at_ev[i] = tv
                        tvb = r.cop("vector", lambda e, C=c0: e.tensor_copy(
                            sb["vblk"][:].rearrange("p (h n) -> p h n", h=8),
                            self.sb["vT"][:].rearrange("p (h n) -> p h n", h=8)[:, :, C:C + 16]))
                        r.wait("tensor", "ve", tvb)
                        tvr = h.transpose(sb["vblk"][:], sb["varr"][:])
                        r.wait("tensor", "ve", tv)
                        r.wait("tensor", tvr[0], tvr[1])
                        r.wait("tensor", "ve", o_ev[i])
                        mo = r.op("tensor", lambda e, P=po: e.matmul(
                            P[:, 0:128], sb["atm"][:], sb["varr"][:],
                            start=True, stop=True, skip_group_check=True), inc="pe")
                        r.wait("vector", "pe", mo)
                        tc = r.cop("vector", lambda e, P=po: e.tensor_copy(sb["o1"][:], P[:, 0:128]))
                        o_ev[i] = tc
                        r.wait("tensor", "ve", tc)
                        i2 = h.pp["idx"] % 2
                        pt = h.pt[i2]
                        if h.pp["last"][i2] is not None:
                            s_, v_ = h.pp["last"][i2]
                            r.wait("tensor", s_, v_)
                        tp = r.op("tensor", lambda e, P=pt: e.transpose(
                            P[:], sb["o1"][:], h.sb["ident"][:]), inc="pe")
                        r.wait("vector", "pe", tp)
                        pos0 = (cc - half * 4) * 512 + c0
                        te = r.cop("vector", lambda e, P=pt, P0=pos0: e.tensor_copy(
                            self.vyTh[:, :, P0:P0 + 16],
                            P[:].rearrange("p (h n) -> p h n", h=8)))
                        h.pp["last"][i2] = ("ve", te)
                        h.pp["idx"] += 1
                r.barrier()
            tiles = [t for t in range(16) if t % 2 == half]
            vyv = self.sb["yTh"][:].rearrange("p (h n s) -> p h s n", h=8, s=16)
            for grp4 in range(2):
                tl4 = tiles[grp4 * 4:(grp4 + 1) * 4]
                ws_use = [0, 0]
                prev_ev = 0
                for colq in range(4):
                    for kt in range(NKT):
                        wbuf = kt % 2
                        wsem = "d0" if wbuf == 0 else "d1"
                        ld = r.dma("gpsimd", self.vwos[:, wbuf, :],
                                   io["l1_woT"][kt * 128:(kt + 1) * 128, colq * 512:(colq + 1) * 512],
                                   wsem, waits=[("pe", ws_use[wbuf])])
                        r.wait("tensor", wsem, ld)
                        for i_t, t in enumerate(tl4):
                            hl = t // 2
                            r.op("tensor", lambda e, K_=kt, IT=i_t, HL=hl, B=wbuf: e.matmul(
                                pbig[:, IT * 512:(IT + 1) * 512],
                                vyv[:, HL, K_, :], self.vwos[:, B, :],
                                start=(K_ == 0), stop=False, skip_group_check=True),
                                inc=("pe" if i_t == 3 else None))
                        ws_use[wbuf] = r.cnt["pe"]
                    for i_t in range(4):
                        r.op("tensor", lambda e, IT=i_t, C=colq: e.matmul(
                            pbig[:, IT * 512:(IT + 1) * 512], sb["ones1"][:],
                            sb["bo"][0:1, C * 512:(C + 1) * 512],
                            start=False, stop=True, skip_group_check=True),
                            inc=("pe" if i_t == 3 else None))
                    ope = r.cnt["pe"]
                    x3st = 0
                    for i_t, t in enumerate(tl4):
                        ld = r.dma("sync", sb["x2o"][:],
                                   io["x2own"][t * 128:(t + 1) * 128, colq * 512:(colq + 1) * 512],
                                   "ds", waits=[("ve", r.cnt["ve"])])
                        r.wait("vector", "ds", r.cnt["ds"])
                        r.wait("vector", "pe", ope)
                        r.wait("vector", "st", x3st)
                        tv = r.cop("vector", lambda e, IT=i_t: e.tensor_tensor(
                            sb["x3p"][:], pbig[:, IT * 512:(IT + 1) * 512],
                            sb["x2o"][:], ALU.add))
                        st = r.dma("sync", io["x3"][t * 128:(t + 1) * 128, colq * 512:(colq + 1) * 512],
                                   sb["x3p"][:], "st", waits=[("ve", tv)])
                        x3st = st
                    prev_ev = r.cnt["ve"]
                    r.barrier()
                r.barrier()
        return self


def declare_io_phase2(nc):
    io = {}
    io["l1_wqT"] = nc.dram_tensor("l1_wqT", [D, 1024], F32, kind="ExternalInput").ap()
    io["l1_wkT"] = nc.dram_tensor("l1_wkT", [D, 1024], F32, kind="ExternalInput").ap()
    io["l1_wvT"] = nc.dram_tensor("l1_wvT", [D, 1024], F32, kind="ExternalInput").ap()
    io["l1_woT"] = nc.dram_tensor("l1_woT", [D, D], F32, kind="ExternalInput").ap()
    io["l1_bq"] = nc.dram_tensor("l1_bq", [1, 1024], F32, kind="ExternalInput").ap()
    io["l1_bk"] = nc.dram_tensor("l1_bk", [1, 1024], F32, kind="ExternalInput").ap()
    io["l1_bv"] = nc.dram_tensor("l1_bv", [1, 1024], F32, kind="ExternalInput").ap()
    io["l1_bo"] = nc.dram_tensor("l1_bo", [1, 2048], F32, kind="ExternalInput").ap()
    io["l1_RT"] = nc.dram_tensor("l1_RT", [128, 64], F32, kind="ExternalInput").ap()
    io["l1_mq"] = nc.dram_tensor("l1_mq", [128, 128], F32, kind="ExternalInput").ap()
    io["cosT2"] = nc.dram_tensor("cosT2", [4096, 64], F32, kind="ExternalInput").ap()
    io["sinT2"] = nc.dram_tensor("sinT2", [4096, 64], F32, kind="ExternalInput").ap()
    return io


def build_p2_program(stop=None):
    nc = bass.Bass("TRN2", target_bir_lowering=False)
    io = declare_io_phase2(nc)
    io["x2f"] = nc.dram_tensor("x2f", [2 * TOK, D], F32, kind="ExternalInput").ap()
    io["x2own"] = nc.dram_tensor("x2own", [TOK, D], F32, kind="ExternalInput").ap()
    io["x3"] = nc.dram_tensor("x3", [TOK, D], F32, kind="ExternalOutput").ap()
    with ExitStack() as ctx:
        r = Rec(nc, ctx)
        Phase2(r, ctx, nc, io).emit(stop=stop)
        r.wait("sync", "st", r.cnt["st"])
        blk = ctx.enter_context(nc.Block())
        r.replay(blk)
    return nc


def prep_phase2_inputs(Wq, bq, Wk, bk, Wv, bv, Wo, bo, rand_matrix):
    WqT = np.ascontiguousarray(np.asarray(Wq, np.float32).T)
    out = {}
    out["l1_wkT"] = np.ascontiguousarray(np.asarray(Wk, np.float32).T)
    out["l1_wvT"] = np.ascontiguousarray(np.asarray(Wv, np.float32).T)
    out["l1_woT"] = np.ascontiguousarray(np.asarray(Wo, np.float32).T)
    out["l1_bk"] = np.asarray(bk, np.float32).reshape(1, -1)
    out["l1_bv"] = np.asarray(bv, np.float32).reshape(1, -1)
    out["l1_bo"] = np.asarray(bo, np.float32).reshape(1, -1)
    out["l1_RT"] = np.ascontiguousarray(np.asarray(rand_matrix, np.float32).T)
    mq = np.zeros((128, 128), np.float32)
    for kv in range(8):
        for p in range(16):
            for hl in range(8):
                mq[kv * 16 + p, hl * 16 + p] = 1.0 / 64.0
    out["l1_mq"] = mq
    inv = 1.0 / 10000.0 ** (np.arange(0, DH, 2, dtype=np.float64) / DH)
    ang = np.outer(np.arange(4096, dtype=np.float64), inv)
    out["cosT2"] = np.ascontiguousarray(np.cos(ang).astype(np.float32))
    out["sinT2"] = np.ascontiguousarray(np.sin(ang).astype(np.float32))
    bq = np.asarray(bq, np.float32).reshape(-1)
    per_core = []
    for c in range(8):
        h_core = c % 2
        per_core.append(dict(
            l1_wqT=np.ascontiguousarray(WqT[:, h_core * 1024:(h_core + 1) * 1024]),
            l1_bq=np.ascontiguousarray(bq[h_core * 1024:(h_core + 1) * 1024].reshape(1, -1)),
        ))
    return out, per_core


class Phase3:

    def __init__(self, r, ctx, nc, io):
        self.r, self.nc, self.io = r, nc, io
        self.h = Helpers(r, ctx, nc, "p3_")
        sb = self.sb = self.h.sb

        def S(name, shape, dt):
            sb[name] = ctx.enter_context(nc.sbuf_tensor("p3_" + name, shape, dt))
        S("x3t", [128, D], F32)
        S("xln", [128, D], BF16)
        S("xlnT", [128, NKT * 128], BF16)
        S("RgT", [128, NKT * 128], BF16)
        S("RuT", [128, NKT * 128], BF16)
        S("sRgT", [128, NKT * 128], BF16)
        S("sRuT", [128, NKT * 128], BF16)
        S("gwT", [128, NKT * 32], BF16)
        S("rgT", [128, 2048], BF16)
        S("ruT", [128, 2048], BF16)
        S("rsgT", [128, 2048], BF16)
        S("rsuT", [128, 2048], BF16)
        S("we4", [128, 16 * 32], F32)
        S("weT", [32, 2048], F32)
        S("e32", [128, 32], F32)
        S("cur", [128, 32], F32)
        S("msk", [128, 32], F32)
        S("wef", [1, 2048], F32)
        S("onesf", [1, 128], F32)
        S("identf", [128, 128], F32)
        S("UgT", [128, 2 * 128], BF16)
        S("UuT", [128, 2 * 128], BF16)
        S("UdT", [128, 2 * 128], BF16)
        S("CgT", [128, 2 * 11 * 128], BF16)
        S("CuT", [128, 2 * 11 * 128], BF16)
        S("sUgT", [128, 128], BF16)
        S("sUuT", [128, 128], BF16)
        S("sUdT", [128, 128], BF16)
        S("sCgT", [128, 22 * 128], BF16)
        S("sCuT", [128, 22 * 128], BF16)
        S("sRdT", [128, 22 * 128], BF16)
        S("RdT", [128, 11 * 128], BF16)
        S("t1g", [128, 512], BF16)
        S("t1u", [128, 512], BF16)
        S("g1", [128, 512], BF16)
        S("u1", [128, 512], BF16)
        S("m1", [128, 512], BF16)
        S("rdb", [128, 512], BF16)
        S("t2b", [128, 512], BF16)
        S("t2T", [128, 17 * 2048], BF16)
        S("CdTg", [128, 17 * 512], BF16)
        S("yp", [128, 512], F32)
        S("x3r", [128, 512], F32)
        S("x4p", [128, 512], F32)
        self.pbig = ctx.enter_context(nc.psum_tensor("p3_pbig", [128, 2048], F32))
        self.pA = ctx.enter_context(nc.psum_tensor("p3_pA", [128, 512], F32))
        self.pB = ctx.enter_context(nc.psum_tensor("p3_pB", [128, 512], F32))
        self.vxlnT = sb["xlnT"][:].rearrange("p (k n) -> p k n", k=NKT)
        self.vRgT = sb["RgT"][:].rearrange("p (k n) -> p k n", k=NKT)
        self.vRuT = sb["RuT"][:].rearrange("p (k n) -> p k n", k=NKT)
        self.vsRgT = sb["sRgT"][:].rearrange("p (k n) -> p k n", k=NKT)
        self.vsRuT = sb["sRuT"][:].rearrange("p (k n) -> p k n", k=NKT)
        self.vgwT = sb["gwT"][:].rearrange("p (k n) -> p k n", k=NKT)
        self.vwe4 = sb["we4"][:].rearrange("p (t n) -> p t n", t=16)
        self.vUgT = sb["UgT"][:].rearrange("p (b n) -> p b n", b=2)
        self.vUuT = sb["UuT"][:].rearrange("p (b n) -> p b n", b=2)
        self.vUdT = sb["UdT"][:].rearrange("p (b n) -> p b n", b=2)
        self.vCgT = sb["CgT"][:].rearrange("p (b k n) -> p b k n", b=2, k=11)
        self.vCuT = sb["CuT"][:].rearrange("p (b k n) -> p b k n", b=2, k=11)
        self.vsCgT = sb["sCgT"][:].rearrange("p (k n) -> p k n", k=22)
        self.vsCuT = sb["sCuT"][:].rearrange("p (k n) -> p k n", k=22)
        self.vsRdT = sb["sRdT"][:].rearrange("p (k n) -> p k n", k=22)
        self.vRdT = sb["RdT"][:].rearrange("p (k n) -> p k n", k=11)
        self.vt2T = sb["t2T"][:].rearrange("p (s n) -> p s n", s=17)
        self.vCdTg = sb["CdTg"][:].rearrange("p (s n) -> p s n", s=17)

    def lowrank_chunk(self, cc, slot, nit, t1w_g, t1w_u, cg, cu, rdw, udw, rg_src, ru_src, wecol):
        r, sb = self.r, self.sb
        pbig, pA, pB = self.pbig, self.pA, self.pB
        csl = slice(cc * 512, (cc + 1) * 512)
        m1_ = r.op("tensor", lambda e, W=t1w_g, S_=rg_src: e.matmul(
            pbig[:, 0:512], W, S_[:, csl], start=True, stop=True, skip_group_check=True))
        m2_ = r.op("tensor", lambda e, W=t1w_u, S_=ru_src: e.matmul(
            pbig[:, 512:1024], W, S_[:, csl], start=True, stop=True, skip_group_check=True), inc="pe")
        t1pe = r.cnt["pe"]
        r.wait("vector", "pe", t1pe)
        r.cop("vector", lambda e: e.tensor_copy(sb["t1g"][:], pbig[:, 0:512]))
        tv = r.cop("vector", lambda e: e.tensor_copy(sb["t1u"][:], pbig[:, 512:1024]))
        r.wait("tensor", "ve", tv)
        g_ev = 0
        for it in range(nit):
            r.wait("tensor", "ve", g_ev)
            r.op("tensor", lambda e, I=it, C=cg: e.matmul(
                pA[:], C(I), sb["t1g"][:], start=True, stop=True, skip_group_check=True))
            mmu = r.op("tensor", lambda e, I=it, C=cu: e.matmul(
                pB[:], C(I), sb["t1u"][:], start=True, stop=True, skip_group_check=True), inc="pe")
            r.wait("scalar", "pe", mmu)
            tsg = r.cop("scalar", lambda e: e.activation(sb["g1"][:], pA[:], AF.Silu))
            r.wait("vector", "pe", mmu)
            r.cop("vector", lambda e: e.tensor_copy(sb["u1"][:], pB[:]))
            r.wait("vector", "se", tsg)
            tv = r.cop("vector", lambda e: e.tensor_tensor(sb["m1"][:], sb["g1"][:], sb["u1"][:], ALU.mult))
            g_ev = tv
            r.wait("tensor", "ve", tv)
            r.op("tensor", lambda e, I=it, W=rdw: e.matmul(
                pbig[:, 1024:1536], W(I), sb["m1"][:],
                start=(I == 0), stop=(I == nit - 1), skip_group_check=True),
                inc=("pe" if it == nit - 1 else None))
        rdpe = r.cnt["pe"]
        r.wait("vector", "pe", rdpe)
        tv = r.cop("vector", lambda e: e.tensor_copy(sb["rdb"][:], pbig[:, 1024:1536]))
        r.wait("tensor", "ve", tv)
        mt2 = r.op("tensor", lambda e, W=udw: e.matmul(
            pbig[:, 1536:2048], W, sb["rdb"][:], start=True, stop=True, skip_group_check=True), inc="pe")
        if wecol is not None:
            mb = r.op("tensor", lambda e, WC=wecol: e.matmul(
                pA[:], sb["onesf"][:], WC, start=True, stop=True, skip_group_check=True), inc="pe")
            r.wait("vector", "pe", r.cnt["pe"])
            tv = r.cop("vector", lambda e: e.tensor_copy(sb["t2b"][:], pbig[:, 1536:2048]))
            tv = r.cop("vector", lambda e, SL=slot, C=cc: e.tensor_tensor(
                self.vt2T[:, SL, C * 512:(C + 1) * 512], sb["t2b"][:], pA[:], ALU.mult))
        else:
            r.wait("vector", "pe", r.cnt["pe"])
            tv = r.cop("vector", lambda e, SL=slot, C=cc: e.tensor_copy(
                self.vt2T[:, SL, C * 512:(C + 1) * 512], pbig[:, 1536:2048]))
        r.wait("tensor", "ve", tv)
        return tv

    def emit(self, stop=None):
        r, sb, io = self.r, self.sb, self.io
        h = self.h
        pbig, pA, pB = self.pbig, self.pA, self.pB
        r.op("vector", lambda e: e.memset(sb["onesf"][:], 1.0), inc="ve")
        tg = r.op("gpsimd", lambda e: e.memset(sb["identf"][:], 1.0), inc="ge")
        r.wait("gpsimd", "ge", tg)
        r.op("gpsimd", lambda e: e.affine_select(sb["identf"][:], sb["identf"][:],
             pattern=[[1, 128]], compare_op=ALU.is_equal, fill=0.0, base=0,
             channel_multiplier=-1), inc="ge")
        r.dma("gpsimd", sb["RgT"][:].rearrange("p (k n) -> p k n", k=NKT),
              io["l1f_RgT"].rearrange("(k p) n -> p k n", p=128), "dg")
        r.dma("gpsimd", sb["RuT"][:].rearrange("p (k n) -> p k n", k=NKT),
              io["l1f_RuT"].rearrange("(k p) n -> p k n", p=128), "dg")
        r.dma("gpsimd", sb["sRgT"][:].rearrange("p (k n) -> p k n", k=NKT),
              io["l1f_sRgT"].rearrange("(k p) n -> p k n", p=128), "dg")
        r.dma("gpsimd", sb["sRuT"][:].rearrange("p (k n) -> p k n", k=NKT),
              io["l1f_sRuT"].rearrange("(k p) n -> p k n", p=128), "dg")
        r.dma("gpsimd", sb["gwT"][:].rearrange("p (k n) -> p k n", k=NKT),
              io["l1f_gwT"].rearrange("(k p) n -> p k n", p=128), "dg")
        r.dma("gpsimd", sb["sUgT"][:], io["l1f_sUgT"], "dg")
        r.dma("gpsimd", sb["sUuT"][:], io["l1f_sUuT"], "dg")
        r.dma("gpsimd", sb["sUdT"][:], io["l1f_sUdT"], "dg")
        r.dma("gpsimd", sb["sCgT"][:].rearrange("p (k n) -> p k n", k=22),
              io["l1f_sCgT"].rearrange("p (k n) -> p k n", k=22), "dg")
        r.dma("gpsimd", sb["sCuT"][:].rearrange("p (k n) -> p k n", k=22),
              io["l1f_sCuT"].rearrange("p (k n) -> p k n", k=22), "dg")
        r.dma("gpsimd", sb["sRdT"][:].rearrange("p (k n) -> p k n", k=22),
              io["l1f_sRdT"].rearrange("(k p) n -> p k n", p=128), "dg")
        r.dma("gpsimd", sb["RdT"][:].rearrange("p (k n) -> p k n", k=11),
              io["l1f_RdT"].rearrange("(k p) n -> p k n", p=128), "dg")
        r.barrier()

        for t in range(16):
            ld = r.dma("sync", sb["x3t"][:], io["x3"][t * 128:(t + 1) * 128, :], "ds")
            r.wait("vector", "ds", r.cnt["ds"])
            r.wait("scalar", "ds", r.cnt["ds"])
            tn = h.ln_tile(sb["x3t"][:], sb["xln"][:])
            r.wait("tensor", "ve", tn)
            for kt in range(NKT):
                h.transpose(sb["xln"][:, kt * 128:(kt + 1) * 128], self.vxlnT[:, kt, :])
            r.wait("tensor", "ve", r.cnt["ve"])
            r.wait("tensor", "se", r.cnt["se"])
            for name, wsrc in (("rgT", self.vRgT), ("ruT", self.vRuT),
                               ("rsgT", self.vsRgT), ("rsuT", self.vsRuT)):
                for kt in range(NKT):
                    r.op("tensor", lambda e, k=kt, W=wsrc, N=name: e.matmul(
                        pA[:, 0:128], W[:, k, :], self.vxlnT[:, k, :],
                        start=(k == 0), stop=(k == NKT - 1), skip_group_check=True),
                        inc=("pe" if kt == NKT - 1 else None))
                pe_ = r.cnt["pe"]
                r.wait("vector", "pe", pe_)
                tv = r.cop("vector", lambda e, N=name, T=t: e.tensor_copy(
                    sb[N][:, T * 128:(T + 1) * 128], pA[:, 0:128]))
                r.wait("tensor", "ve", tv)
            for kt in range(NKT):
                r.op("tensor", lambda e, k=kt: e.matmul(
                    pB[:, 0:32], self.vxlnT[:, k, :], self.vgwT[:, k, :],
                    start=(k == 0), stop=(k == NKT - 1), skip_group_check=True),
                    inc=("pe" if kt == NKT - 1 else None))
            lg = r.cnt["pe"]
            st = sb["stat"]
            mx, s4, mi, rcp = (st[:, 8 + i:9 + i] for i in range(4))
            r.wait("vector", "pe", lg)
            r.cop("vector", lambda e: e.reduce_max(mx, pB[:, 0:32], axis=AX.X))
            tv = r.cop("vector", lambda e: e.tensor_scalar(mx, mx, -1.0, None, ALU.mult))
            r.wait("scalar", "ve", tv)
            te = r.cop("scalar", lambda e: e.activation(sb["e32"][:], pB[:, 0:32], AF.Exp, bias=mx))
            r.wait("vector", "se", te)
            r.cop("vector", lambda e: e.tensor_copy(sb["cur"][:], sb["e32"][:]))
            r.cop("vector", lambda e: e.memset(s4, 0.0))
            for rnd in range(4):
                r.cop("vector", lambda e: e.reduce_max(mi, sb["cur"][:], axis=AX.X))
                r.cop("vector", lambda e: e.tensor_tensor(s4, s4, mi, ALU.add))
                if rnd < 3:
                    r.cop("vector", lambda e: e.tensor_scalar(sb["msk"][:], sb["cur"][:], mi, None, ALU.is_lt))
                    r.cop("vector", lambda e: e.tensor_tensor(sb["cur"][:], sb["cur"][:], sb["msk"][:], ALU.mult))
                    r.cop("vector", lambda e: e.tensor_scalar(sb["msk"][:], sb["msk"][:], 1e30, -1e30, ALU.mult, ALU.add))
                    r.cop("vector", lambda e: e.tensor_tensor(sb["cur"][:], sb["cur"][:], sb["msk"][:], ALU.add))
            r.cop("vector", lambda e: e.reciprocal(rcp, s4))
            r.cop("vector", lambda e: e.tensor_scalar(sb["msk"][:], sb["e32"][:], mi, None, ALU.is_ge))
            twe = r.cop("vector", lambda e, T=t: e.scalar_tensor_tensor(
                self.vwe4[:, T, :], sb["e32"][:], rcp, sb["msk"][:], ALU.mult, ALU.mult))
            r.wait("tensor", "ve", twe)
            tpw = r.op("tensor", lambda e, T=t: e.transpose(
                pA[0:32, 0:128], self.vwe4[:, T, :], sb["identf"][:]), inc="pe")
            r.wait("vector", "pe", tpw)
            tv = r.cop("vector", lambda e, T=t: e.tensor_copy(
                sb["weT"][0:32, T * 128:(T + 1) * 128], pA[0:32, 0:128]))
            r.wait("tensor", "ve", tv)
            r.barrier()
        st_w = r.dma("sync", io["weT_hbm"], sb["weT"][:], "st")
        r.barrier()

        CG_NIT, S_NIT = 11, 22
        for half in range(2):
            ew_use = [0, 0]
            for ei in range(16):
                e_glob = half * 16 + ei
                b = ei % 2
                wsem = "d0" if b == 0 else "d1"
                r.dma("gpsimd", self.vUgT[:, b, :], io["l1f_UgT"][e_glob], wsem,
                      waits=[("pe", ew_use[b])])
                r.dma("gpsimd", self.vUuT[:, b, :], io["l1f_UuT"][e_glob], wsem)
                r.dma("gpsimd", self.vUdT[:, b, :], io["l1f_UdT"][e_glob], wsem)
                r.dma("gpsimd", self.vCgT[:, b, :, :].rearrange("p k n -> p (k n)"),
                      io["l1f_CgT"][e_glob], wsem)
                ldw = r.dma("gpsimd", self.vCuT[:, b, :, :].rearrange("p k n -> p (k n)"),
                            io["l1f_CuT"][e_glob], wsem)
                r.wait("tensor", wsem, ldw)
                ldwe = r.dma("sync", sb["wef"][:], io["weT_hbm"][e_glob:e_glob + 1, :], "ds",
                             waits=[("st", st_w), ("pe", r.cnt["pe"])])
                r.wait("tensor", "ds", r.cnt["ds"])
                for cc in range(4):
                    self.lowrank_chunk(
                        cc, ei, CG_NIT,
                        sb["UgT"][:, b * 128:(b + 1) * 128],
                        sb["UuT"][:, b * 128:(b + 1) * 128],
                        (lambda I, B=b: self.vCgT[:, B, I, :]),
                        (lambda I, B=b: self.vCuT[:, B, I, :]),
                        (lambda I: self.vRdT[:, I, :]),
                        sb["UdT"][:, b * 128:(b + 1) * 128],
                        sb["rgT"], sb["ruT"],
                        sb["wef"][0:1, cc * 512:(cc + 1) * 512])
                ew_use[b] = r.cnt["pe"]
            if half == 0:
                for cc in range(4):
                    self.lowrank_chunk(
                        cc, 16, S_NIT,
                        sb["sUgT"][:], sb["sUuT"][:],
                        (lambda I: self.vsCgT[:, I, :]),
                        (lambda I: self.vsCuT[:, I, :]),
                        (lambda I: self.vsRdT[:, I, :]),
                        sb["sUdT"][:],
                        sb["rsgT"], sb["rsuT"], None)
            r.barrier()
            nslot = 17 if half == 0 else 16
            for grp in range(4):
                for s_i in range(nslot):
                    if s_i < 16:
                        srcap = io["l1f_CdT"][half * 16 + s_i][0:128, grp * 512:(grp + 1) * 512]
                    else:
                        srcap = io["l1f_sCdT"][0:128, grp * 512:(grp + 1) * 512]
                    r.dma("gpsimd", self.vCdTg[:, s_i, :], srcap, "dg")
                ldc = r.cnt["dg"]
                r.wait("tensor", "dg", ldc)
                x4st = 0
                for t in range(16):
                    pb_ = pbig[:, (t % 4) * 512:((t % 4) + 1) * 512]
                    for s_i in range(nslot):
                        r.op("tensor", lambda e, SI=s_i, T=t, P=pb_: e.matmul(
                            P, self.vt2T[:, SI, T * 128:(T + 1) * 128],
                            self.vCdTg[:, SI, :],
                            start=(SI == 0), stop=(SI == nslot - 1), skip_group_check=True),
                            inc=("pe" if s_i == nslot - 1 else None))
                    dpe = r.cnt["pe"]
                    if half == 0:
                        r.wait("vector", "pe", dpe)
                        r.wait("vector", "st", x4st)
                        tv = r.cop("vector", lambda e, P=pb_: e.tensor_copy(sb["yp"][:], P))
                        stp = r.dma("sync", io["y_part"][t * 128:(t + 1) * 128, grp * 512:(grp + 1) * 512],
                                    sb["yp"][:], "st", waits=[("ve", tv)])
                        x4st = stp
                    else:
                        ld1 = r.dma("sync", sb["yp"][:],
                                    io["y_part"][t * 128:(t + 1) * 128, grp * 512:(grp + 1) * 512],
                                    "ds", waits=[("ve", r.cnt["ve"])])
                        ld2 = r.dma("sync", sb["x3r"][:],
                                    io["x3"][t * 128:(t + 1) * 128, grp * 512:(grp + 1) * 512], "ds")
                        r.wait("vector", "ds", r.cnt["ds"])
                        r.wait("vector", "pe", dpe)
                        r.wait("vector", "st", x4st)
                        r.cop("vector", lambda e, P=pb_: e.tensor_tensor(
                            sb["x4p"][:], P, sb["yp"][:], ALU.add))
                        tv = r.cop("vector", lambda e: e.tensor_tensor(
                            sb["x4p"][:], sb["x4p"][:], sb["x3r"][:], ALU.add))
                        stp = r.dma("sync", io["x4"][t * 128:(t + 1) * 128, grp * 512:(grp + 1) * 512],
                                    sb["x4p"][:], "st", waits=[("ve", tv)])
                        x4st = stp
                    if t % 4 == 3:
                        r.barrier(("tensor", "vector"))
                r.barrier()
            r.barrier()
        return self


def declare_io_phase3(nc):
    io = {}
    io["l1f_gwT"] = nc.dram_tensor("l1f_gwT", [D, 32], F32, kind="ExternalInput").ap()
    io["l1f_RgT"] = nc.dram_tensor("l1f_RgT", [D, 128], F32, kind="ExternalInput").ap()
    io["l1f_RuT"] = nc.dram_tensor("l1f_RuT", [D, 128], F32, kind="ExternalInput").ap()
    io["l1f_RdT"] = nc.dram_tensor("l1f_RdT", [1408, 128], F32, kind="ExternalInput").ap()
    io["l1f_UgT"] = nc.dram_tensor("l1f_UgT", [32, 128, 128], F32, kind="ExternalInput").ap()
    io["l1f_CgT"] = nc.dram_tensor("l1f_CgT", [32, 128, 1408], F32, kind="ExternalInput").ap()
    io["l1f_UuT"] = nc.dram_tensor("l1f_UuT", [32, 128, 128], F32, kind="ExternalInput").ap()
    io["l1f_CuT"] = nc.dram_tensor("l1f_CuT", [32, 128, 1408], F32, kind="ExternalInput").ap()
    io["l1f_UdT"] = nc.dram_tensor("l1f_UdT", [32, 128, 128], F32, kind="ExternalInput").ap()
    io["l1f_CdT"] = nc.dram_tensor("l1f_CdT", [32, 128, 2048], F32, kind="ExternalInput").ap()
    io["l1f_sRgT"] = nc.dram_tensor("l1f_sRgT", [D, 128], F32, kind="ExternalInput").ap()
    io["l1f_sUgT"] = nc.dram_tensor("l1f_sUgT", [128, 128], F32, kind="ExternalInput").ap()
    io["l1f_sCgT"] = nc.dram_tensor("l1f_sCgT", [128, 2816], F32, kind="ExternalInput").ap()
    io["l1f_sRuT"] = nc.dram_tensor("l1f_sRuT", [D, 128], F32, kind="ExternalInput").ap()
    io["l1f_sUuT"] = nc.dram_tensor("l1f_sUuT", [128, 128], F32, kind="ExternalInput").ap()
    io["l1f_sCuT"] = nc.dram_tensor("l1f_sCuT", [128, 2816], F32, kind="ExternalInput").ap()
    io["l1f_sRdT"] = nc.dram_tensor("l1f_sRdT", [2816, 128], F32, kind="ExternalInput").ap()
    io["l1f_sUdT"] = nc.dram_tensor("l1f_sUdT", [128, 128], F32, kind="ExternalInput").ap()
    io["l1f_sCdT"] = nc.dram_tensor("l1f_sCdT", [128, 2048], F32, kind="ExternalInput").ap()
    io["weT_hbm"] = nc.dram_tensor("weT_hbm", [32, 2048], F32, kind="Internal").ap()
    io["y_part"] = nc.dram_tensor("y_part", [TOK, D], F32, kind="Internal").ap()
    return io


def build_p3_program(stop=None):
    nc = bass.Bass("TRN2", target_bir_lowering=False)
    io = declare_io_phase3(nc)
    io["x3"] = nc.dram_tensor("x3", [TOK, D], F32, kind="ExternalInput").ap()
    io["x4"] = nc.dram_tensor("x4", [TOK, D], F32, kind="ExternalOutput").ap()
    with ExitStack() as ctx:
        r = Rec(nc, ctx)
        Phase3(r, ctx, nc, io).emit(stop=stop)
        r.wait("sync", "st", r.cnt["st"])
        blk = ctx.enter_context(nc.Block())
        r.replay(blk)
    return nc


def prep_phase3_weights(ins):
    T = lambda a: np.ascontiguousarray(np.asarray(a, np.float32).T)
    T3 = lambda a: np.ascontiguousarray(np.asarray(a, np.float32).transpose(0, 2, 1))
    return dict(
        l1f_gwT=T(ins["L1_ffn_gate_w"]), l1f_RgT=T(ins["L1_ffn_Rg"]),
        l1f_RuT=T(ins["L1_ffn_Ru"]), l1f_RdT=T(ins["L1_ffn_Rd"]),
        l1f_UgT=T3(ins["L1_ffn_Ug"]), l1f_CgT=T3(ins["L1_ffn_Cg"]),
        l1f_UuT=T3(ins["L1_ffn_Uu"]), l1f_CuT=T3(ins["L1_ffn_Cu"]),
        l1f_UdT=T3(ins["L1_ffn_Ud"]), l1f_CdT=T3(ins["L1_ffn_Cd"]),
        l1f_sRgT=T(ins["L1_ffn_s_Rg"]), l1f_sUgT=T(ins["L1_ffn_s_Ug"]),
        l1f_sCgT=T(ins["L1_ffn_s_Cg"]), l1f_sRuT=T(ins["L1_ffn_s_Ru"]),
        l1f_sUuT=T(ins["L1_ffn_s_Uu"]), l1f_sCuT=T(ins["L1_ffn_s_Cu"]),
        l1f_sRdT=T(ins["L1_ffn_s_Rd"]), l1f_sUdT=T(ins["L1_ffn_s_Ud"]),
        l1f_sCdT=T(ins["L1_ffn_s_Cd"]))


def declare_io_phase0(nc):
    io = {}
    io["x0c"] = nc.dram_tensor("x0c", [TOT, D], F32, kind="ExternalInput").ap()
    io["l0_wqT"] = nc.dram_tensor("l0_wqT", [D, D], F32, kind="ExternalInput").ap()
    io["l0_wkT"] = nc.dram_tensor("l0_wkT", [D, 512], F32, kind="ExternalInput").ap()
    io["l0_wvT"] = nc.dram_tensor("l0_wvT", [D, 512], F32, kind="ExternalInput").ap()
    io["l0_woT"] = nc.dram_tensor("l0_woT", [D, D], F32, kind="ExternalInput").ap()
    io["cosT"] = nc.dram_tensor("cosT", [TOT, 64], F32, kind="ExternalInput").ap()
    io["sinT"] = nc.dram_tensor("sinT", [TOT, 64], F32, kind="ExternalInput").ap()
    io["valid"] = nc.dram_tensor("valid", [128, 6], F32, kind="ExternalInput").ap()
    return io


def build_p0_program(x1_kind="ExternalOutput", stop=None, dump_it=2):
    nc = bass.Bass("TRN2", target_bir_lowering=False)
    io = declare_io_phase0(nc)
    io["x1"] = nc.dram_tensor("x1", [TOK, D], F32, kind=x1_kind).ap()
    if stop is not None:
        io["dbg0"] = nc.dram_tensor("dbg0", [128, 16 * 512], BF16, kind="ExternalOutput").ap()
        io["dbg1"] = nc.dram_tensor("dbg1", [128, 16 * 512], BF16, kind="ExternalOutput").ap()
    with ExitStack() as ctx:
        r = Rec(nc, ctx)
        Phase0(r, ctx, nc, io).emit(stop=stop, dump_it=dump_it)
        r.wait("sync", "st", r.cnt["st"])
        blk = ctx.enter_context(nc.Block())
        r.replay(blk)
    return nc



def prep_phase0_inputs(x, wq, wk, wv, wo):
    x = np.asarray(x, dtype=np.float32)
    inv = (1.0 / 10000.0 ** (np.arange(0, DH, 2, dtype=np.float64) / DH))
    wqT = np.ascontiguousarray(np.asarray(wq, np.float32).T)
    wkT = np.ascontiguousarray(np.asarray(wk, np.float32).T)
    wvT = np.ascontiguousarray(np.asarray(wv, np.float32).T)
    woT = np.ascontiguousarray(np.asarray(wo, np.float32).T)
    maps = []
    for c in range(8):
        b, h = c // 2, c % 2
        own = x[b, h * TOK:(h + 1) * TOK]
        halo = x[b, TOK - HALO:TOK] if h == 1 else np.zeros((HALO, D), np.float32)
        x0c = np.ascontiguousarray(np.concatenate([halo, own], axis=0))
        pos = (h * TOK - HALO) + np.arange(TOT, dtype=np.float64)
        pos = np.maximum(pos, 0.0)
        ang = np.outer(pos, inv)
        cosT = np.ascontiguousarray(np.cos(ang).astype(np.float32))
        sinT = np.ascontiguousarray(np.sin(ang).astype(np.float32))
        valid = np.ones((128, 6), np.float32)
        if h == 0:
            valid[:, 0:2] = 0.0
        maps.append(dict(x0c=x0c, l0_wqT=wqT, l0_wkT=wkT, l0_wvT=wvT, l0_woT=woT,
                         cosT=cosT, sinT=sinT, valid=valid))
    return maps


def build_full_program():
    nc = bass.Bass("TRN2", target_bir_lowering=False)
    io = {}
    io.update(declare_io_phase0(nc))
    io["x1"] = nc.dram_tensor("x1", [TOK, D], F32, kind="Internal").ap()
    io.update(declare_io_phase1(nc))
    io["x2"] = nc.dram_tensor("x2", [TOK, D], F32, kind="Internal").ap()
    io.update(declare_io_phase2(nc))
    io["batchX2"] = nc.dram_tensor("batchX2", [2 * TOK, D], F32, kind="Internal").ap()
    io["x2f"] = io["batchX2"]
    io["x2own"] = io["x2"]
    io["x3"] = nc.dram_tensor("x3", [TOK, D], F32, kind="Internal").ap()
    io.update(declare_io_phase3(nc))
    io["x4"] = nc.dram_tensor("x4", [TOK, D], F32, kind="ExternalOutput").ap()
    with ExitStack() as ctx:
        r = Rec(nc, ctx)
        with ExitStack() as c0:
            Phase0(r, c0, nc, io).emit()
            r.barrier()
        with ExitStack() as c1:
            Phase1(r, c1, nc, io).emit()
            r.barrier()
        cc_t = r.op("gpsimd", lambda e: e.collective_compute(
            "AllGather", ALU.bypass,
            replica_groups=[[0, 1], [2, 3], [4, 5], [6, 7]],
            ins=[io["x2"].opt()], outs=[io["batchX2"].opt()]), inc="dg", inc_by=16)
        r.wait("gpsimd", "dg", cc_t)
        r.barrier()
        with ExitStack() as c2:
            Phase2(r, c2, nc, io).emit()
            r.barrier()
        with ExitStack() as c3:
            Phase3(r, c3, nc, io).emit()
        r.wait("sync", "st", r.cnt["st"])
        blk = ctx.enter_context(nc.Block())
        r.replay(blk)
    return nc


def prep_full_maps(inputs):
    maps0 = prep_phase0_inputs(inputs["L0_attn_x"], inputs["L0_attn_wq"],
                               inputs["L0_attn_wk"], inputs["L0_attn_wv"],
                               inputs["L0_attn_wo"])
    rgT, wgu, wd = prep_phase1_weights(inputs["L0_ffn_router_gate"],
                                       inputs["L0_ffn_expert_gate_up"],
                                       inputs["L0_ffn_expert_down"])
    shared2, per_core2 = prep_phase2_inputs(
        inputs["L1_attn_Wq"], inputs["L1_attn_bq"], inputs["L1_attn_Wk"],
        inputs["L1_attn_bk"], inputs["L1_attn_Wv"], inputs["L1_attn_bv"],
        inputs["L1_attn_Wo"], inputs["L1_attn_bo"], inputs["L1_attn_rand_matrix"])
    w3 = prep_phase3_weights(inputs)
    maps = []
    for c in range(8):
        m = dict(maps0[c])
        m.update(dict(l0_rgT=rgT, l0_wgu=wgu, l0_wd=wd))
        m.update(shared2)
        m.update(per_core2[c])
        m.update(w3)
        maps.append(m)
    return maps


def kernel_fused(**inputs):
    nc = build_full_program()
    maps = prep_full_maps(inputs)
    res = run_bass_kernel_spmd(nc, maps, core_ids=list(range(8)))
    out = np.zeros((4, 4096, D), np.float32)
    for c in range(8):
        b, h = c // 2, c % 2
        out[b, h * TOK:(h + 1) * TOK] = np.asarray(res.results[c]["x4"])
    return out


def build_A_program():
    nc = bass.Bass("TRN2", target_bir_lowering=False)
    io = {}
    io.update(declare_io_phase0(nc))
    io["x1"] = nc.dram_tensor("x1", [TOK, D], F32, kind="Internal").ap()
    io.update(declare_io_phase1(nc))
    io["x2"] = nc.dram_tensor("x2", [TOK, D], F32, kind="ExternalOutput").ap()
    with ExitStack() as ctx:
        r = Rec(nc, ctx)
        with ExitStack() as c0:
            Phase0(r, c0, nc, io).emit()
            r.barrier()
        with ExitStack() as c1:
            Phase1(r, c1, nc, io).emit()
        r.wait("sync", "st", r.cnt["st"])
        blk = ctx.enter_context(nc.Block())
        r.replay(blk)
    return nc


def build_B_program():
    nc = bass.Bass("TRN2", target_bir_lowering=False)
    io = {}
    io.update(declare_io_phase2(nc))
    io["x2f"] = nc.dram_tensor("x2f", [2 * TOK, D], F32, kind="ExternalInput").ap()
    io["x2own"] = nc.dram_tensor("x2own", [TOK, D], F32, kind="ExternalInput").ap()
    io["x3"] = nc.dram_tensor("x3", [TOK, D], F32, kind="Internal").ap()
    io.update(declare_io_phase3(nc))
    io["x4"] = nc.dram_tensor("x4", [TOK, D], F32, kind="ExternalOutput").ap()
    with ExitStack() as ctx:
        r = Rec(nc, ctx)
        with ExitStack() as c2:
            Phase2(r, c2, nc, io).emit()
            r.barrier()
        with ExitStack() as c3:
            Phase3(r, c3, nc, io).emit()
        r.wait("sync", "st", r.cnt["st"])
        blk = ctx.enter_context(nc.Block())
        r.replay(blk)
    return nc


def kernel(**inputs):
    n = 8
    ncA = build_A_program()
    mapsA = prep_phase0_inputs(inputs["L0_attn_x"], inputs["L0_attn_wq"],
                               inputs["L0_attn_wk"], inputs["L0_attn_wv"],
                               inputs["L0_attn_wo"])
    rgT, wgu, wd = prep_phase1_weights(inputs["L0_ffn_router_gate"],
                                       inputs["L0_ffn_expert_gate_up"],
                                       inputs["L0_ffn_expert_down"])
    for m in mapsA:
        m.update(dict(l0_rgT=rgT, l0_wgu=wgu, l0_wd=wd))
    resA = run_bass_kernel_spmd(ncA, mapsA, core_ids=list(range(n)))
    x2 = [np.asarray(resA.results[c]["x2"]) for c in range(n)]

    ncB = build_B_program()
    shared2, per_core2 = prep_phase2_inputs(
        inputs["L1_attn_Wq"], inputs["L1_attn_bq"], inputs["L1_attn_Wk"],
        inputs["L1_attn_bk"], inputs["L1_attn_Wv"], inputs["L1_attn_bv"],
        inputs["L1_attn_Wo"], inputs["L1_attn_bo"], inputs["L1_attn_rand_matrix"])
    w3 = prep_phase3_weights(inputs)
    mapsB = []
    for c in range(n):
        b, h = c // 2, c % 2
        m = dict(shared2)
        m.update(per_core2[c])
        m.update(w3)
        m["x2f"] = np.ascontiguousarray(np.concatenate([x2[2 * b], x2[2 * b + 1]], axis=0))
        m["x2own"] = x2[c]
        mapsB.append(m)
    resB = run_bass_kernel_spmd(ncB, mapsB, core_ids=list(range(n)))
    out = np.zeros((4, 4096, D), np.float32)
    for c in range(n):
        b, h = c // 2, c % 2
        out[b, h * TOK:(h + 1) * TOK] = np.asarray(resB.results[c]["x4"])
    return out
